# version 07
# baseline (speedup 1.0000x reference)
import math
import numpy as np
import ml_dtypes
from contextlib import ExitStack
import concourse.bass as bass
import concourse.mybir as mybir
from concourse.bass_utils import run_bass_kernel_spmd

F32 = mybir.dt.float32
BF16 = mybir.dt.bfloat16
I32 = mybir.dt.int32
AF = mybir.ActivationFunctionType
ALU = mybir.AluOpType
AX = mybir.AxisListType

B, S, D = 2, 4096, 2048
CH, NCH, TOK = 1024, 4, 1024
BASE_LR_INV = 0.001 + math.log(-math.expm1(-0.001))
C_L0, C_L1 = 24.0, 16.0
SC0 = 1.0 / math.sqrt(128.0)
R2048 = 1.0 / math.sqrt(2048.0)
R4096 = 1.0 / math.sqrt(4096.0)
R8192 = 1.0 / math.sqrt(8192.0)
ENGINES = ("tensor", "vector", "scalar", "sync", "gpsimd")
ARN_ELEMS = 53200

CONST_VALS = sorted(set([
    -C_L0, -C_L1, 0.0, SC0, -0.5, 0.5, 1.0, 1e-6, 1e-5, 1e-24, 1e-12,
    1.0 / D, 1.0 / 512.0, R8192, R4096, R2048,
    -1.0, 2.0, math.log(0.5), math.log(0.25), 1.0 / 8388608.0,
]))


def _hadamard(n):
    h = np.array([[1.0]])
    while h.shape[0] < n:
        h = np.block([[h, h], [h, -h]])
    return h


def _f1_perm(n, nch):
    lb = int(math.log2(nch))
    out = np.zeros(n, np.int64)
    for m in range(nch):
        for p in range(128):
            out[m * 128 + p] = (p << lb) | m
    return out


def _f3_perm(n, nch):
    lb_hi = int(math.log2(nch))
    k = 128 // nch
    lbk = int(math.log2(k))
    out = np.zeros(n, np.int64)
    for m2 in range(nch):
        for p2 in range(128):
            i_hi = (m2 << lbk) | (p2 & (k - 1))
            q = p2 >> lbk
            out[m2 * 128 + p2] = (i_hi << lb_hi) | q
    return out


def _pack_w(W):
    K, N = W.shape
    return np.ascontiguousarray(
        W.reshape(K // 128, 128, N // 128, 128)
        .transpose(1, 2, 0, 3).reshape(128, -1))


def host_prepare(inputs):
    f32 = np.float32
    x0 = np.ascontiguousarray(np.asarray(inputs["L0_attn_hidden_states"], f32))
    WqkvT = np.asarray(inputs["L0_attn_Wqkv"], f32).T
    WoT_p = _pack_w(np.ascontiguousarray(np.asarray(inputs["L0_attn_Wo"], f32).T))
    lr_wT = np.asarray(inputs["L0_attn_lr_w"], f32).T
    f1_8192 = _f1_perm(8192, 64)
    f3_8192 = _f3_perm(8192, 64)
    f1_4096 = _f1_perm(4096, 32)
    f3_4096 = _f3_perm(4096, 32)
    f3_2048 = _f3_perm(2048, 16)
    W1_p = _pack_w(np.asarray(inputs["L0_ffn_W1"], f32).T[:, f1_8192])
    W3_p = _pack_w(np.asarray(inputs["L0_ffn_W3"], f32).T[:, f1_8192])
    W2Tp = np.ascontiguousarray(np.asarray(inputs["L0_ffn_W2"], f32).T[f3_8192, :])
    wup_p = _pack_w(np.asarray(inputs["L1_ffn_w_up"], f32).T[f3_2048, :][:, f1_4096])
    wdnTp = np.ascontiguousarray(np.asarray(inputs["L1_ffn_w_down"], f32).T[f3_4096, :])
    wqkv1_p = _pack_w(np.asarray(inputs["L1_attn_w_qkv"], f32))
    wo1_p = _pack_w(np.asarray(inputs["L1_attn_w_o"], f32))

    ang0 = np.arange(S, dtype=np.float64)[:, None] / (
        500000.0 ** (np.arange(0, 128, 2, dtype=np.float64) / 128))[None, :]
    cos0f = np.ascontiguousarray(np.concatenate([np.cos(ang0).T] * 2, 0).astype(f32))
    sin0f = np.ascontiguousarray(np.concatenate([np.sin(ang0).T] * 2, 0).astype(f32))
    ang1 = np.arange(S, dtype=np.float64)[:, None] / (
        1000000.0 ** (np.arange(0, 128, 2, dtype=np.float64) / 128))[None, :]
    cos1f = np.concatenate([np.cos(ang1).T] * 2, 0).astype(f32)
    sin1f = np.concatenate([np.sin(ang1).T] * 2, 0).astype(f32)

    P = np.zeros((128, 128), f32)
    for i in range(64):
        P[i, i + 64] = -1.0
        P[i + 64, i] = 1.0
    PT = np.ascontiguousarray(P.T)
    ident = np.eye(128, dtype=f32)
    kk = np.arange(128)[:, None]
    cc = np.arange(1024)[None, :]
    maskM = (kk <= cc - 512).astype(f32)
    maskW = np.ascontiguousarray(1.0 - maskM)

    def chunkmajor(vec):
        return np.ascontiguousarray(np.asarray(vec, f32).reshape(4, 128).T)

    in_maps = []
    for core in range(8):
        g, t = core // 4, core % 4
        cols = slice(512 * t, 512 * t + 512)
        m = {}
        m["x0T"] = np.ascontiguousarray(x0[g].T)
        m["x0T_own"] = np.ascontiguousarray(x0[g].T[:, t * TOK:(t + 1) * TOK])
        m["wqkv_p"] = _pack_w(np.concatenate(
            [WqkvT[:, 512 * t:512 * t + 512],
             WqkvT[:, 2048 + 512 * t:2048 + 512 * t + 512],
             WqkvT[:, 4096 + 512 * t:4096 + 512 * t + 512]], 1))
        m["lrw_p"] = np.ascontiguousarray(
            lr_wT[:, [t, 4 + t, 8 + t]].reshape(16, 128, 3)
            .transpose(1, 0, 2).reshape(128, 48))
        m["lrb"] = np.ascontiguousarray(
            np.asarray(inputs["L0_attn_lr_b"], f32)[[t, 4 + t, 8 + t]].reshape(1, 3))
        m["pvecs"] = np.ascontiguousarray(np.concatenate([
            chunkmajor(np.asarray(inputs["L0_attn_q_norm_w"])[cols]),
            chunkmajor(np.asarray(inputs["L0_attn_k_norm_w"])[cols]),
            chunkmajor(np.asarray(inputs["L0_attn_qk_scale"])[cols, 0]),
            chunkmajor(np.asarray(inputs["L0_attn_qk_offset"])[cols, 0]),
            chunkmajor(np.asarray(inputs["L0_attn_qk_scale"])[cols, 1]),
            chunkmajor(np.asarray(inputs["L0_attn_qk_offset"])[cols, 1]),
            chunkmajor(np.asarray(inputs["L0_attn_ttt_norm_w"])),
        ], axis=1))
        m["w0T"] = np.ascontiguousarray(
            np.asarray(inputs["L0_attn_w0"][t], f32).T.reshape(4, 128, 512)
            .transpose(1, 0, 2).reshape(128, 2048))
        m["w2T"] = np.ascontiguousarray(
            np.asarray(inputs["L0_attn_w2"][t], f32).T.reshape(4, 128, 512)
            .transpose(1, 0, 2).reshape(128, 2048))
        m["w1N"] = np.ascontiguousarray(
            np.asarray(inputs["L0_attn_w1"][t], f32).reshape(4, 128, 512)
            .transpose(1, 0, 2).reshape(128, 2048))
        m["WoT_p"] = WoT_p
        m["W1_p"] = W1_p
        m["W3_p"] = W3_p
        m["W2Tp"] = W2Tp
        m["wqkv1_p"] = wqkv1_p
        m["wo1_p"] = wo1_p
        m["wup_p"] = wup_p
        m["wdnTp"] = wdnTp
        m["cos0"] = cos0f
        m["sin0"] = sin0f
        m["cos1"] = np.ascontiguousarray(cos1f[:, t * TOK:(t + 1) * TOK])
        m["sin1"] = np.ascontiguousarray(sin1f[:, t * TOK:(t + 1) * TOK])
        m["PT"] = PT
        m["ident"] = ident
        m["maskM"] = maskM
        m["maskW"] = maskW
        m["maskWc"] = np.ascontiguousarray(
            maskW if t > 0 else np.zeros((128, 1024), f32))
        m["H128"] = _hadamard(128).astype(f32)
        m["H64I2"] = np.kron(_hadamard(64), np.eye(2)).astype(f32)
        m["H32I4"] = np.kron(_hadamard(32), np.eye(4)).astype(f32)
        m["H16I8"] = np.kron(_hadamard(16), np.eye(8)).astype(f32)
        msel = np.zeros((128, 4), f32)
        msel[:, t] = 1.0
        m["msel"] = msel
        hsel = np.zeros((128, 4), f32)
        hsel[:, (t + 3) % 4] = 1.0
        m["hsel"] = hsel
        in_maps.append(m)
    return in_maps


class Ctx:
    BL = {"tensor": ("V", "S"), "vector": ("T", "S"), "scalar": ("T", "V")}
    TICK = {"tensor": "T", "vector": "V", "scalar": "S"}

    def __init__(self, nc, serial=True):
        self.nc = nc
        self.serial = serial
        self.prog = {e: [] for e in ENGINES}
        self.cnt = {}
        self.lastwait = {}
        self.S = {}

    def addsem(self, es, name):
        self.S[name] = es.enter_context(self.nc.semaphore(name))
        self.cnt[name] = 0

    def wait(self, eng, sem, val):
        if val is None or val <= 0:
            return
        key = (eng, sem)
        if self.lastwait.get(key, 0) >= val:
            return
        self.lastwait[key] = val
        self.prog[eng].append([None, (sem, int(val))])

    def inc(self, eng, sem, by):
        self.cnt[sem] += by
        slot = None
        for item in reversed(self.prog[eng]):
            if item[0] is not None:
                slot = item
                break
        assert slot is not None
        fn = slot[0]
        sh = self.S[sem]
        slot[0] = (lambda e, fn=fn, sh=sh, by=by: fn(e).then_inc(sh, by))
        return self.cnt[sem]

    def op(self, eng, fn):
        if self.serial and eng in self.BL:
            for s in self.BL[eng]:
                self.wait(eng, s, self.cnt[s])
        self.prog[eng].append([fn, None])
        if eng in self.TICK:
            return self.inc(eng, self.TICK[eng], 1)
        return None

    def opx(self, eng, fn, waits=()):
        for s, v in waits:
            self.wait(eng, s, v)
        self.prog[eng].append([fn, None])
        if eng in self.TICK:
            return self.inc(eng, self.TICK[eng], 1)
        return None

    def fence(self, engines=ENGINES):
        marks = dict(self.cnt)
        for e in engines:
            for s, v in marks.items():
                self.wait(e, s, v)

    def replay(self, block):
        def mk(eng):
            def body(e):
                for fn, w in self.prog[eng]:
                    if fn is not None:
                        fn(e)
                    else:
                        e.wait_ge(self.S[w[0]], w[1])
            return body
        for eng in ENGINES:
            getattr(block, eng)(mk(eng))


class Alloc:
    def __init__(self, arena, nbytes):
        self.arena = arena
        self.nbytes = nbytes
        self.off = 0
        self.peak = 0
        self.marks = []
        self.last_at = 0

    def view(self, at, elems, dt=F32):
        bs = 2 if dt == BF16 else 4
        assert at % bs == 0 and at + elems * bs <= self.nbytes
        if dt == F32:
            return self.arena[:, at // 4: at // 4 + elems]
        return self.arena[:, :].bitcast(dt)[:, at // bs: at // bs + elems]

    def alloc(self, elems, dt=F32):
        self.off = (self.off + 3) & ~3
        at = self.off
        self.off += elems * (2 if dt == BF16 else 4)
        self.peak = max(self.peak, self.off)
        assert self.off <= self.nbytes, f"arena overflow {self.off} > {self.nbytes}"
        self.last_at = at
        return self.view(at, elems, dt)

    def push(self):
        self.marks.append(self.off)

    def pop(self):
        self.off = self.marks.pop()


def build_nc(stop_after="out", taps=(), skip=()):
    nc = bass.Bass("TRN2", target_bir_lowering=False, debug=False)
    din = {}
    for name, shape in [
        ("x0T", [D, S]), ("x0T_own", [D, TOK]),
        ("wqkv_p", [128, 12 * 2048]), ("lrw_p", [128, 48]), ("lrb", [1, 3]),
        ("pvecs", [128, 28]),
        ("w0T", [128, 2048]), ("w2T", [128, 2048]), ("w1N", [128, 2048]),
        ("WoT_p", [128, 16 * 2048]),
        ("W1_p", [128, 64 * 2048]), ("W3_p", [128, 64 * 2048]), ("W2Tp", [8192, D]),
        ("wqkv1_p", [128, 24 * 2048]), ("wo1_p", [128, 16 * 2048]),
        ("wup_p", [128, 32 * 2048]), ("wdnTp", [4096, D]),
        ("cos0", [128, S]), ("sin0", [128, S]),
        ("cos1", [128, TOK]), ("sin1", [128, TOK]),
        ("PT", [128, 128]), ("ident", [128, 128]),
        ("maskM", [128, 1024]), ("maskW", [128, 1024]), ("maskWc", [128, 1024]),
        ("H128", [128, 128]), ("H64I2", [128, 128]), ("H32I4", [128, 128]),
        ("H16I8", [128, 128]), ("msel", [128, 4]), ("hsel", [128, 4]),
    ]:
        din[name] = nc.dram_tensor(name, shape, F32, kind="ExternalInput")

    y_out = nc.dram_tensor("y_out", [D, TOK], F32, kind="ExternalOutput")
    tap_t = {n: nc.dram_tensor("tap_" + n, list(sh), F32, kind="ExternalOutput")
             for n, sh in dict(taps).items()}

    wqkv_bfd = nc.dram_tensor("wqkv_bfd", [128, 12 * 2048], BF16)
    ag_mix_in = [nc.dram_tensor(f"ag_mix_in{c}", [512, CH], BF16) for c in range(NCH)]
    ag_mix_out = [nc.dram_tensor(f"ag_mix_out{c}", [D, CH], BF16) for c in range(NCH)]
    ar_in = [nc.dram_tensor(f"ar_in{c}", [1, 2048], F32) for c in range(NCH)]
    ar_out = [nc.dram_tensor(f"ar_out{c}", [1, 2048], F32) for c in range(NCH)]
    ag_kv_in = nc.dram_tensor("ag_kv_in", [1024, 512], BF16)
    ag_kv_out = nc.dram_tensor("ag_kv_out", [4096, 512], BF16)
    shufd = nc.dram_tensor("shufd", [128, 64 * 1024], BF16)
    RG = [[0, 1, 2, 3], [4, 5, 6, 7]]

    marks = {}
    es = ExitStack()
    C = Ctx(nc, serial=True)
    for s in ("T", "V", "S", "L", "G", "GA", "GB", "LS", "LE",
              "LX0", "LX1", "LW0", "LW1", "LM0", "LM1", "CCR", "CCA"):
        C.addsem(es, s)

    def sbt(name, shape, dt=F32):
        return es.enter_context(nc.sbuf_tensor(name, shape, dt))

    ARN = sbt("ARN", [128, ARN_ELEMS])
    AL = Alloc(ARN, ARN_ELEMS * 4)
    cpool = AL.alloc(len(CONST_VALS))
    for i, cv in enumerate(CONST_VALS):
        nc.const_aps.register(F32, float(cv), cpool[:, i:i + 1])
    pvecs = AL.alloc(28)
    ones_bf = AL.alloc(2, BF16)
    onesc_f = AL.alloc(1)
    onesr_f = AL.alloc(128)
    PT_bf = AL.alloc(128, BF16)
    id_bf = AL.alloc(128, BF16)
    H128_bf = AL.alloc(128, BF16)
    H64I2_bf = AL.alloc(128, BF16)
    H32I4_bf = AL.alloc(128, BF16)
    H16I8_bf = AL.alloc(128, BF16)
    maskM_bf = AL.alloc(1024, BF16)
    maskW_bf = AL.alloc(1024, BF16)
    maskWc_bf = AL.alloc(1024, BF16)
    rows = AL.alloc(2048)
    rows2 = AL.alloc(2048)
    lrb_bc = AL.alloc(3)
    lrw_bf = AL.alloc(48, BF16)
    lr_sb = AL.alloc(24)
    srow = AL.alloc(8)
    msel = AL.alloc(4)
    hsel = AL.alloc(4)
    mxacc = AL.alloc(8)
    pp = [es.enter_context(nc.psum_tensor(f"pb{i}", [128, 512], F32)) for i in range(8)]
    ppb = [pp[i][:, :].bitcast(BF16) for i in range(8)]

    def dma(qeng, out, in_, sem=None):
        if sem is None:
            sem = "G" if qeng == "gpsimd" else "L"
        C.op(qeng, lambda e, o=out, i=in_: e.dma_start(out=o, in_=i))
        return C.inc(qeng, sem, 16)

    def mm(out, lhsT, rhs, start=True, stop=True):
        return C.op("tensor", lambda e, o=out, l=lhsT, r=rhs, a=start, b=stop:
                    e.matmul(o, l, r, start=a, stop=b))

    def vop(fn):
        return C.op("vector", fn)

    def sop(fn):
        return C.op("scalar", fn)

    def mmx(out, lhsT, rhs, start=True, stop=True, waits=()):
        return C.opx("tensor", lambda e, o=out, l=lhsT, r=rhs, a=start, b=stop:
                     e.matmul(o, l, r, start=a, stop=b), waits)

    def vopx(fn, waits=()):
        return C.opx("vector", fn, waits)

    def sopx(fn, waits=()):
        return C.opx("scalar", fn, waits)

    def colsum(dst_ps, src_bf, start=True, stop=True):
        return mm(dst_ps, ones_bf[:, 0:1], src_bf, start, stop)

    def bcast(dst_ps, row_ap):
        return mm(dst_ps, onesr_f[:1, :], row_ap, True, True)

    def transpose(dst_ps, src_bf):
        return C.op("tensor", lambda e, o=dst_ps, i=src_bf:
                    e.transpose(o, i, id_bf[:, :]))

    def collective(kind, op_, in_t, out_t, sem="CCA"):
        C.op("gpsimd", lambda e, k=kind, o=op_, i=in_t, u=out_t: e.collective_compute(
            k, o, replica_groups=RG, ins=[i.ap().opt()], outs=[u.ap().opt()]))
        return C.inc("gpsimd", sem, 1)

    AL.push()
    ist = [AL.alloc(2048), AL.alloc(2048)]
    ist_bf = AL.alloc(2048, BF16)
    cast_v = {0: 0, 1: 0}
    st_g = {0: 0}

    def phase_init():
        for i, cv in enumerate(CONST_VALS):
            vop(lambda e, i=i, cv=float(cv): e.memset(cpool[:, i:i + 1], cv))
        vop(lambda e: e.memset(ones_bf[:, 0:1], 1.0))
        vop(lambda e: e.memset(onesc_f[:, :], 1.0))
        vop(lambda e: e.memset(onesr_f[:, :], 1.0))
        dma("sync", pvecs[:, :], din["pvecs"][:, :])
        dma("sync", msel[:, :], din["msel"][:, :])
        dma("sync", hsel[:, :], din["hsel"][:, :])
        lv = dma("sync", ist[0][:, 0:48], din["lrw_p"][:, :], "LX0")
        C.wait("vector", "LX0", lv)
        vv = vop(lambda e: e.tensor_copy(lrw_bf[:, :], ist[0][:, 0:48]))
        C.wait("sync", "V", vv)
        lv = dma("sync", rows[:1, 0:3], din["lrb"][:, :])
        C.wait("vector", "L", lv)
        vop(lambda e: e.tensor_scalar(rows[:1, 0:3], rows[:1, 0:3],
                                      BASE_LR_INV, None, ALU.add))
        bcast(pp[0][:, 0:3], rows[:1, 0:3])
        vop(lambda e: e.tensor_copy(lrb_bc[:, :], pp[0][:, 0:3]))
        for nm, dst, n in [("PT", PT_bf, 128), ("ident", id_bf, 128),
                           ("H128", H128_bf, 128), ("H64I2", H64I2_bf, 128),
                           ("H32I4", H32I4_bf, 128), ("H16I8", H16I8_bf, 128),
                           ("maskM", maskM_bf, 1024), ("maskW", maskW_bf, 1024),
                           ("maskWc", maskWc_bf, 1024)]:
            lv = dma("sync", ist[0][:, 0:n], din[nm][:, :], "LX0")
            C.wait("vector", "LX0", lv)
            vv = vop(lambda e, d=dst, s=ist[0][:, 0:n]: e.tensor_copy(d[:, :], s))
            C.wait("sync", "V", vv)
        for m_ in range(12):
            buf = ist[m_ % 2]
            if m_ >= 2:
                C.wait("sync", "V", cast_v[m_ % 2])
            lv = dma("sync", buf, din["wqkv_p"][:, m_ * 2048:(m_ + 1) * 2048],
                     "LX%d" % (m_ % 2))
            C.wait("vector", "LX%d" % (m_ % 2), lv)
            C.wait("vector", "G", 0 if m_ == 0 else st_g[0])
            vv = vop(lambda e, b=buf: e.tensor_copy(ist_bf[:, :], b))
            cast_v[m_ % 2] = vv
            C.wait("gpsimd", "V", vv)
            gv = dma("gpsimd", wqkv_bfd[:, m_ * 2048:(m_ + 1) * 2048], ist_bf[:, :])
            st_g[0] = gv
            C.wait("vector", "G", gv)
        C.fence()

    phase_init()

    def layer_norm(src_bf, sq_bf, mbc, rbc, scrA, dst_bf=None, out_fn=None):
        for fc in range(16):
            sop(lambda e, fc=fc: e.activation(
                sq_bf[:, 0:1024], src_bf[:, fc * 1024:(fc + 1) * 1024], AF.Square))
            for half in range(2):
                h0 = half * 512
                colsum(pp[6 + half][:1, :],
                       src_bf[:, fc * 1024 + h0:fc * 1024 + h0 + 512],
                       fc == 0, fc == 15)
                colsum(pp[4 + half][:1, :], sq_bf[:, h0:h0 + 512], fc == 0, fc == 15)
        vop(lambda e: e.tensor_copy(rows[:1, 0:512], pp[6][:1, :]))
        vop(lambda e: e.tensor_copy(rows[:1, 512:1024], pp[7][:1, :]))
        vop(lambda e: e.tensor_copy(rows[:1, 1024:1536], pp[4][:1, :]))
        vop(lambda e: e.tensor_copy(rows[:1, 1536:2048], pp[5][:1, :]))
        vop(lambda e: e.tensor_scalar(rows[:1, 0:2048], rows[:1, 0:2048],
                                      1.0 / D, None, ALU.mult))
        vop(lambda e: e.tensor_mul(rows2[:1, 0:1024], rows[:1, 0:1024],
                                   rows[:1, 0:1024]))
        vop(lambda e: e.tensor_sub(rows[:1, 1024:2048], rows[:1, 1024:2048],
                                   rows2[:1, 0:1024]))
        sop(lambda e: e.activation(rows[:1, 1024:2048], rows[:1, 1024:2048],
                                   AF.Ln, bias=1e-5, scale=1.0))
        sop(lambda e: e.activation(rows[:1, 1024:2048], rows[:1, 1024:2048],
                                   AF.Exp, bias=0.0, scale=-0.5))
        for half in range(2):
            h0 = half * 512
            bcast(pp[0][:, :], rows[:1, h0:h0 + 512])
            bcast(pp[1][:, :], rows[:1, 1024 + h0:1024 + h0 + 512])
            vop(lambda e, h0=h0: e.tensor_copy(mbc[:, h0:h0 + 512], pp[0][:, :]))
            vop(lambda e, h0=h0: e.tensor_copy(rbc[:, h0:h0 + 512], pp[1][:, :]))
        if dst_bf is None:
            dst_bf = src_bf
        for fc in range(16):
            vop(lambda e, fc=fc: e.tensor_sub(
                scrA[:, 0:1024], src_bf[:, fc * 1024:(fc + 1) * 1024], mbc[:, :]))
            if out_fn is not None:
                out_fn(fc, scrA, rbc)
            else:
                vop(lambda e, fc=fc: e.tensor_mul(
                    dst_bf[:, fc * 1024:(fc + 1) * 1024], scrA[:, 0:1024], rbc[:, :]))

    def dump(name, src_ap, rows_n, cols_n):
        if name not in tap_t:
            return
        C.fence()
        stg = AL.view(AL.nbytes - 8192, 2048, F32)
        for r in range(rows_n // 128):
            vv = vop(lambda e, r=r: e.tensor_copy(
                stg[:, 0:cols_n], src_ap[:, r * cols_n:(r + 1) * cols_n]))
            C.wait("gpsimd", "V", vv)
            gv = dma("gpsimd", tap_t[name][r * 128:(r + 1) * 128, :], stg[:, 0:cols_n])
            C.wait("vector", "G", gv)
        C.fence()

    AL.pop()
    AL.push()
    x0bf = AL.alloc(16 * 512, BF16)
    at_x0bf = AL.last_at
    wslab = [AL.alloc(2048, BF16), AL.alloc(2048, BF16)]
    x0f = [AL.alloc(512), AL.alloc(512)]
    q_f = AL.alloc(4096, BF16)
    k_f = AL.alloc(4096, BF16)
    fvf = AL.alloc(4096, BF16)
    fvl1 = AL.alloc(4096, BF16)
    fkt = AL.alloc(4096, BF16)
    akbuf = [AL.alloc(4096, BF16), AL.alloc(4096, BF16)]
    vtbuf = [AL.alloc(4096, BF16), AL.alloc(4096, BF16)]
    attnT = AL.alloc(4096, BF16)
    ctab = AL.alloc(1024)
    at_ctab = AL.last_at
    stab = AL.alloc(1024)
    rqbc = AL.alloc(1024)
    at_rqbc = AL.last_at
    rkbc = AL.alloc(1024)
    aqh = AL.alloc(1024, BF16)
    at_aqh = AL.last_at
    expt = [AL.alloc(512, BF16), AL.alloc(512, BF16)]
    st_w0 = AL.alloc(2048)
    st_w2 = AL.alloc(2048)
    st_w1 = AL.alloc(2048)
    st_bf = AL.alloc(3 * 2048, BF16)
    scr = AL.alloc(2048)
    scrB = AL.alloc(1024)
    scr_bf = AL.alloc(2048, BF16)
    sbw0, sbw2, sbw1 = st_bf[:, 0:2048], st_bf[:, 2048:4096], st_bf[:, 4096:6144]
    hid = AL.view(at_x0bf, 4096, BF16)
    dgate0 = AL.view(at_x0bf + 8192, 4096, BF16)
    dhpre2 = AL.view(at_x0bf + 16384, 4096, BF16)
    hq = AL.view(at_ctab, 4096, BF16)
    oisb = AL.view(at_rqbc, 4096, BF16)
    w1NT = AL.view(at_aqh, 2048, BF16)
    L0_PEAK = AL.off

    lv = dma("sync", st_w0, din["w0T"][:, :], "LS")
    lv = dma("sync", st_w2, din["w2T"][:, :], "LS")
    lv = dma("sync", st_w1, din["w1N"][:, :], "LS")
    C.wait("vector", "LS", lv)

    slab_t = {0: 0, 1: 0}
    cast_t = {0: 0, 1: 0}
    lv_tab = {0: 0}

    def l0_qkv(c):
        C.wait("sync", "T", C.cnt["T"])
        C.wait("sync", "V", C.cnt["V"])
        dma("sync", ctab, din["cos0"][:, c * CH:c * CH + 1024], "LE")
        lv_tab[0] = dma("sync", stab, din["sin0"][:, c * CH:c * CH + 1024], "LE")
        for half in range(2):
            tok0 = c * CH + half * 512
            for k in range(16):
                buf = x0f[k % 2]
                if k >= 2:
                    C.wait("sync", "V", cast_t[k % 2])
                lv = dma("sync", buf, din["x0T"][k * 128:(k + 1) * 128,
                                                 tok0:tok0 + 512],
                         "LX%d" % (k % 2))
                C.wait("vector", "LX%d" % (k % 2), lv)
                cast_t[k % 2] = vop(lambda e, k=k, b=buf: e.tensor_copy(
                    x0bf[:, k * 512:(k + 1) * 512], b))
            for ttl in range(4):
                tt = half * 4 + ttl
                for k in range(16):
                    mm(pp[5][:, 0:3],
                       x0bf[:, k * 512 + ttl * 128:k * 512 + ttl * 128 + 128],
                       lrw_bf[:, k * 3:(k + 1) * 3], k == 0, k == 15)
                vop(lambda e: e.tensor_add(scr[:, 0:3], pp[5][:, 0:3], lrb_bc[:, :]))
                sop(lambda e: e.activation(scr[:, 4:7], scr[:, 0:3], AF.Exp))
                sop(lambda e, tt=tt: e.activation(
                    lr_sb[:, tt * 3:tt * 3 + 3], scr[:, 4:7], AF.Ln,
                    bias=1.0, scale=1.0))
            for m_ in range(12):
                wb = wslab[m_ % 2]
                if m_ >= 2 or half > 0 or c > 0:
                    C.wait("sync", "T", slab_t[m_ % 2])
                lv = dma("sync", wb, wqkv_bfd[:, m_ * 2048:(m_ + 1) * 2048],
                         "LW%d" % (m_ % 2))
                C.wait("tensor", "LW%d" % (m_ % 2), lv)
                bk = m_ % 2
                for k in range(16):
                    tv = mm(pp[bk][:, :], wb[:, k * 128:(k + 1) * 128],
                            x0bf[:, k * 512:(k + 1) * 512], k == 0, k == 15)
                slab_t[m_ % 2] = tv
                if m_ < 8:
                    fc = m_ % 4
                    dst = q_f if m_ < 4 else k_f
                    vop(lambda e, d=dst, fc=fc, half=half, bk=bk: e.tensor_copy(
                        d[:, fc * 1024 + half * 512:fc * 1024 + half * 512 + 512],
                        pp[bk][:, :]))
                    sop(lambda e, bk=bk: e.activation(
                        scr_bf[:, 0:512], pp[bk][:, :], AF.Square))
                    colsum(pp[6 if m_ < 4 else 7][:1, :], scr_bf[:, 0:512],
                           fc == 0, fc == 3)
                else:
                    oc = m_ - 8
                    sop(lambda e, bk=bk: e.activation(
                        scr[:, 0:512], pp[bk][:, :], AF.Sigmoid))
                    vop(lambda e, oc=oc, half=half, bk=bk: e.tensor_mul(
                        fvf[:, oc * 1024 + half * 512:oc * 1024 + half * 512 + 512],
                        scr[:, 0:512], pp[bk][:, :]))
                    vop(lambda e, bk=bk: e.tensor_copy(
                        scr_bf[:, 1024:1536], pp[bk][:, :]))
                    for sub in range(4):
                        tb = 3 if sub % 2 == 0 else 4
                        transpose(ppb[tb][:, 0:128],
                                  scr_bf[:, 1024 + sub * 128:1024 + sub * 128 + 128])
                        tt = half * 4 + sub
                        vop(lambda e, tb=tb, tt=tt, oc=oc, c=c: e.tensor_copy(
                            vtbuf[c % 2][:, tt * 512 + oc * 128:
                                         tt * 512 + oc * 128 + 128],
                            ppb[tb][:, 0:128]))
            vop(lambda e, half=half: e.tensor_copy(
                rows[:1, half * 512:half * 512 + 512], pp[6][:1, :]))
            vop(lambda e, half=half: e.tensor_copy(
                rows[:1, 1024 + half * 512:1024 + half * 512 + 512], pp[7][:1, :]))

    def l0_stats_ar(c):
        C.fence(("gpsimd",))
        gv = dma("gpsimd", ar_in[c][:, :], rows[:1, 0:2048])
        C.wait("gpsimd", "G", gv)
        cv = collective("AllReduce", ALU.add, ar_in[c], ar_out[c], "CCR")
        C.wait("gpsimd", "CCR", cv)
        gv = dma("gpsimd", rows[:1, 0:2048], ar_out[c][:, :])
        for e_ in ("scalar", "vector", "tensor"):
            C.wait(e_, "G", gv)
        sop(lambda e: e.activation(rows2[:1, 0:2048], rows[:1, 0:2048],
                                   AF.Ln, bias=1e-6, scale=1.0 / D))
        sop(lambda e: e.activation(rows2[:1, 0:2048], rows2[:1, 0:2048],
                                   AF.Exp, bias=0.0, scale=-0.5))
        for half in range(2):
            h0 = half * 512
            bcast(pp[0][:, :], rows2[:1, h0:h0 + 512])
            bcast(pp[1][:, :], rows2[:1, 1024 + h0:1024 + h0 + 512])
            vop(lambda e, h0=h0: e.tensor_copy(rqbc[:, h0:h0 + 512], pp[0][:, :]))
            vop(lambda e, h0=h0: e.tensor_copy(rkbc[:, h0:h0 + 512], pp[1][:, :]))

    def l0_rope_ak(c):
        C.wait("vector", "LE", lv_tab[0])
        for fc in range(4):
            vop(lambda e, fc=fc: e.tensor_scalar(
                scr_bf[:, 0:1024], k_f[:, fc * 1024:(fc + 1) * 1024],
                pvecs[:, 4 + fc:4 + fc + 1], None, ALU.mult))
            vop(lambda e: e.tensor_mul(scr_bf[:, 0:1024], scr_bf[:, 0:1024],
                                       rkbc[:, 0:1024]))
            for half in range(2):
                mm(pp[half][:, :], PT_bf[:, :],
                   scr_bf[:, half * 512:half * 512 + 512])
            vop(lambda e: e.tensor_mul(scr[:, 0:1024], scr_bf[:, 0:1024], ctab[:, :]))
            vop(lambda e: e.tensor_mul(scrB[:, 0:512], pp[0][:, :], stab[:, 0:512]))
            vop(lambda e: e.tensor_mul(scrB[:, 512:1024], pp[1][:, :],
                                       stab[:, 512:1024]))
            vop(lambda e, fc=fc, c=c: e.tensor_add(
                akbuf[c % 2][:, fc * 1024:(fc + 1) * 1024], scr[:, 0:1024],
                scrB[:, 0:1024]))

    def l0_attention(c):
        for h in range(4):
            vop(lambda e, h=h: e.tensor_scalar(
                scr_bf[:, 0:1024], q_f[:, h * 1024:(h + 1) * 1024],
                pvecs[:, 0 + h:0 + h + 1], None, ALU.mult))
            vop(lambda e: e.tensor_mul(scr_bf[:, 0:1024], scr_bf[:, 0:1024],
                                       rqbc[:, 0:1024]))
            for half in range(2):
                mm(pp[half][:, :], PT_bf[:, :],
                   scr_bf[:, half * 512:half * 512 + 512])
            vop(lambda e: e.tensor_mul(scr[:, 0:1024], scr_bf[:, 0:1024], ctab[:, :]))
            vop(lambda e: e.tensor_mul(scrB[:, 0:512], pp[0][:, :], stab[:, 0:512]))
            vop(lambda e: e.tensor_mul(scrB[:, 512:1024], pp[1][:, :],
                                       stab[:, 512:1024]))
            vop(lambda e: e.tensor_add(aqh[:, :], scr[:, 0:1024], scrB[:, 0:1024]))
            for qt in range(2):
                q0g = c * CH + qt * 512
                subs = []
                for k0 in range(q0g - 1024, q0g + 512, 128):
                    if k0 < 0:
                        continue
                    delta = k0 - q0g
                    if delta <= -640:
                        subs.append((k0, maskW_bf, (delta + 1024) // 128))
                    elif delta < 0:
                        subs.append((k0, None, 0))
                    else:
                        subs.append((k0, maskM_bf, delta // 128))
                nsub = len(subs)
                for i, (k0, mk, j) in enumerate(subs):
                    kc = (k0 // CH) % 2
                    ko = k0 % CH
                    bk = i % 2
                    mm(pp[bk][:, :],
                       akbuf[kc][:, h * 1024 + ko:h * 1024 + ko + 128],
                       aqh[:, qt * 512:qt * 512 + 512])
                    eb = expt[i % 2]
                    sop(lambda e, eb=eb, bk=bk: e.activation(
                        eb[:, :], pp[bk][:, :], AF.Exp, bias=-C_L0, scale=SC0))
                    if mk is not None:
                        vop(lambda e, eb=eb, mk=mk, j=j: e.tensor_mul(
                            eb[:, :], eb[:, :],
                            mk[:, 512 - 128 * j:1024 - 128 * j]))
                    colsum(pp[3][:1, :], eb[:, :], i == 0, i == nsub - 1)
                    mm(pp[2][:, :],
                       vtbuf[kc][:, (ko // 128) * 512 + h * 128:
                                  (ko // 128) * 512 + h * 128 + 128],
                       eb[:, :], i == 0, i == nsub - 1)
                vop(lambda e: e.reciprocal(rows[:1, 0:512], pp[3][:1, :]))
                vop(lambda e: e.tensor_copy(scr[:, 0:512], pp[2][:, :]))
                bcast(pp[4][:, :], rows[:1, 0:512])
                vop(lambda e, h=h, qt=qt: e.tensor_mul(
                    attnT[:, h * 1024 + qt * 512:h * 1024 + qt * 512 + 512],
                    scr[:, 0:512], pp[4][:, :]))

    def l0_derived(c):
        for which, src in [(0, q_f), (1, k_f)]:
            sc_col, of_col = (8, 12) if which == 0 else (16, 20)
            for fc in range(4):
                sop(lambda e, s=src, fc=fc: e.activation(
                    scr[:, 0:1024], s[:, fc * 1024:(fc + 1) * 1024], AF.Sigmoid))
                vop(lambda e, s=src, fc=fc: e.tensor_mul(
                    scr[:, 1024:2048], scr[:, 0:1024],
                    s[:, fc * 1024:(fc + 1) * 1024]))
                vop(lambda e, s=src, fc=fc, sc=sc_col, of=of_col: e.tensor_scalar(
                    s[:, fc * 1024:(fc + 1) * 1024], scr[:, 1024:2048],
                    pvecs[:, sc + fc:sc + fc + 1], pvecs[:, of + fc:of + fc + 1],
                    ALU.mult, ALU.add))
                sop(lambda e, s=src, fc=fc: e.activation(
                    scr_bf[:, 0:1024], s[:, fc * 1024:(fc + 1) * 1024], AF.Square))
                b0 = 2 + which
                colsum(pp[b0][:1, :], scr_bf[:, 0:512], fc == 0, fc == 3)
                colsum(pp[b0 + 2][:1, :], scr_bf[:, 512:1024], fc == 0, fc == 3)
        vop(lambda e: e.tensor_copy(rows[:1, 0:512], pp[2][:1, :]))
        vop(lambda e: e.tensor_copy(rows[:1, 512:1024], pp[4][:1, :]))
        vop(lambda e: e.tensor_copy(rows[:1, 1024:1536], pp[3][:1, :]))
        vop(lambda e: e.tensor_copy(rows[:1, 1536:2048], pp[5][:1, :]))
        sop(lambda e: e.activation(rows[:1, 0:2048], rows[:1, 0:2048],
                                   AF.Ln, bias=1e-24, scale=1.0))
        sop(lambda e: e.activation(rows[:1, 0:2048], rows[:1, 0:2048],
                                   AF.Exp, bias=0.0, scale=0.5))
        sop(lambda e: e.add(rows[:1, 0:2048], rows[:1, 0:2048], 1e-12))
        vop(lambda e: e.reciprocal(rows2[:1, 0:2048], rows[:1, 0:2048]))
        for which, src in [(0, q_f), (1, k_f)]:
            for half in range(2):
                bcast(pp[half][:, :],
                      rows2[:1, which * 1024 + half * 512:
                            which * 1024 + half * 512 + 512])
            for fc in range(4):
                for half in range(2):
                    o = fc * 1024 + half * 512
                    vop(lambda e, s=src, o=o, half=half: e.tensor_mul(
                        s[:, o:o + 512], s[:, o:o + 512], pp[half][:, :]))
        for dc in range(4):
            for tt in range(8):
                tb = 3 if tt % 2 == 0 else 4
                transpose(ppb[tb][:, 0:128],
                          k_f[:, dc * 1024 + tt * 128:dc * 1024 + tt * 128 + 128])
                vop(lambda e, tb=tb, tt=tt, dc=dc: e.tensor_copy(
                    fkt[:, tt * 512 + dc * 128:tt * 512 + dc * 128 + 128],
                    ppb[tb][:, 0:128]))
        for tt in range(8):
            sop(lambda e, tt=tt, c=c: e.activation(
                scr[:, 0:512], vtbuf[c % 2][:, tt * 512:(tt + 1) * 512], AF.Sigmoid))
            vop(lambda e, tt=tt, c=c: e.scalar_tensor_tensor(
                scrB[:, 0:512], scr[:, 0:512], lr_sb[:, tt * 3 + 1:tt * 3 + 2],
                vtbuf[c % 2][:, tt * 512:(tt + 1) * 512], ALU.mult, ALU.mult))
            vop(lambda e, tt=tt: e.tensor_copy(fvl1[:, tt * 512:(tt + 1) * 512],
                                               scrB[:, 0:512]))

    def l0_ttt(c):
        for dst, src in [(sbw0, st_w0), (sbw2, st_w2), (sbw1, st_w1)]:
            vop(lambda e, d=dst, s=src: e.tensor_copy(d, s))
        for tt in range(8):
            for bk, lhs_buf, rhs in [(0, k_f, sbw0), (1, k_f, sbw2), (2, fvf, sbw1)]:
                for dc in range(4):
                    mm(pp[bk][:, :],
                       lhs_buf[:, dc * 1024 + tt * 128:dc * 1024 + tt * 128 + 128],
                       rhs[:, dc * 512:(dc + 1) * 512], dc == 0, dc == 3)
            sop(lambda e: e.activation(scr[:, 0:512], pp[0][:, :], AF.Sigmoid))
            vop(lambda e: e.tensor_mul(scr[:, 512:1024], scr[:, 0:512], pp[0][:, :]))
            vop(lambda e: e.tensor_sub(scr[:, 1024:1536], pp[0][:, :],
                                       scr[:, 512:1024]))
            vop(lambda e: e.tensor_mul(scr[:, 1536:2048], scr[:, 0:512],
                                       scr[:, 1024:1536]))
            vop(lambda e: e.tensor_add(scr[:, 1536:2048], scr[:, 1536:2048],
                                       scr[:, 0:512]))
            vop(lambda e: e.tensor_copy(scrB[:, 0:512], pp[1][:, :]))
            vop(lambda e: e.tensor_mul(scrB[:, 512:1024], pp[2][:, :], scrB[:, 0:512]))
            vop(lambda e, tt=tt: e.tensor_mul(hid[:, tt * 512:(tt + 1) * 512],
                                              scr[:, 512:1024], scrB[:, 0:512]))
            vop(lambda e, tt=tt: e.scalar_tensor_tensor(
                dgate0[:, tt * 512:(tt + 1) * 512], scrB[:, 512:1024],
                lr_sb[:, tt * 3:tt * 3 + 1], scr[:, 1536:2048], ALU.mult, ALU.mult))
            vop(lambda e, tt=tt: e.scalar_tensor_tensor(
                dhpre2[:, tt * 512:(tt + 1) * 512], pp[2][:, :],
                lr_sb[:, tt * 3 + 2:tt * 3 + 3], scr[:, 512:1024],
                ALU.mult, ALU.mult))
        for st_f, lhs_buf, rhs_buf in [(st_w0, fkt, dgate0), (st_w2, fkt, dhpre2),
                                       (st_w1, fvl1, hid)]:
            for mc in range(4):
                bk = mc % 2
                for tt in range(8):
                    mm(pp[bk][:, :],
                       lhs_buf[:, tt * 512 + mc * 128:tt * 512 + mc * 128 + 128],
                       rhs_buf[:, tt * 512:(tt + 1) * 512], tt == 0, tt == 7)
                vop(lambda e, s=st_f, mc=mc, bk=bk: e.tensor_add(
                    s[:, mc * 512:(mc + 1) * 512], s[:, mc * 512:(mc + 1) * 512],
                    pp[bk][:, :]))
        for dst, src in [(sbw0, st_w0), (sbw2, st_w2), (sbw1, st_w1)]:
            vop(lambda e, d=dst, s=src: e.tensor_copy(d, s))
        for ocn in range(4):
            for hcn in range(4):
                tb = 3 if (ocn * 4 + hcn) % 2 == 0 else 4
                transpose(ppb[tb][:, 0:128],
                          sbw1[:, ocn * 512 + hcn * 128:ocn * 512 + hcn * 128 + 128])
                vop(lambda e, tb=tb, ocn=ocn, hcn=hcn: e.tensor_copy(
                    w1NT[:, hcn * 512 + ocn * 128:hcn * 512 + ocn * 128 + 128],
                    ppb[tb][:, 0:128]))
        for mh in range(4):
            for nh in range(2):
                for dc in range(4):
                    mm(pp[0][:, :],
                       sbw0[:, dc * 512 + mh * 128:dc * 512 + mh * 128 + 128],
                       q_f[:, dc * 1024 + nh * 512:dc * 1024 + nh * 512 + 512],
                       dc == 0, dc == 3)
                    mm(pp[1][:, :],
                       sbw2[:, dc * 512 + mh * 128:dc * 512 + mh * 128 + 128],
                       q_f[:, dc * 1024 + nh * 512:dc * 1024 + nh * 512 + 512],
                       dc == 0, dc == 3)
                sop(lambda e: e.activation(scr[:, 0:512], pp[0][:, :], AF.Sigmoid))
                vop(lambda e: e.tensor_mul(scr[:, 512:1024], scr[:, 0:512],
                                           pp[0][:, :]))
                vop(lambda e, mh=mh, nh=nh: e.tensor_mul(
                    hq[:, mh * 1024 + nh * 512:mh * 1024 + nh * 512 + 512],
                    scr[:, 512:1024], pp[1][:, :]))
        for nh in range(2):
            for oc in range(4):
                bk = oc % 2
                for hc in range(4):
                    mm(pp[bk][:, :],
                       w1NT[:, hc * 512 + oc * 128:hc * 512 + oc * 128 + 128],
                       hq[:, hc * 1024 + nh * 512:hc * 1024 + nh * 512 + 512],
                       hc == 0, hc == 3)
                sop(lambda e, bk=bk: e.activation(scr_bf[:, 0:512], pp[bk][:, :],
                                                  AF.Square))
                vop(lambda e, oc=oc, nh=nh, bk=bk: e.tensor_copy(
                    oisb[:, oc * 1024 + nh * 512:oc * 1024 + nh * 512 + 512],
                    pp[bk][:, :]))
                colsum(pp[5][:1, :], scr_bf[:, 0:512], oc == 0, oc == 3)
            sop(lambda e: e.activation(rows[:1, 0:512], pp[5][:1, :],
                                       AF.Ln, bias=1e-6, scale=1.0 / 512.0))
            sop(lambda e: e.activation(rows[:1, 0:512], rows[:1, 0:512],
                                       AF.Exp, bias=0.0, scale=-0.5))
            bcast(pp[6][:, :], rows[:1, 0:512])
            for oc in range(4):
                vop(lambda e, oc=oc, nh=nh: e.scalar_tensor_tensor(
                    scr[:, 0:512],
                    oisb[:, oc * 1024 + nh * 512:oc * 1024 + nh * 512 + 512],
                    pvecs[:, 24 + oc:24 + oc + 1], pp[6][:, :], ALU.mult, ALU.mult))
                vv = vop(lambda e, oc=oc, nh=nh: e.tensor_add(
                    scr_bf[:, 1024:1536], scr[:, 0:512],
                    attnT[:, oc * 1024 + nh * 512:oc * 1024 + nh * 512 + 512]))
                C.wait("gpsimd", "V", vv)
                gv = dma("gpsimd", ag_mix_in[c][oc * 128:(oc + 1) * 128,
                                                nh * 512:nh * 512 + 512],
                         scr_bf[:, 1024:1536])
                C.wait("vector", "G", gv)
        C.wait("gpsimd", "G", C.cnt["G"])
        return collective("AllGather", ALU.bypass, ag_mix_in[c], ag_mix_out[c])

    cv_ags = [0, 0, 0, 0]
    for c in range(NCH):
        l0_qkv(c)
        if stop_after == f"l0q{c}":
            break
        if "stats" not in skip:
            l0_stats_ar(c)
        if "rope" not in skip:
            l0_rope_ak(c)
        if "attn" not in skip:
            l0_attention(c)
        if "derived" not in skip:
            l0_derived(c)
        if "ttt" not in skip:
            cv_ags[c] = l0_ttt(c)
        if stop_after == f"l0c{c}":
            break

    def finish():
        C.fence()
        gv = dma("gpsimd", y_out[0:128, :], ARN[:, 0:1024])
        C.wait("gpsimd", "G", gv)
        C.replay(es.enter_context(nc.Block()))
        es.close()
        return nc

    if stop_after.startswith("l0"):
        dump("x0bf", x0bf, 2048, 512)
        dump("qf", q_f, 512, 1024)
        dump("kf", k_f, 512, 1024)
        dump("fvf", fvf, 512, 1024)
        dump("attnT", attnT, 512, 1024)
        dump("ak", akbuf[(NCH - 1) % 2] if stop_after == "l0c3" else akbuf[0],
             512, 1024)
        dump("lr", lr_sb, 128, 24)
        dump("oisb", oisb, 512, 1024)
        dump("stw0", st_w0, 128, 2048)
        return finish()

    AL.pop()
    AL.push()
    slotA = AL.alloc(16 * 1024, BF16)
    slotB = AL.alloc(16 * 1024, BF16)
    slotB_at = AL.last_at
    AL.push()
    mixT = AL.alloc(16 * 1024, BF16)
    mstage = [AL.alloc(1024, BF16), AL.alloc(1024, BF16)]
    wstage = [AL.alloc(2048), AL.alloc(2048)]
    w_bf2 = [AL.alloc(2048, BF16), AL.alloc(2048, BF16)]
    x0own = [AL.alloc(1024), AL.alloc(1024)]
    sq_bf = AL.alloc(1024, BF16)
    mbc = AL.alloc(1024)
    rbc = AL.alloc(1024)
    scrA = AL.alloc(1024)
    W_PEAK = AL.off

    def phase_w():
        C.fence(("sync",))
        msel_t = {0: 0, 1: 0}
        for r in range(4):
            C.wait("sync", "CCA", cv_ags[r])
            for k in range(16):
                sb2 = mstage[k % 2]
                if r > 0 or k >= 2:
                    C.wait("sync", "V", msel_t[k % 2])
                lv = dma("sync", sb2, ag_mix_out[r][k * 128:(k + 1) * 128, :],
                         "LM%d" % (k % 2))
                if r == 0:
                    msel_t[k % 2] = vopx(lambda e, k=k, s=sb2: e.tensor_scalar(
                        mixT[:, k * 1024:(k + 1) * 1024], s, msel[:, 0:1],
                        None, ALU.mult), [("LM%d" % (k % 2), lv)])
                else:
                    msel_t[k % 2] = vopx(lambda e, k=k, r=r, s=sb2:
                                         e.scalar_tensor_tensor(
                        mixT[:, k * 1024:(k + 1) * 1024], s, msel[:, r:r + 1],
                        mixT[:, k * 1024:(k + 1) * 1024], ALU.mult, ALU.add),
                        [("LM%d" % (k % 2), lv)])
        C.wait("sync", "T", C.cnt["T"])
        cast_w, mmg, addm = {}, {}, {}
        addt = {0: 0, 1: 0}
        for m_ in range(16):
            e_ = m_ % 2
            sl = wstage[e_]
            if m_ >= 2:
                C.wait("sync", "V", cast_w[m_ - 2])
                C.wait("sync", "V", addm[(m_ - 2, 1)])
            dma("sync", sl, din["WoT_p"][:, m_ * 2048:(m_ + 1) * 2048],
                "LW%d" % e_)
            lv = dma("sync", x0own[e_], din["x0T_own"][m_ * 128:(m_ + 1) * 128, :],
                     "LW%d" % e_)
            cwt = [("LW%d" % e_, lv)]
            if m_ >= 2:
                cwt.append(("T", mmg[(m_ - 2, 1)]))
            cast_w[m_] = vopx(lambda e, sl=sl, w=w_bf2[e_]: e.tensor_copy(w, sl),
                              cwt)
            for nh in range(2):
                wt0 = [("V", cast_w[m_])]
                if m_ >= 1:
                    wt0.append(("V", addt[nh]))
                else:
                    wt0.append(("V", max(msel_t[0], msel_t[1])))
                for k in range(16):
                    mmg[(m_, nh)] = mmx(
                        pp[nh][:, :], w_bf2[e_][:, k * 128:(k + 1) * 128],
                        mixT[:, k * 1024 + nh * 512:k * 1024 + nh * 512 + 512],
                        k == 0, k == 15, wt0 if k == 0 else ())
            for nh in range(2):
                addt[nh] = vopx(lambda e, m_=m_, nh=nh, e_=e_: e.tensor_add(
                    slotA[:, m_ * 1024 + nh * 512:m_ * 1024 + nh * 512 + 512],
                    pp[nh][:, :], x0own[e_][:, nh * 512:nh * 512 + 512]),
                    [("T", mmg[(m_, nh)])])
                addm[(m_, nh)] = addt[nh]
        layer_norm(slotA, sq_bf, mbc, rbc, scrA)

    phase_w()
    if stop_after == "xA":
        dump("xA", slotA, 2048, 1024)
        return finish()

    AL.pop()
    AL.push()
    stage4 = [AL.alloc(2048), AL.alloc(2048), AL.alloc(2048), AL.alloc(2048)]
    w13 = AL.alloc(4 * 2048, BF16)
    w2sl = AL.alloc(4 * 2048, BF16)
    hgrp = AL.alloc(4 * 1024, BF16)
    f2t = [AL.alloc(1024, BF16), AL.alloc(1024, BF16)]
    sq_bf0 = AL.alloc(1024, BF16)
    mbc0 = AL.alloc(1024)
    rbc0 = AL.alloc(1024)
    scr0 = AL.alloc(2048)
    scr0bf = AL.alloc(2048, BF16)
    FFN0_PEAK = AL.off

    def phase_ffn0():
        C.fence(("sync",))
        vop(lambda e: e.memset(slotB[:, :], 0.0))
        C.fence()
        cast2, mmg, sig, evac, h128t, cpy = {}, {}, {}, {}, {}, {}
        gv_prev = {0: 0, 1: 0}
        for m_ in range(64):
            e_ = m_ % 2
            s1, s3 = stage4[e_ * 2], stage4[e_ * 2 + 1]
            wA = w13[:, e_ * 4096:e_ * 4096 + 2048]
            wB = w13[:, e_ * 4096 + 2048:e_ * 4096 + 4096]
            if m_ >= 2:
                C.wait("sync", "V", cast2[m_ - 2])
            dma("sync", s1, din["W1_p"][:, m_ * 2048:(m_ + 1) * 2048],
                "LW%d" % e_)
            lv = dma("sync", s3, din["W3_p"][:, m_ * 2048:(m_ + 1) * 2048],
                     "LW%d" % e_)
            wts = [("LW%d" % e_, lv)]
            if m_ >= 2:
                wts.append(("T", mmg[(m_ - 2, 1)]))
            vopx(lambda e, s=s1, w=wA: e.tensor_copy(w, s), wts)
            cast2[m_] = vopx(lambda e, s=s3, w=wB: e.tensor_copy(w, s))
            for nh in range(2):
                b0, b1, bh = nh * 2, nh * 2 + 1, 4 + nh
                o5 = nh * 1024
                wt0 = [("V", cast2[m_])]
                if m_ >= 1:
                    wt0 += [("V", evac[(m_ - 1, nh)]), ("S", sig[(m_ - 1, nh)])]
                for k in range(16):
                    mmx(pp[b0][:, :], wA[:, k * 128:(k + 1) * 128],
                        slotA[:, k * 1024 + nh * 512:k * 1024 + nh * 512 + 512],
                        k == 0, k == 15, wt0 if k == 0 else ())
                for k in range(16):
                    mmg[(m_, nh)] = mmx(
                        pp[b1][:, :], wB[:, k * 128:(k + 1) * 128],
                        slotA[:, k * 1024 + nh * 512:k * 1024 + nh * 512 + 512],
                        k == 0, k == 15)
                swt = [("T", mmg[(m_, nh)])]
                if m_ >= 1:
                    swt.append(("V", evac[(m_ - 1, nh)]))
                sig[(m_, nh)] = sopx(lambda e, b=b0, o5=o5: e.activation(
                    scr0[:, o5:o5 + 512], pp[b][:, :], AF.Sigmoid), swt)
                vopx(lambda e, b=b0, o5=o5: e.tensor_mul(
                    scr0[:, o5 + 512:o5 + 1024], scr0[:, o5:o5 + 512],
                    pp[b][:, :]), [("T", mmg[(m_, nh)]), ("S", sig[(m_, nh)])])
                evac[(m_, nh)] = vopx(lambda e, b=b1, o5=o5: e.tensor_mul(
                    scr0bf[:, o5:o5 + 512], scr0[:, o5 + 512:o5 + 1024],
                    pp[b][:, :]))
                hwt = [("V", evac[(m_, nh)])]
                if m_ >= 1:
                    hwt.append(("V", cpy[(m_ - 1, nh)]))
                h128t[(m_, nh)] = mmx(pp[bh][:, :], H128_bf[:, :],
                                      scr0bf[:, o5:o5 + 512], True, True, hwt)
                gs = "GA" if nh == 0 else "GB"
                cpy[(m_, nh)] = vopx(lambda e, b=bh, o5=o5: e.tensor_copy(
                    scr0bf[:, o5 + 512:o5 + 1024], pp[b][:, :]),
                    [("T", h128t[(m_, nh)]), (gs, gv_prev[nh])])
                C.wait("gpsimd", "V", cpy[(m_, nh)])
                for l in range(2):
                    def g_sh(e, m_=m_, l=l, nh=nh, o5=o5):
                        dstap = shufd[m_ * 2 + l:m_ * 2 + l + 1,
                                      0:64 * 1024].rearrange(
                            "p (u hh t) -> (p u) hh t", hh=2, t=512)
                        return e.dma_start(out=dstap[:, nh, :],
                                           in_=scr0bf[l:128:2, o5 + 512:o5 + 1024])
                    C.op("gpsimd", g_sh)
                    gv = C.inc("gpsimd", gs, 16)
                gv_prev[nh] = gv
        C.fence()
        w2c, hgt, h64 = {}, {}, {}
        odlast = {}
        oduse = {0: 0, 1: 0, 2: 0, 3: 0}
        f2use = {0: 0, 1: 0}
        for grp in range(16):
            for j in range(4):
                m2 = grp * 4 + j
                jj = j % 2
                fb = f2t[jj]
                prev = (grp, j - 2) if j >= 2 else (grp - 1, j + 2)
                C.wait("sync", "T", f2use[jj])
                C.wait("sync", "V", w2c.get(prev, 0))
                lv = dma("sync", fb, shufd[:, m2 * 1024:(m2 + 1) * 1024],
                         "LX%d" % jj)
                lv2 = dma("sync", stage4[2 + jj],
                          din["W2Tp"][m2 * 128:(m2 + 1) * 128, :],
                          "LW%d" % jj)
                cwt = [("LW%d" % jj, lv2)]
                if grp > 0:
                    cwt.append(("T", odlast[grp - 1]))
                w2c[(grp, j)] = vopx(lambda e, j=j, s=stage4[2 + jj]:
                                     e.tensor_copy(
                    w2sl[:, j * 2048:(j + 1) * 2048], s), cwt)
                t0 = mmx(pp[6 + jj][:, :], H64I2_bf[:, :], fb[:, 0:512],
                         True, True,
                         [("LX%d" % jj, lv), ("V", hgt.get(prev, 0))])
                hwt = [("T", t0)]
                if grp > 0:
                    hwt.append(("T", odlast[grp - 1]))
                g0 = vopx(lambda e, j=j, jj=jj: e.tensor_scalar(
                    hgrp[:, j * 1024:j * 1024 + 512],
                    pp[6 + jj][:, :], R8192, None, ALU.mult), hwt)
                t1 = mmx(pp[6 + jj][:, :], H64I2_bf[:, :], fb[:, 512:1024],
                         True, True, [("V", g0)])
                hgt[(grp, j)] = vopx(lambda e, j=j, jj=jj: e.tensor_scalar(
                    hgrp[:, j * 1024 + 512:j * 1024 + 1024],
                    pp[6 + jj][:, :], R8192, None, ALU.mult), [("T", t1)])
                f2use[jj] = t1
            for od in range(16):
                for nh in range(2):
                    bk = (od * 2 + nh) % 4
                    gwt = [("V", oduse[bk])]
                    if od == 0 and nh == 0:
                        gwt += [("V", w2c[(grp, 3)]), ("V", hgt[(grp, 3)])]
                    for j in range(4):
                        tvx = mmx(
                            pp[bk][:, :],
                            w2sl[:, j * 2048 + od * 128:j * 2048 + od * 128 + 128],
                            hgrp[:, j * 1024 + nh * 512:j * 1024 + nh * 512 + 512],
                            j == 0, j == 3, gwt if j == 0 else ())
                    odlast[grp] = tvx
                    oduse[bk] = vopx(lambda e, od=od, nh=nh, bk=bk: e.tensor_add(
                        slotB[:, od * 1024 + nh * 512:od * 1024 + nh * 512 + 512],
                        slotB[:, od * 1024 + nh * 512:od * 1024 + nh * 512 + 512],
                        pp[bk][:, :]), [("T", tvx)])
        C.fence()
        for fc in range(16):
            vop(lambda e, fc=fc: e.tensor_add(
                slotB[:, fc * 1024:(fc + 1) * 1024],
                slotB[:, fc * 1024:(fc + 1) * 1024],
                slotA[:, fc * 1024:(fc + 1) * 1024]))
        layer_norm(slotB, sq_bf0, mbc0, rbc0, scr0[:, 0:1024])

    phase_ffn0()
    if stop_after == "xB":
        dump("xB", slotB, 2048, 1024)
        return finish()

    AL.pop()
    AL.push()
    q1 = AL.alloc(8 * 1024, BF16)
    k1 = AL.alloc(4 * 1536, BF16)
    v1t = AL.alloc(4 * 1536, BF16)
    wstage1 = [AL.alloc(2048), AL.alloc(2048)]
    w1bf2 = [AL.alloc(2048, BF16), AL.alloc(2048, BF16)]
    c1tab = AL.alloc(1024)
    s1tab = AL.alloc(1024)
    s1tab_at = AL.last_at
    kvstage = [AL.alloc(512, BF16), AL.alloc(512, BF16)]
    scr1 = AL.alloc(2048)
    scr1b = AL.alloc(1024)
    scr1bf = AL.alloc(1536, BF16)
    expt1 = [AL.alloc(512, BF16), AL.alloc(512, BF16)]
    sq_bf1 = AL.alloc(1024, BF16)
    mbc1 = AL.alloc(1024)
    rbc1 = AL.alloc(1024)
    L1A_PEAK = AL.off

    def phase_l1a():
        C.fence(("sync",))
        dma("sync", c1tab, din["cos1"][:, :], "LE")
        lv = dma("sync", s1tab, din["sin1"][:, :], "LE")
        C.wait("vector", "LE", lv)
        C.wait("tensor", "LE", lv)
        qst = {"hist": [], "cast": {}, "mmg": {}, "evac": {}, "ptv": 0,
               "sbfT": 0, "tpv": {3: 0, 4: 0}}

        def slab_load(m_, dname):
            h = qst["hist"]
            uid = len(h)
            p2 = h[-2] if uid >= 2 else None
            p1 = h[-1] if uid >= 1 else None
            h.append(uid)
            e_ = uid % 2
            sl = wstage1[e_]
            if p2 is not None:
                C.wait("sync", "V", qst["cast"][p2])
            lv = dma("sync", sl, din[dname][:, m_ * 2048:(m_ + 1) * 2048],
                     "LW%d" % e_)
            cwt = [("LW%d" % e_, lv)]
            if p2 is not None:
                cwt.append(("T", qst["mmg"][p2]))
            qst["cast"][uid] = vopx(lambda e, s=sl, w=w1bf2[e_]:
                                    e.tensor_copy(w, s), cwt)
            return uid, e_, p1

        def qkv1_slab(m_):
            uid, e_, p1 = slab_load(m_, "wqkv1_p")
            for nh in range(2):
                wt0 = [("V", qst["cast"][uid])]
                if p1 is not None:
                    wt0.append(("V", qst["evac"][p1]))
                for k in range(16):
                    qst["mmg"][uid] = mmx(
                        pp[nh][:, :], w1bf2[e_][:, k * 128:(k + 1) * 128],
                        slotB[:, k * 1024 + nh * 512:k * 1024 + nh * 512 + 512],
                        k == 0, k == 15, wt0 if k == 0 else ())
            return uid

        def rope_post(uid, dst_ap):
            cwt = [("T", qst["mmg"][uid]), ("T", qst["sbfT"])]
            vopx(lambda e: e.tensor_copy(scr1bf[:, 0:512], pp[0][:, :]), cwt)
            ev = vopx(lambda e: e.tensor_copy(scr1bf[:, 512:1024], pp[1][:, :]))
            qst["evac"][uid] = ev
            ptl = 0
            for half in range(2):
                ptl = mmx(pp[2 + half][:, :], PT_bf[:, :],
                          scr1bf[:, half * 512:half * 512 + 512], True, True,
                          [("V", ev), ("V", qst["ptv"])])
            qst["sbfT"] = ptl
            vopx(lambda e: e.tensor_mul(scr1[:, 0:1024], scr1bf[:, 0:1024],
                                        c1tab[:, :]))
            vopx(lambda e: e.tensor_mul(scr1b[:, 0:512], pp[2][:, :],
                                        s1tab[:, 0:512]), [("T", ptl)])
            vopx(lambda e: e.tensor_mul(scr1b[:, 512:1024], pp[3][:, :],
                                        s1tab[:, 512:1024]))
            qst["ptv"] = vopx(lambda e, d=dst_ap: e.tensor_add(
                d, scr1[:, 0:1024], scr1b[:, 0:1024]))

        C.fence()
        for m_ in range(16, 24):
            uid = qkv1_slab(m_)
            kvh = (m_ - 16) % 4
            if m_ < 20:
                rope_post(uid, k1[:, kvh * 1536 + 512:kvh * 1536 + 1536])
            else:
                cwt = [("T", qst["mmg"][uid]), ("T", qst["sbfT"])]
                vopx(lambda e: e.tensor_copy(scr1bf[:, 0:512], pp[0][:, :]), cwt)
                ev = vopx(lambda e: e.tensor_copy(scr1bf[:, 512:1024],
                                                  pp[1][:, :]))
                qst["evac"][uid] = ev
                tpt = 0
                for sub in range(8):
                    tb = 3 if sub % 2 == 0 else 4
                    tpt = C.opx("tensor", lambda e, tb=tb, sub=sub: e.transpose(
                        ppb[tb][:, 0:128],
                        scr1bf[:, sub * 128:sub * 128 + 128], id_bf[:, :]),
                        [("V", ev), ("V", qst["tpv"][tb])])
                    qst["tpv"][tb] = vopx(lambda e, tb=tb, kvh=kvh, sub=sub:
                                          e.tensor_copy(
                        v1t[:, kvh * 1536 + 512 + sub * 128:
                            kvh * 1536 + 512 + sub * 128 + 128],
                        ppb[tb][:, 0:128]), [("T", tpt)])
                qst["sbfT"] = tpt
        C.fence(("gpsimd",))
        for kvh in range(4):
            dma("gpsimd", ag_kv_in[kvh * 128:(kvh + 1) * 128, :],
                k1[:, kvh * 1536 + 1024:kvh * 1536 + 1536])
            dma("gpsimd", ag_kv_in[(4 + kvh) * 128:(5 + kvh) * 128, :],
                v1t[:, kvh * 1536 + 1024:kvh * 1536 + 1536])
        C.wait("gpsimd", "G", C.cnt["G"])
        cv_kv = collective("AllGather", ALU.bypass, ag_kv_in, ag_kv_out, "CCA")

        def q_group(g2):
            for hl in range(8):
                uid = qkv1_slab(g2 * 8 + hl)
                rope_post(uid, q1[:, hl * 1024:(hl + 1) * 1024])

        def halo_select():
            C.wait("sync", "CCA", cv_kv)
            ks_t = {0: 0, 1: 0}
            for r in range(4):
                for ch in range(8):
                    stg = kvstage[ch % 2]
                    if r > 0 or ch >= 2:
                        C.wait("sync", "V", ks_t[ch % 2])
                    lv = dma("sync", stg,
                             ag_kv_out[r * 1024 + ch * 128:
                                       r * 1024 + (ch + 1) * 128, :],
                             "LX%d" % (ch % 2))
                    C.wait("vector", "LX%d" % (ch % 2), lv)
                    if ch < 4:
                        dst = k1[:, ch * 1536:ch * 1536 + 512]
                    else:
                        dst = v1t[:, (ch - 4) * 1536:(ch - 4) * 1536 + 512]
                    if r == 0:
                        ks_t[ch % 2] = vop(lambda e, d=dst, s=stg: e.tensor_scalar(
                            d, s, hsel[:, 0:1], None, ALU.mult))
                    else:
                        ks_t[ch % 2] = vop(
                            lambda e, d=dst, s=stg, r=r: e.scalar_tensor_tensor(
                                d, s, hsel[:, r:r + 1], d, ALU.mult, ALU.add))

        def l1_attention(g2):
            for hl in range(8):
                h = g2 * 8 + hl
                kvh = h // 4
                for qt in range(2):
                    q0l = qt * 512
                    subs = []
                    for k0 in range(q0l - 512, q0l + 512, 128):
                        delta = k0 - q0l
                        if delta < 0:
                            mk = maskWc_bf if k0 < 0 else maskW_bf
                            subs.append((k0, mk, (delta + 512) // 128))
                        else:
                            subs.append((k0, maskM_bf, delta // 128))
                    nsub = len(subs)
                    for i, (k0, mk, j) in enumerate(subs):
                        kb = k0 + 512
                        bk = i % 2
                        mm(pp[bk][:, :],
                           k1[:, kvh * 1536 + kb:kvh * 1536 + kb + 128],
                           q1[:, hl * 1024 + q0l:hl * 1024 + q0l + 512])
                        eb = expt1[i % 2]
                        sop(lambda e, eb=eb, bk=bk: e.activation(
                            eb[:, :], pp[bk][:, :], AF.Exp, bias=-C_L1, scale=SC0))
                        vop(lambda e, eb=eb, mk=mk, j=j: e.tensor_mul(
                            eb[:, :], eb[:, :], mk[:, 512 - 128 * j:1024 - 128 * j]))
                        colsum(pp[3][:1, :], eb[:, :], i == 0, i == nsub - 1)
                        mm(pp[2][:, :],
                           v1t[:, kvh * 1536 + (kb // 128) * 128:
                               kvh * 1536 + (kb // 128) * 128 + 128],
                           eb[:, :], i == 0, i == nsub - 1)
                    vop(lambda e: e.reciprocal(rows[:1, 0:512], pp[3][:1, :]))
                    vop(lambda e: e.tensor_copy(scr1[:, 0:512], pp[2][:, :]))
                    bcast(pp[4][:, :], rows[:1, 0:512])
                    vop(lambda e, hl=hl, q0l=q0l: e.tensor_mul(
                        q1[:, hl * 1024 + q0l:hl * 1024 + q0l + 512],
                        scr1[:, 0:512], pp[4][:, :]))

        def wo1_partial(g2):
            addw = {}
            for m_ in range(16):
                uid, e_, p1 = slab_load(m_, "wo1_p")
                for nh in range(2):
                    wt0 = [("V", qst["cast"][uid]), ("V", C.cnt["V"])]
                    if p1 is not None:
                        wt0.append(("V", qst["evac"][p1]))
                    for kl in range(8):
                        k = g2 * 8 + kl
                        qst["mmg"][uid] = mmx(
                            pp[nh][:, :], w1bf2[e_][:, k * 128:(k + 1) * 128],
                            q1[:, kl * 1024 + nh * 512:kl * 1024 + nh * 512 + 512],
                            kl == 0, kl == 7, wt0 if kl == 0 else ())
                for nh in range(2):
                    if g2 == 0:
                        addw[nh] = vopx(lambda e, m_=m_, nh=nh: e.tensor_add(
                            slotA[:, m_ * 1024 + nh * 512:
                                  m_ * 1024 + nh * 512 + 512],
                            pp[nh][:, :],
                            slotB[:, m_ * 1024 + nh * 512:
                                  m_ * 1024 + nh * 512 + 512]),
                            [("T", qst["mmg"][uid])])
                    else:
                        addw[nh] = vopx(lambda e, m_=m_, nh=nh: e.tensor_add(
                            slotA[:, m_ * 1024 + nh * 512:
                                  m_ * 1024 + nh * 512 + 512],
                            slotA[:, m_ * 1024 + nh * 512:
                                  m_ * 1024 + nh * 512 + 512],
                            pp[nh][:, :]), [("T", qst["mmg"][uid])])
                qst["evac"][uid] = addw[1]

        q_group(0)
        halo_select()
        l1_attention(0)
        wo1_partial(0)
        q_group(1)
        marks["wm"] = dict(C.cnt)
        l1_attention(1)
        wo1_partial(1)
        layer_norm(slotA, sq_bf1, mbc1, rbc1, scr1[:, 0:1024])

    phase_l1a()
    if stop_after == "xC":
        dump("xC", slotA, 2048, 1024)
        return finish()

    AL.pop()
    AL.push()
    xq = AL.alloc(16 * 1024, BF16)
    stageF = [AL.alloc(2048), AL.alloc(2048)]
    wslbf = AL.alloc(4 * 2048, BF16)
    scFbf = AL.alloc(2048, BF16)
    hgrp2 = AL.alloc(4 * 1024, BF16)
    f2b = [AL.alloc(1024, BF16), AL.alloc(1024, BF16)]
    scbc = AL.alloc(1024)
    invbc = AL.alloc(1024)
    scF = AL.alloc(2048)
    mxtab = AL.alloc(256)
    i32t = AL.alloc(1024, I32)
    sq_bf2 = AL.alloc(1024, BF16)
    mbc2 = AL.alloc(1024)
    rbc2 = AL.alloc(1024)
    outf = [AL.alloc(1024), AL.alloc(1024)]
    L1F_PEAK = AL.off

    wm_stage = [AL.view(slotB_at, 2048, F32),
                AL.view(slotB_at + 8192, 2048, F32)]
    wm_cols = AL.view(s1tab_at, 64, F32)

    def wmean_scale(dram, nld, slot, entry=None):
        wm_t = {0: 0, 1: 0}
        for r in range(nld):
            buf = wm_stage[r % 2]
            if r >= 2:
                C.wait("sync", "V", wm_t[r % 2])
            elif entry is not None:
                for sem in ("T", "V", "S"):
                    C.wait("sync", sem, entry[sem])
            lv = dma("sync", buf, dram[r], "LM%d" % (r % 2))
            wm_t[r % 2] = vopx(lambda e, b=buf, r=r: e.tensor_reduce(
                wm_cols[:, r:r + 1], b, AX.X, ALU.add,
                apply_absolute_value=True), [("LM%d" % (r % 2), lv)])
        sop(lambda e, nld=nld: e.add(wm_cols[:, 0:nld], wm_cols[:, 0:nld], 0.0))
        vop(lambda e, nld=nld: e.tensor_reduce(
            wm_cols[:, 40:41], wm_cols[:, 0:nld], AX.X, ALU.add))
        mm(pp[0][:1, 0:1], onesc_f[:, 0:1], wm_cols[:, 40:41])
        vop(lambda e: e.tensor_copy(rows[:1, 1024:1025], pp[0][:1, 0:1]))
        sop(lambda e: e.activation(rows[:1, 1025:1026], rows[:1, 1024:1025],
                                   AF.Ln, bias=0.0, scale=1.0 / 8388608.0))
        vop(lambda e: e.tensor_scalar(rows[:1, 1026:1027], rows[:1, 1025:1026],
                                      math.log(1e-5), None, ALU.max))
        sop(lambda e: e.activation(rows[:1, 4:5], rows[:1, 1026:1027],
                                   AF.Exp, bias=0.0, scale=-1.0))
        sop(lambda e: e.activation(rows[:1, 5:6], rows[:1, 1026:1027],
                                   AF.Exp, bias=math.log(0.25), scale=2.0))
        sop(lambda e: e.activation(rows[:1, 6:7], rows[:1, 1026:1027],
                                   AF.Exp, bias=math.log(0.5), scale=1.0))
        bcast(pp[0][:, 0:3], rows[:1, 4:7])
        vop(lambda e, slot=slot: e.tensor_copy(srow[:, slot:slot + 1],
                                               pp[0][:, 0:1]))
        vop(lambda e, slot=slot: e.tensor_copy(srow[:, slot + 1:slot + 2],
                                               pp[0][:, 1:2]))
        vop(lambda e, slot=slot: e.tensor_copy(srow[:, slot + 2:slot + 3],
                                               pp[0][:, 2:3]))

    def fwht_rows_from_maxcols(nsc, stride):
        for tt in range(8):
            vop(lambda e, tt=tt, st=stride: e.tensor_reduce(
                mxacc[:, tt:tt + 1], mxtab[:, tt * st:(tt + 1) * st],
                AX.X, ALU.max))
        for tt in range(8):
            sop(lambda e, tt=tt: e.add(scFbf[:, tt:tt + 1],
                                       mxacc[:, tt:tt + 1], 0.0))
        for tt in range(8):
            tb = 5 if tt % 2 == 0 else 6
            transpose(ppb[tb][:1, 0:128], scFbf[:, tt:tt + 1])
            vop(lambda e, tb=tb, tt=tt: e.tensor_copy(
                rows[:1, tt * 128:(tt + 1) * 128], ppb[tb][:1, 0:128]))
        sop(lambda e: e.add(rows[:1, 0:1024], rows[:1, 0:1024], 0.0))
        vop(lambda e, nsc=nsc: e.tensor_scalar(rows[:1, 0:1024], rows[:1, 0:1024],
                                               nsc, None, ALU.mult))
        vop(lambda e: e.tensor_scalar(rows[:1, 0:1024], rows[:1, 0:1024],
                                      1e-5, None, ALU.max))
        vop(lambda e: e.reciprocal(rows2[:1, 0:1024], rows[:1, 0:1024]))
        vop(lambda e: e.tensor_scalar(rows2[:1, 0:1024], rows2[:1, 0:1024],
                                      127.0, None, ALU.mult))
        vop(lambda e: e.tensor_scalar(rows[:1, 0:1024], rows[:1, 0:1024],
                                      1.0 / 127.0, None, ALU.mult))
        for half in range(2):
            h0 = half * 512
            bcast(pp[0][:, :], rows2[:1, h0:h0 + 512])
            bcast(pp[1][:, :], rows[:1, h0:h0 + 512])
            vop(lambda e, h0=h0: e.tensor_copy(scbc[:, h0:h0 + 512], pp[0][:, :]))
            vop(lambda e, h0=h0: e.tensor_copy(invbc[:, h0:h0 + 512], pp[1][:, :]))

    def phase_l1f():
        wmean_scale([din["wup_p"][:, r * 2048:(r + 1) * 2048] for r in range(32)],
                    32, 0, entry=marks["wm"])
        if stop_after == "wsc":
            dump("srow", srow, 128, 8)
            return True
        C.fence(("gpsimd",))
        for u in range(8):
            def g_pre(e, u=u):
                src = slotA[u * 16:(u + 1) * 16, :].rearrange(
                    "cp (c t) -> cp c t", t=1024)
                dstap = shufd[:, 0:16 * 1024].rearrange("p (cp t) -> p cp t", t=1024)
                return e.dma_start(out=dstap[u:128:8, :, :].transpose([1, 0, 2]),
                                   in_=src)
            C.op("gpsimd", g_pre)
            C.inc("gpsimd", "G", 16)
        C.wait("gpsimd", "G", C.cnt["G"])
        gv = dma("gpsimd", xq[:, :], shufd[:, 0:16 * 1024])
        C.wait("tensor", "G", gv)
        C.wait("vector", "G", gv)
        for fc in range(16):
            for nh in range(2):
                mm(pp[nh][:, :], H128_bf[:, :],
                   xq[:, fc * 1024 + nh * 512:fc * 1024 + nh * 512 + 512])
            for nh in range(2):
                vop(lambda e, fc=fc, nh=nh: e.tensor_copy(
                    xq[:, fc * 1024 + nh * 512:fc * 1024 + nh * 512 + 512],
                    pp[nh][:, :]))
        C.fence(("gpsimd",))
        for l in range(8):
            def g_mid(e, l=l):
                src = xq[l:128:8, :].rearrange("hg (c t) -> hg c t", t=1024)
                dstap = shufd[:, 0:16 * 1024].rearrange("p (hg t) -> p hg t", t=1024)
                return e.dma_start(out=dstap[l:128:8, :, :].transpose([1, 0, 2]),
                                   in_=src)
            C.op("gpsimd", g_mid)
            C.inc("gpsimd", "G", 16)
        C.wait("gpsimd", "G", C.cnt["G"])
        gv = dma("gpsimd", xq[:, :], shufd[:, 0:16 * 1024])
        C.wait("tensor", "G", gv)
        C.wait("vector", "G", gv)
        for fc in range(16):
            for tt in range(8):
                bb = 2 + (tt % 4)
                mm(pp[bb][:, 0:128],
                   xq[:, fc * 1024 + tt * 128:fc * 1024 + tt * 128 + 128],
                   H16I8_bf[:, :])
                vop(lambda e, bb=bb, tt=tt, fc=fc: e.tensor_reduce(
                    mxtab[:, tt * 16 + fc:tt * 16 + fc + 1], pp[bb][:, 0:128],
                    AX.X, ALU.max, apply_absolute_value=True))
            for nh in range(2):
                mm(pp[nh][:, :], H16I8_bf[:, :],
                   xq[:, fc * 1024 + nh * 512:fc * 1024 + nh * 512 + 512])
            for nh in range(2):
                vop(lambda e, fc=fc, nh=nh: e.tensor_scalar(
                    xq[:, fc * 1024 + nh * 512:fc * 1024 + nh * 512 + 512],
                    pp[nh][:, :], R2048, None, ALU.mult))
        fwht_rows_from_maxcols(R2048, 16)
        if stop_after == "xh1":
            dump("xh", xq, 2048, 1024)
            dump("scbc", scbc, 128, 1024)
            dump("invbc", invbc, 128, 1024)
            return True
        for fc in range(16):
            vop(lambda e, fc=fc: e.tensor_mul(
                scF[:, 0:1024], xq[:, fc * 1024:(fc + 1) * 1024], scbc[:, :]))
            vop(lambda e: e.tensor_copy(i32t[:, :], scF[:, 0:1024]))
            vop(lambda e, fc=fc: e.scalar_tensor_tensor(
                xq[:, fc * 1024:(fc + 1) * 1024], i32t[:, :], 1.0, invbc[:, :],
                ALU.mult, ALU.mult))
        if stop_after == "xq1":
            dump("xq", xq, 2048, 1024)
            return True
        C.fence()
        sg2, addt, mmg, rel, stt, h128t, cpy = {}, {}, {}, {}, {}, {}, {}
        gv_prev = {0: C.cnt["GA"], 1: C.cnt["GB"]}
        for m_ in range(32):
            e_ = m_ % 2
            sl = stageF[e_]
            sA = wslbf[:, e_ * 4096:e_ * 4096 + 2048]
            sB = wslbf[:, e_ * 4096 + 2048:e_ * 4096 + 4096]
            if m_ >= 2:
                C.wait("sync", "S", sg2[m_ - 2])
            lv = dma("sync", sl, din["wup_p"][:, m_ * 2048:(m_ + 1) * 2048],
                     "LW%d" % e_)
            swt = [("LW%d" % e_, lv)]
            if m_ >= 2:
                swt.append(("T", mmg[(m_ - 2, 1)]))
            sopx(lambda e, s=sl, w=sA: e.activation(
                w, s, AF.Sign, bias=-0.5, scale=srow[:, 0:1]), swt)
            sg2[m_] = sopx(lambda e, s=sl, w=sB: e.activation(
                w, s, AF.Sign, bias=0.5, scale=srow[:, 0:1]),
                [("V", addt.get(m_ - 2, 0))])
            addt[m_] = vopx(lambda e, a=sA, b=sB: e.tensor_add(a, a, b),
                            [("S", sg2[m_])])
            for nh in range(2):
                o5 = nh * 1024
                wt0 = [("V", addt[m_])]
                if m_ >= 1:
                    wt0.append(("S", rel[(m_ - 1, nh)]))
                for k in range(16):
                    mmg[(m_, nh)] = mmx(
                        pp[nh][:, :], sA[:, k * 128:(k + 1) * 128],
                        xq[:, k * 1024 + nh * 512:k * 1024 + nh * 512 + 512],
                        k == 0, k == 15, wt0 if k == 0 else ())
                rwt = [("T", mmg[(m_, nh)])]
                if m_ >= 1:
                    rwt.append(("V", stt[(m_ - 1, nh)]))
                rel[(m_, nh)] = sopx(lambda e, nh=nh: e.activation(
                    scF[:, nh * 512:nh * 512 + 512], pp[nh][:, :], AF.Relu), rwt)
                stt[(m_, nh)] = vopx(lambda e, nh=nh, o5=o5:
                                     e.scalar_tensor_tensor(
                    scFbf[:, o5:o5 + 512], scF[:, nh * 512:nh * 512 + 512],
                    srow[:, 1:2], scF[:, nh * 512:nh * 512 + 512],
                    ALU.mult, ALU.mult), [("S", rel[(m_, nh)])])
                hwt = [("V", stt[(m_, nh)])]
                if m_ >= 1:
                    hwt.append(("V", cpy[(m_ - 1, nh)]))
                h128t[(m_, nh)] = mmx(pp[2 + nh][:, :], H128_bf[:, :],
                                      scFbf[:, o5:o5 + 512], True, True, hwt)
                gs = "GA" if nh == 0 else "GB"
                cpy[(m_, nh)] = vopx(lambda e, nh=nh, o5=o5: e.tensor_copy(
                    scFbf[:, o5 + 512:o5 + 1024], pp[2 + nh][:, :]),
                    [("T", h128t[(m_, nh)]), (gs, gv_prev[nh])])
                C.wait("gpsimd", "V", cpy[(m_, nh)])
                for l in range(4):
                    def g_sh4(e, m_=m_, l=l, nh=nh, o5=o5):
                        dstap = shufd[m_ * 4 + l:m_ * 4 + l + 1,
                                      0:32 * 1024].rearrange(
                            "p (u hh t) -> (p u) hh t", hh=2, t=512)
                        return e.dma_start(out=dstap[:, nh, :],
                                           in_=scFbf[l:128:4, o5 + 512:o5 + 1024])
                    C.op("gpsimd", g_sh4)
                    gv = C.inc("gpsimd", gs, 16)
                gv_prev[nh] = gv
        wmean_scale([din["wdnTp"][r * 128:(r + 1) * 128, :] for r in range(32)],
                    32, 4)
        C.fence()
        if stop_after == "h4s":
            if "h4s" in tap_t:
                stg = AL.view(AL.nbytes - 8192, 2048, F32)
                for m2 in range(32):
                    lv = dma("sync", f2b[0], shufd[:, m2 * 1024:(m2 + 1) * 1024],
                             "LX0")
                    C.wait("vector", "LX0", lv)
                    vv = vop(lambda e: e.tensor_copy(stg[:, 0:1024], f2b[0]))
                    C.wait("gpsimd", "V", vv)
                    gv = dma("gpsimd", tap_t["h4s"][m2 * 128:(m2 + 1) * 128, :],
                             stg[:, 0:1024])
                    C.wait("vector", "G", gv)
                    C.wait("sync", "G", gv)
            return True
        mx_t = {0: 0, 1: 0}
        for m2 in range(32):
            fb = f2b[m2 % 2]
            if m2 >= 2:
                C.wait("sync", "T", mx_t[m2 % 2])
            lv = dma("sync", fb, shufd[:, m2 * 1024:(m2 + 1) * 1024],
                     "LX%d" % (m2 % 2))
            C.wait("tensor", "LX%d" % (m2 % 2), lv)
            for tt in range(8):
                bb = 2 + (tt % 4)
                tvx = mm(pp[bb][:, 0:128], fb[:, tt * 128:(tt + 1) * 128],
                         H32I4_bf[:, :])
                vop(lambda e, bb=bb, tt=tt, m2=m2: e.tensor_reduce(
                    mxtab[:, tt * 32 + m2:tt * 32 + m2 + 1], pp[bb][:, 0:128],
                    AX.X, ALU.max, apply_absolute_value=True))
            mx_t[m2 % 2] = tvx
        fwht_rows_from_maxcols(R4096, 32)
        if stop_after == "fw4m":
            dump("scbc", scbc, 128, 1024)
            dump("invbc", invbc, 128, 1024)
            return True
        vop(lambda e: e.memset(slotB[:, :], 0.0))
        C.fence()
        sg1d, sg2d, addd, qtt, bankev = {}, {}, {}, {}, {}
        odlast = {}
        oduse = {0: 0, 1: 0, 2: 0, 3: 0}
        f2use = {0: 0, 1: 0}
        for grp in range(8):
            for j in range(4):
                m2 = grp * 4 + j
                jj = j % 2
                fb = f2b[jj]
                prev = (grp, j - 2) if j >= 2 else (grp - 1, j + 2)
                prevj = (grp, j - 1) if j >= 1 else (grp - 1, 3)
                C.wait("sync", "T", f2use[jj])
                C.wait("sync", "S", sg2d.get(prev, 0))
                lv = dma("sync", fb, shufd[:, m2 * 1024:(m2 + 1) * 1024],
                         "LX%d" % jj)
                sl = stageF[jj]
                lv2 = dma("sync", sl, din["wdnTp"][m2 * 128:(m2 + 1) * 128, :],
                          "LW%d" % jj)
                swt = [("LW%d" % jj, lv2), ("V", addd.get(prevj, 0))]
                sg1d[(grp, j)] = sopx(lambda e, s=sl: e.activation(
                    scFbf[:, 0:2048], s, AF.Sign, bias=-0.5,
                    scale=srow[:, 4:5]), swt)
                s2wt = []
                if grp > 0:
                    s2wt.append(("T", odlast[grp - 1]))
                sg2d[(grp, j)] = sopx(lambda e, s=sl, j=j: e.activation(
                    wslbf[:, j * 2048:(j + 1) * 2048], s, AF.Sign, bias=0.5,
                    scale=srow[:, 4:5]), s2wt)
                addd[(grp, j)] = vopx(lambda e, j=j: e.tensor_add(
                    wslbf[:, j * 2048:(j + 1) * 2048],
                    wslbf[:, j * 2048:(j + 1) * 2048],
                    scFbf[:, 0:2048]), [("S", sg2d[(grp, j)])])
                for nh in range(2):
                    o6 = nh * 1024
                    mwt = [("V", bankev.get(jj, 0))]
                    if nh == 0:
                        mwt.append(("LX%d" % jj, lv))
                    tvx = mmx(pp[6 + jj][:, :], H32I4_bf[:, :],
                              fb[:, nh * 512:nh * 512 + 512], True, True, mwt)
                    bankev[jj] = vopx(lambda e, o6=o6, jj=jj: e.tensor_scalar(
                        scF[:, o6:o6 + 512], pp[6 + jj][:, :], R4096, None,
                        ALU.mult), [("T", tvx)])
                    vopx(lambda e, o6=o6, nh=nh: e.tensor_mul(
                        scF[:, o6 + 512:o6 + 1024], scF[:, o6:o6 + 512],
                        scbc[:, nh * 512:nh * 512 + 512]))
                    vopx(lambda e, nh=nh, o6=o6: e.tensor_copy(
                        i32t[:, nh * 512:nh * 512 + 512],
                        scF[:, o6 + 512:o6 + 1024]))
                    hwt = []
                    if grp > 0:
                        hwt.append(("T", odlast[grp - 1]))
                    qtt[(grp, j)] = vopx(lambda e, j=j, nh=nh:
                                         e.scalar_tensor_tensor(
                        hgrp2[:, j * 1024 + nh * 512:j * 1024 + nh * 512 + 512],
                        i32t[:, nh * 512:nh * 512 + 512], 1.0,
                        invbc[:, nh * 512:nh * 512 + 512],
                        ALU.mult, ALU.mult), hwt)
                    f2use[jj] = tvx
            for od in range(16):
                for nh in range(2):
                    bk = (od * 2 + nh) % 4
                    gwt = [("V", oduse[bk])]
                    if od == 0 and nh == 0:
                        gwt += [("V", addd[(grp, 3)]), ("V", qtt[(grp, 3)])]
                    for j in range(4):
                        tvx = mmx(
                            pp[bk][:, :],
                            wslbf[:, j * 2048 + od * 128:j * 2048 + od * 128 + 128],
                            hgrp2[:, j * 1024 + nh * 512:j * 1024 + nh * 512 + 512],
                            j == 0, j == 3, gwt if j == 0 else ())
                    odlast[grp] = tvx
                    oduse[bk] = vopx(lambda e, od=od, nh=nh, bk=bk:
                                     e.scalar_tensor_tensor(
                        slotB[:, od * 1024 + nh * 512:od * 1024 + nh * 512 + 512],
                        pp[bk][:, :], srow[:, 6:7],
                        slotB[:, od * 1024 + nh * 512:od * 1024 + nh * 512 + 512],
                        ALU.mult, ALU.add), [("T", tvx)])
        C.fence()
        for fc in range(16):
            vop(lambda e, fc=fc: e.tensor_add(
                slotB[:, fc * 1024:(fc + 1) * 1024],
                slotB[:, fc * 1024:(fc + 1) * 1024],
                slotA[:, fc * 1024:(fc + 1) * 1024]))
        out_g = {0: 0, 1: 0}

        def out_fn(fc, scrA_, rbc_):
            ob = outf[fc % 2]
            gs = "GA" if fc % 2 == 0 else "GB"
            C.wait("vector", gs, out_g[fc % 2])
            vv = vop(lambda e, ob=ob: e.tensor_mul(ob[:, :], scrA_[:, 0:1024],
                                                   rbc_[:, :]))
            C.wait("gpsimd", "V", vv)
            out_g[fc % 2] = dma("gpsimd", y_out[fc * 128:(fc + 1) * 128, :],
                                ob[:, :], gs)

        layer_norm(slotB, sq_bf2, mbc2, rbc2, scF[:, 0:1024], out_fn=out_fn)
        C.wait("gpsimd", "GA", C.cnt["GA"])
        C.wait("gpsimd", "GB", C.cnt["GB"])

    if phase_l1f():
        return finish()
    C.fence()
    C.replay(es.enter_context(nc.Block()))
    es.close()
    return nc


_NC_CACHE = {}


def kernel(**inputs):
    in_maps = host_prepare(inputs)
    key = "full"
    if key not in _NC_CACHE:
        _NC_CACHE[key] = build_nc(stop_after="out")
    nc = _NC_CACHE[key]
    res = run_bass_kernel_spmd(nc, in_maps, core_ids=list(range(8)))
    out = np.zeros((B, S, D), np.float32)
    for core in range(8):
        g, t = core // 4, core % 4
        out[g, t * TOK:(t + 1) * TOK, :] = np.asarray(res.results[core]["y_out"]).T
    return out


# version 10
# speedup vs baseline: 1.0028x; 1.0028x over previous
import bisect
import math
import numpy as np
import ml_dtypes
from contextlib import ExitStack
import concourse.bass as bass
import concourse.mybir as mybir
from concourse.bass_utils import run_bass_kernel_spmd

F32 = mybir.dt.float32
BF16 = mybir.dt.bfloat16
I32 = mybir.dt.int32
AF = mybir.ActivationFunctionType
ALU = mybir.AluOpType
AX = mybir.AxisListType

B, S, D = 2, 4096, 2048
CH, NCH, TOK = 1024, 4, 1024
BASE_LR_INV = 0.001 + math.log(-math.expm1(-0.001))
C_L0, C_L1 = 24.0, 16.0
SC0 = 1.0 / math.sqrt(128.0)
R2048 = 1.0 / math.sqrt(2048.0)
R4096 = 1.0 / math.sqrt(4096.0)
R8192 = 1.0 / math.sqrt(8192.0)
ENGINES = ("tensor", "vector", "scalar", "sync", "gpsimd")
ARN_ELEMS = 53200

CONST_VALS = sorted(set([
    -C_L0, -C_L1, 0.0, SC0, -0.5, 0.5, 1.0, 1e-6, 1e-5, 1e-24, 1e-12,
    1.0 / D, 1.0 / 512.0, R8192, R4096, R2048,
    -1.0, 2.0, math.log(0.5), math.log(0.25), 1.0 / 8388608.0,
]))


def _hadamard(n):
    h = np.array([[1.0]])
    while h.shape[0] < n:
        h = np.block([[h, h], [h, -h]])
    return h


def _f1_perm(n, nch):
    lb = int(math.log2(nch))
    out = np.zeros(n, np.int64)
    for m in range(nch):
        for p in range(128):
            out[m * 128 + p] = (p << lb) | m
    return out


def _f3_perm(n, nch):
    lb_hi = int(math.log2(nch))
    k = 128 // nch
    lbk = int(math.log2(k))
    out = np.zeros(n, np.int64)
    for m2 in range(nch):
        for p2 in range(128):
            i_hi = (m2 << lbk) | (p2 & (k - 1))
            q = p2 >> lbk
            out[m2 * 128 + p2] = (i_hi << lb_hi) | q
    return out


def _pack_w(W):
    K, N = W.shape
    return np.ascontiguousarray(
        W.reshape(K // 128, 128, N // 128, 128)
        .transpose(1, 2, 0, 3).reshape(128, -1))


def host_prepare(inputs):
    f32 = np.float32
    x0 = np.ascontiguousarray(np.asarray(inputs["L0_attn_hidden_states"], f32))
    WqkvT = np.asarray(inputs["L0_attn_Wqkv"], f32).T
    WoT_p = _pack_w(np.ascontiguousarray(np.asarray(inputs["L0_attn_Wo"], f32).T))
    lr_wT = np.asarray(inputs["L0_attn_lr_w"], f32).T
    f1_8192 = _f1_perm(8192, 64)
    f3_8192 = _f3_perm(8192, 64)
    f1_4096 = _f1_perm(4096, 32)
    f3_4096 = _f3_perm(4096, 32)
    f3_2048 = _f3_perm(2048, 16)
    W1_p = _pack_w(np.asarray(inputs["L0_ffn_W1"], f32).T[:, f1_8192])
    W3_p = _pack_w(np.asarray(inputs["L0_ffn_W3"], f32).T[:, f1_8192])
    W2Tp = np.ascontiguousarray(np.asarray(inputs["L0_ffn_W2"], f32).T[f3_8192, :])
    wup_p = _pack_w(np.asarray(inputs["L1_ffn_w_up"], f32).T[f3_2048, :][:, f1_4096])
    wdnTp = np.ascontiguousarray(np.asarray(inputs["L1_ffn_w_down"], f32).T[f3_4096, :])
    wqkv1_p = _pack_w(np.asarray(inputs["L1_attn_w_qkv"], f32))
    wo1_p = _pack_w(np.asarray(inputs["L1_attn_w_o"], f32))

    ang0 = np.arange(S, dtype=np.float64)[:, None] / (
        500000.0 ** (np.arange(0, 128, 2, dtype=np.float64) / 128))[None, :]
    cos0f = np.ascontiguousarray(np.concatenate([np.cos(ang0).T] * 2, 0).astype(f32))
    sin0f = np.ascontiguousarray(np.concatenate([np.sin(ang0).T] * 2, 0).astype(f32))
    ang1 = np.arange(S, dtype=np.float64)[:, None] / (
        1000000.0 ** (np.arange(0, 128, 2, dtype=np.float64) / 128))[None, :]
    cos1f = np.concatenate([np.cos(ang1).T] * 2, 0).astype(f32)
    sin1f = np.concatenate([np.sin(ang1).T] * 2, 0).astype(f32)

    P = np.zeros((128, 128), f32)
    for i in range(64):
        P[i, i + 64] = -1.0
        P[i + 64, i] = 1.0
    PT = np.ascontiguousarray(P.T)
    ident = np.eye(128, dtype=f32)
    kk = np.arange(128)[:, None]
    cc = np.arange(1024)[None, :]
    maskM = (kk <= cc - 512).astype(f32)
    maskW = np.ascontiguousarray(1.0 - maskM)

    def chunkmajor(vec):
        return np.ascontiguousarray(np.asarray(vec, f32).reshape(4, 128).T)

    in_maps = []
    for core in range(8):
        g, t = core // 4, core % 4
        cols = slice(512 * t, 512 * t + 512)
        m = {}
        m["x0T"] = np.ascontiguousarray(x0[g].T)
        m["x0T_own"] = np.ascontiguousarray(x0[g].T[:, t * TOK:(t + 1) * TOK])
        m["wqkv_p"] = _pack_w(np.concatenate(
            [WqkvT[:, 512 * t:512 * t + 512],
             WqkvT[:, 2048 + 512 * t:2048 + 512 * t + 512],
             WqkvT[:, 4096 + 512 * t:4096 + 512 * t + 512]], 1))
        m["lrw_p"] = np.ascontiguousarray(
            lr_wT[:, [t, 4 + t, 8 + t]].reshape(16, 128, 3)
            .transpose(1, 0, 2).reshape(128, 48))
        m["lrb"] = np.ascontiguousarray(
            np.asarray(inputs["L0_attn_lr_b"], f32)[[t, 4 + t, 8 + t]].reshape(1, 3))
        m["pvecs"] = np.ascontiguousarray(np.concatenate([
            chunkmajor(np.asarray(inputs["L0_attn_q_norm_w"])[cols]),
            chunkmajor(np.asarray(inputs["L0_attn_k_norm_w"])[cols]),
            chunkmajor(np.asarray(inputs["L0_attn_qk_scale"])[cols, 0]),
            chunkmajor(np.asarray(inputs["L0_attn_qk_offset"])[cols, 0]),
            chunkmajor(np.asarray(inputs["L0_attn_qk_scale"])[cols, 1]),
            chunkmajor(np.asarray(inputs["L0_attn_qk_offset"])[cols, 1]),
            chunkmajor(np.asarray(inputs["L0_attn_ttt_norm_w"])),
        ], axis=1))
        m["w0T"] = np.ascontiguousarray(
            np.asarray(inputs["L0_attn_w0"][t], f32).T.reshape(4, 128, 512)
            .transpose(1, 0, 2).reshape(128, 2048))
        m["w2T"] = np.ascontiguousarray(
            np.asarray(inputs["L0_attn_w2"][t], f32).T.reshape(4, 128, 512)
            .transpose(1, 0, 2).reshape(128, 2048))
        m["w1N"] = np.ascontiguousarray(
            np.asarray(inputs["L0_attn_w1"][t], f32).reshape(4, 128, 512)
            .transpose(1, 0, 2).reshape(128, 2048))
        m["WoT_p"] = WoT_p
        m["W1_p"] = W1_p
        m["W3_p"] = W3_p
        m["W2Tp"] = W2Tp
        m["wqkv1_p"] = wqkv1_p
        m["wo1_p"] = wo1_p
        m["wup_p"] = wup_p
        m["wdnTp"] = wdnTp
        m["cos0"] = cos0f
        m["sin0"] = sin0f
        m["cos1"] = np.ascontiguousarray(cos1f[:, t * TOK:(t + 1) * TOK])
        m["sin1"] = np.ascontiguousarray(sin1f[:, t * TOK:(t + 1) * TOK])
        m["PT"] = PT
        m["ident"] = ident
        m["maskM"] = maskM
        m["maskW"] = maskW
        m["maskWc"] = np.ascontiguousarray(
            maskW if t > 0 else np.zeros((128, 1024), f32))
        m["H128"] = _hadamard(128).astype(f32)
        m["H64I2"] = np.kron(_hadamard(64), np.eye(2)).astype(f32)
        m["H32I4"] = np.kron(_hadamard(32), np.eye(4)).astype(f32)
        m["H16I8"] = np.kron(_hadamard(16), np.eye(8)).astype(f32)
        msel = np.zeros((128, 4), f32)
        msel[:, t] = 1.0
        m["msel"] = msel
        hsel = np.zeros((128, 4), f32)
        hsel[:, (t + 3) % 4] = 1.0
        m["hsel"] = hsel
        in_maps.append(m)
    return in_maps


class Ctx:
    BL = {"tensor": ("V", "S"), "vector": ("T", "S"), "scalar": ("T", "V")}
    TICK = {"tensor": "T", "vector": "V", "scalar": "S"}

    THIN = ("T", "V", "S")

    def __init__(self, nc, serial=True, want=None):
        self.nc = nc
        self.serial = serial
        self.prog = {e: [] for e in ENGINES}
        self.cnt = {}
        self.lastwait = {}
        self.S = {}
        self.want = want
        self.waited = {}
        self.attached = {}

    def addsem(self, es, name):
        self.S[name] = es.enter_context(self.nc.semaphore(name))
        self.cnt[name] = 0

    def wait(self, eng, sem, val):
        if val is None or val <= 0:
            return
        key = (eng, sem)
        if self.lastwait.get(key, 0) >= val:
            return
        self.lastwait[key] = val
        self.waited.setdefault(sem, set()).add(int(val))
        if self.want is not None and sem in self.THIN:
            att = self.attached.get(sem, [])
            rv = bisect.bisect_right(att, int(val))
            assert rv > 0 and att[rv - 1] == int(val), \
                f"wait on unattached tick {sem}>={val}"
            self.prog[eng].append([None, (sem, rv)])
        else:
            self.prog[eng].append([None, (sem, int(val))])

    def inc(self, eng, sem, by):
        self.cnt[sem] += by
        if (self.want is not None and sem in self.THIN
                and self.cnt[sem] not in self.want.get(sem, ())):
            return self.cnt[sem]
        self.attached.setdefault(sem, []).append(self.cnt[sem])
        slot = None
        for item in reversed(self.prog[eng]):
            if item[0] is not None:
                slot = item
                break
        assert slot is not None
        fn = slot[0]
        sh = self.S[sem]
        slot[0] = (lambda e, fn=fn, sh=sh, by=by: fn(e).then_inc(sh, by))
        return self.cnt[sem]

    def op(self, eng, fn):
        if self.serial and eng in self.BL:
            for s in self.BL[eng]:
                self.wait(eng, s, self.cnt[s])
        self.prog[eng].append([fn, None])
        if eng in self.TICK:
            return self.inc(eng, self.TICK[eng], 1)
        return None

    def opx(self, eng, fn, waits=()):
        for s, v in waits:
            self.wait(eng, s, v)
        self.prog[eng].append([fn, None])
        if eng in self.TICK:
            return self.inc(eng, self.TICK[eng], 1)
        return None

    def fence(self, engines=ENGINES):
        marks = dict(self.cnt)
        for e in engines:
            for s, v in marks.items():
                self.wait(e, s, v)

    def replay(self, block):
        def mk(eng):
            def body(e):
                for fn, w in self.prog[eng]:
                    if fn is not None:
                        fn(e)
                    else:
                        e.wait_ge(self.S[w[0]], w[1])
            return body
        for eng in ENGINES:
            getattr(block, eng)(mk(eng))


class Alloc:
    def __init__(self, arena, nbytes):
        self.arena = arena
        self.nbytes = nbytes
        self.off = 0
        self.peak = 0
        self.marks = []
        self.last_at = 0

    def view(self, at, elems, dt=F32):
        bs = 2 if dt == BF16 else 4
        assert at % bs == 0 and at + elems * bs <= self.nbytes
        if dt == F32:
            return self.arena[:, at // 4: at // 4 + elems]
        return self.arena[:, :].bitcast(dt)[:, at // bs: at // bs + elems]

    def alloc(self, elems, dt=F32):
        self.off = (self.off + 3) & ~3
        at = self.off
        self.off += elems * (2 if dt == BF16 else 4)
        self.peak = max(self.peak, self.off)
        assert self.off <= self.nbytes, f"arena overflow {self.off} > {self.nbytes}"
        self.last_at = at
        return self.view(at, elems, dt)

    def push(self):
        self.marks.append(self.off)

    def pop(self):
        self.off = self.marks.pop()


LAST_WAITED = {}


def build_nc(stop_after="out", taps=(), skip=(), want=None):
    nc = bass.Bass("TRN2", target_bir_lowering=False, debug=False)
    din = {}
    for name, shape in [
        ("x0T", [D, S]), ("x0T_own", [D, TOK]),
        ("wqkv_p", [128, 12 * 2048]), ("lrw_p", [128, 48]), ("lrb", [1, 3]),
        ("pvecs", [128, 28]),
        ("w0T", [128, 2048]), ("w2T", [128, 2048]), ("w1N", [128, 2048]),
        ("WoT_p", [128, 16 * 2048]),
        ("W1_p", [128, 64 * 2048]), ("W3_p", [128, 64 * 2048]), ("W2Tp", [8192, D]),
        ("wqkv1_p", [128, 24 * 2048]), ("wo1_p", [128, 16 * 2048]),
        ("wup_p", [128, 32 * 2048]), ("wdnTp", [4096, D]),
        ("cos0", [128, S]), ("sin0", [128, S]),
        ("cos1", [128, TOK]), ("sin1", [128, TOK]),
        ("PT", [128, 128]), ("ident", [128, 128]),
        ("maskM", [128, 1024]), ("maskW", [128, 1024]), ("maskWc", [128, 1024]),
        ("H128", [128, 128]), ("H64I2", [128, 128]), ("H32I4", [128, 128]),
        ("H16I8", [128, 128]), ("msel", [128, 4]), ("hsel", [128, 4]),
    ]:
        din[name] = nc.dram_tensor(name, shape, F32, kind="ExternalInput")

    y_out = nc.dram_tensor("y_out", [D, TOK], F32, kind="ExternalOutput")
    tap_t = {n: nc.dram_tensor("tap_" + n, list(sh), F32, kind="ExternalOutput")
             for n, sh in dict(taps).items()}

    wqkv_bfd = nc.dram_tensor("wqkv_bfd", [128, 12 * 2048], BF16)
    ag_mix_in = [nc.dram_tensor(f"ag_mix_in{c}", [512, CH], BF16) for c in range(NCH)]
    ag_mix_out = [nc.dram_tensor(f"ag_mix_out{c}", [D, CH], BF16) for c in range(NCH)]
    ar_in = [nc.dram_tensor(f"ar_in{c}", [1, 2048], F32) for c in range(NCH)]
    ar_out = [nc.dram_tensor(f"ar_out{c}", [1, 2048], F32) for c in range(NCH)]
    ag_kv_in = nc.dram_tensor("ag_kv_in", [1024, 512], BF16)
    ag_kv_out = nc.dram_tensor("ag_kv_out", [4096, 512], BF16)
    shufd = nc.dram_tensor("shufd", [128, 64 * 1024], BF16)
    RG = [[0, 1, 2, 3], [4, 5, 6, 7]]

    marks = {}
    es = ExitStack()
    C = Ctx(nc, serial=True, want=want)
    for s in ("T", "V", "S", "L", "G", "GA", "GB", "LS", "LE",
              "LX0", "LX1", "LW0", "LW1", "LM0", "LM1", "CCR", "CCA"):
        C.addsem(es, s)

    def sbt(name, shape, dt=F32):
        return es.enter_context(nc.sbuf_tensor(name, shape, dt))

    ARN = sbt("ARN", [128, ARN_ELEMS])
    AL = Alloc(ARN, ARN_ELEMS * 4)
    cpool = AL.alloc(len(CONST_VALS))
    for i, cv in enumerate(CONST_VALS):
        nc.const_aps.register(F32, float(cv), cpool[:, i:i + 1])
    pvecs = AL.alloc(28)
    ones_bf = AL.alloc(2, BF16)
    onesc_f = AL.alloc(1)
    onesr_f = AL.alloc(128)
    PT_bf = AL.alloc(128, BF16)
    id_bf = AL.alloc(128, BF16)
    H128_bf = AL.alloc(128, BF16)
    H64I2_bf = AL.alloc(128, BF16)
    H32I4_bf = AL.alloc(128, BF16)
    H16I8_bf = AL.alloc(128, BF16)
    maskM_bf = AL.alloc(1024, BF16)
    maskW_bf = AL.alloc(1024, BF16)
    maskWc_bf = AL.alloc(1024, BF16)
    rows = AL.alloc(2048)
    rows2 = AL.alloc(2048)
    lrb_bc = AL.alloc(3)
    lrw_bf = AL.alloc(48, BF16)
    lr_sb = AL.alloc(24)
    srow = AL.alloc(8)
    msel = AL.alloc(4)
    hsel = AL.alloc(4)
    mxacc = AL.alloc(8)
    pp = [es.enter_context(nc.psum_tensor(f"pb{i}", [128, 512], F32)) for i in range(8)]
    ppb = [pp[i][:, :].bitcast(BF16) for i in range(8)]

    def dma(qeng, out, in_, sem=None):
        if sem is None:
            sem = "G" if qeng == "gpsimd" else "L"
        C.op(qeng, lambda e, o=out, i=in_: e.dma_start(out=o, in_=i))
        return C.inc(qeng, sem, 16)

    def mm(out, lhsT, rhs, start=True, stop=True):
        return C.op("tensor", lambda e, o=out, l=lhsT, r=rhs, a=start, b=stop:
                    e.matmul(o, l, r, start=a, stop=b))

    def vop(fn):
        return C.op("vector", fn)

    def sop(fn):
        return C.op("scalar", fn)

    def mmx(out, lhsT, rhs, start=True, stop=True, waits=()):
        return C.opx("tensor", lambda e, o=out, l=lhsT, r=rhs, a=start, b=stop:
                     e.matmul(o, l, r, start=a, stop=b), waits)

    def vopx(fn, waits=()):
        return C.opx("vector", fn, waits)

    def sopx(fn, waits=()):
        return C.opx("scalar", fn, waits)

    def colsum(dst_ps, src_bf, start=True, stop=True):
        return mm(dst_ps, ones_bf[:, 0:1], src_bf, start, stop)

    def bcast(dst_ps, row_ap):
        return mm(dst_ps, onesr_f[:1, :], row_ap, True, True)

    def transpose(dst_ps, src_bf):
        return C.op("tensor", lambda e, o=dst_ps, i=src_bf:
                    e.transpose(o, i, id_bf[:, :]))

    def collective(kind, op_, in_t, out_t, sem="CCA"):
        C.op("gpsimd", lambda e, k=kind, o=op_, i=in_t, u=out_t: e.collective_compute(
            k, o, replica_groups=RG, ins=[i.ap().opt()], outs=[u.ap().opt()]))
        return C.inc("gpsimd", sem, 1)

    AL.push()
    ist = [AL.alloc(2048), AL.alloc(2048)]
    ist_bf = AL.alloc(2048, BF16)
    cast_v = {0: 0, 1: 0}
    st_g = {0: 0}

    def phase_init():
        for i, cv in enumerate(CONST_VALS):
            vop(lambda e, i=i, cv=float(cv): e.memset(cpool[:, i:i + 1], cv))
        vop(lambda e: e.memset(ones_bf[:, 0:1], 1.0))
        vop(lambda e: e.memset(onesc_f[:, :], 1.0))
        vop(lambda e: e.memset(onesr_f[:, :], 1.0))
        dma("sync", pvecs[:, :], din["pvecs"][:, :])
        dma("sync", msel[:, :], din["msel"][:, :])
        dma("sync", hsel[:, :], din["hsel"][:, :])
        lv = dma("sync", ist[0][:, 0:48], din["lrw_p"][:, :], "LX0")
        C.wait("vector", "LX0", lv)
        vv = vop(lambda e: e.tensor_copy(lrw_bf[:, :], ist[0][:, 0:48]))
        C.wait("sync", "V", vv)
        lv = dma("sync", rows[:1, 0:3], din["lrb"][:, :])
        C.wait("vector", "L", lv)
        vop(lambda e: e.tensor_scalar(rows[:1, 0:3], rows[:1, 0:3],
                                      BASE_LR_INV, None, ALU.add))
        bcast(pp[0][:, 0:3], rows[:1, 0:3])
        vop(lambda e: e.tensor_copy(lrb_bc[:, :], pp[0][:, 0:3]))
        for nm, dst, n in [("PT", PT_bf, 128), ("ident", id_bf, 128),
                           ("H128", H128_bf, 128), ("H64I2", H64I2_bf, 128),
                           ("H32I4", H32I4_bf, 128), ("H16I8", H16I8_bf, 128),
                           ("maskM", maskM_bf, 1024), ("maskW", maskW_bf, 1024),
                           ("maskWc", maskWc_bf, 1024)]:
            lv = dma("sync", ist[0][:, 0:n], din[nm][:, :], "LX0")
            C.wait("vector", "LX0", lv)
            vv = vop(lambda e, d=dst, s=ist[0][:, 0:n]: e.tensor_copy(d[:, :], s))
            C.wait("sync", "V", vv)
        for m_ in range(12):
            buf = ist[m_ % 2]
            if m_ >= 2:
                C.wait("sync", "V", cast_v[m_ % 2])
            lv = dma("sync", buf, din["wqkv_p"][:, m_ * 2048:(m_ + 1) * 2048],
                     "LX%d" % (m_ % 2))
            C.wait("vector", "LX%d" % (m_ % 2), lv)
            C.wait("vector", "G", 0 if m_ == 0 else st_g[0])
            vv = vop(lambda e, b=buf: e.tensor_copy(ist_bf[:, :], b))
            cast_v[m_ % 2] = vv
            C.wait("gpsimd", "V", vv)
            gv = dma("gpsimd", wqkv_bfd[:, m_ * 2048:(m_ + 1) * 2048], ist_bf[:, :])
            st_g[0] = gv
            C.wait("vector", "G", gv)
        C.fence()

    phase_init()

    def layer_norm(src_bf, sq_bf, mbc, rbc, scrA, dst_bf=None, out_fn=None):
        for fc in range(16):
            sop(lambda e, fc=fc: e.activation(
                sq_bf[:, 0:1024], src_bf[:, fc * 1024:(fc + 1) * 1024], AF.Square))
            for half in range(2):
                h0 = half * 512
                colsum(pp[6 + half][:1, :],
                       src_bf[:, fc * 1024 + h0:fc * 1024 + h0 + 512],
                       fc == 0, fc == 15)
                colsum(pp[4 + half][:1, :], sq_bf[:, h0:h0 + 512], fc == 0, fc == 15)
        vop(lambda e: e.tensor_copy(rows[:1, 0:512], pp[6][:1, :]))
        vop(lambda e: e.tensor_copy(rows[:1, 512:1024], pp[7][:1, :]))
        vop(lambda e: e.tensor_copy(rows[:1, 1024:1536], pp[4][:1, :]))
        vop(lambda e: e.tensor_copy(rows[:1, 1536:2048], pp[5][:1, :]))
        vop(lambda e: e.tensor_scalar(rows[:1, 0:2048], rows[:1, 0:2048],
                                      1.0 / D, None, ALU.mult))
        vop(lambda e: e.tensor_mul(rows2[:1, 0:1024], rows[:1, 0:1024],
                                   rows[:1, 0:1024]))
        vop(lambda e: e.tensor_sub(rows[:1, 1024:2048], rows[:1, 1024:2048],
                                   rows2[:1, 0:1024]))
        sop(lambda e: e.activation(rows[:1, 1024:2048], rows[:1, 1024:2048],
                                   AF.Ln, bias=1e-5, scale=1.0))
        sop(lambda e: e.activation(rows[:1, 1024:2048], rows[:1, 1024:2048],
                                   AF.Exp, bias=0.0, scale=-0.5))
        for half in range(2):
            h0 = half * 512
            bcast(pp[0][:, :], rows[:1, h0:h0 + 512])
            bcast(pp[1][:, :], rows[:1, 1024 + h0:1024 + h0 + 512])
            vop(lambda e, h0=h0: e.tensor_copy(mbc[:, h0:h0 + 512], pp[0][:, :]))
            vop(lambda e, h0=h0: e.tensor_copy(rbc[:, h0:h0 + 512], pp[1][:, :]))
        if dst_bf is None:
            dst_bf = src_bf
        for fc in range(16):
            vop(lambda e, fc=fc: e.tensor_sub(
                scrA[:, 0:1024], src_bf[:, fc * 1024:(fc + 1) * 1024], mbc[:, :]))
            if out_fn is not None:
                out_fn(fc, scrA, rbc)
            else:
                vop(lambda e, fc=fc: e.tensor_mul(
                    dst_bf[:, fc * 1024:(fc + 1) * 1024], scrA[:, 0:1024], rbc[:, :]))

    def dump(name, src_ap, rows_n, cols_n):
        if name not in tap_t:
            return
        C.fence()
        stg = AL.view(AL.nbytes - 8192, 2048, F32)
        for r in range(rows_n // 128):
            vv = vop(lambda e, r=r: e.tensor_copy(
                stg[:, 0:cols_n], src_ap[:, r * cols_n:(r + 1) * cols_n]))
            C.wait("gpsimd", "V", vv)
            gv = dma("gpsimd", tap_t[name][r * 128:(r + 1) * 128, :], stg[:, 0:cols_n])
            C.wait("vector", "G", gv)
        C.fence()

    AL.pop()
    AL.push()
    x0bf = AL.alloc(16 * 512, BF16)
    at_x0bf = AL.last_at
    wslab = [AL.alloc(2048, BF16), AL.alloc(2048, BF16)]
    x0f = [AL.alloc(512), AL.alloc(512)]
    q_f = AL.alloc(4096, BF16)
    k_f = AL.alloc(4096, BF16)
    fvf = AL.alloc(4096, BF16)
    fvl1 = AL.alloc(4096, BF16)
    fkt = AL.alloc(4096, BF16)
    akbuf = [AL.alloc(4096, BF16), AL.alloc(4096, BF16)]
    vtbuf = [AL.alloc(4096, BF16), AL.alloc(4096, BF16)]
    attnT = AL.alloc(4096, BF16)
    ctab = AL.alloc(1024)
    at_ctab = AL.last_at
    stab = AL.alloc(1024)
    rqbc = AL.alloc(1024)
    at_rqbc = AL.last_at
    rkbc = AL.alloc(1024)
    aqh = AL.alloc(1024, BF16)
    at_aqh = AL.last_at
    expt = [AL.alloc(512, BF16), AL.alloc(512, BF16)]
    st_w0 = AL.alloc(2048)
    st_w2 = AL.alloc(2048)
    st_w1 = AL.alloc(2048)
    st_bf = AL.alloc(3 * 2048, BF16)
    scr = AL.alloc(2048)
    scrB = AL.alloc(1024)
    scr_bf = AL.alloc(2048, BF16)
    sbw0, sbw2, sbw1 = st_bf[:, 0:2048], st_bf[:, 2048:4096], st_bf[:, 4096:6144]
    hid = AL.view(at_x0bf, 4096, BF16)
    dgate0 = AL.view(at_x0bf + 8192, 4096, BF16)
    dhpre2 = AL.view(at_x0bf + 16384, 4096, BF16)
    hq = AL.view(at_ctab, 4096, BF16)
    oisb = AL.view(at_rqbc, 4096, BF16)
    w1NT = AL.view(at_aqh, 2048, BF16)
    L0_PEAK = AL.off

    lv = dma("sync", st_w0, din["w0T"][:, :], "LS")
    lv = dma("sync", st_w2, din["w2T"][:, :], "LS")
    lv = dma("sync", st_w1, din["w1N"][:, :], "LS")
    C.wait("vector", "LS", lv)

    slab_t = {0: 0, 1: 0}
    cast_t = {0: 0, 1: 0}
    lv_tab = {0: 0}

    def l0_qkv(c):
        C.wait("sync", "T", C.cnt["T"])
        C.wait("sync", "V", C.cnt["V"])
        dma("sync", ctab, din["cos0"][:, c * CH:c * CH + 1024], "LE")
        lv_tab[0] = dma("sync", stab, din["sin0"][:, c * CH:c * CH + 1024], "LE")
        for half in range(2):
            tok0 = c * CH + half * 512
            for k in range(16):
                buf = x0f[k % 2]
                if k >= 2:
                    C.wait("sync", "V", cast_t[k % 2])
                lv = dma("sync", buf, din["x0T"][k * 128:(k + 1) * 128,
                                                 tok0:tok0 + 512],
                         "LX%d" % (k % 2))
                C.wait("vector", "LX%d" % (k % 2), lv)
                cast_t[k % 2] = vop(lambda e, k=k, b=buf: e.tensor_copy(
                    x0bf[:, k * 512:(k + 1) * 512], b))
            for ttl in range(4):
                tt = half * 4 + ttl
                for k in range(16):
                    mm(pp[5][:, 0:3],
                       x0bf[:, k * 512 + ttl * 128:k * 512 + ttl * 128 + 128],
                       lrw_bf[:, k * 3:(k + 1) * 3], k == 0, k == 15)
                vop(lambda e: e.tensor_add(scr[:, 0:3], pp[5][:, 0:3], lrb_bc[:, :]))
                sop(lambda e: e.activation(scr[:, 4:7], scr[:, 0:3], AF.Exp))
                sop(lambda e, tt=tt: e.activation(
                    lr_sb[:, tt * 3:tt * 3 + 3], scr[:, 4:7], AF.Ln,
                    bias=1.0, scale=1.0))
            for m_ in range(12):
                wb = wslab[m_ % 2]
                if m_ >= 2 or half > 0 or c > 0:
                    C.wait("sync", "T", slab_t[m_ % 2])
                lv = dma("sync", wb, wqkv_bfd[:, m_ * 2048:(m_ + 1) * 2048],
                         "LW%d" % (m_ % 2))
                C.wait("tensor", "LW%d" % (m_ % 2), lv)
                bk = m_ % 2
                for k in range(16):
                    tv = mm(pp[bk][:, :], wb[:, k * 128:(k + 1) * 128],
                            x0bf[:, k * 512:(k + 1) * 512], k == 0, k == 15)
                slab_t[m_ % 2] = tv
                if m_ < 8:
                    fc = m_ % 4
                    dst = q_f if m_ < 4 else k_f
                    vop(lambda e, d=dst, fc=fc, half=half, bk=bk: e.tensor_copy(
                        d[:, fc * 1024 + half * 512:fc * 1024 + half * 512 + 512],
                        pp[bk][:, :]))
                    sop(lambda e, bk=bk: e.activation(
                        scr_bf[:, 0:512], pp[bk][:, :], AF.Square))
                    colsum(pp[6 if m_ < 4 else 7][:1, :], scr_bf[:, 0:512],
                           fc == 0, fc == 3)
                else:
                    oc = m_ - 8
                    sop(lambda e, bk=bk: e.activation(
                        scr[:, 0:512], pp[bk][:, :], AF.Sigmoid))
                    vop(lambda e, oc=oc, half=half, bk=bk: e.tensor_mul(
                        fvf[:, oc * 1024 + half * 512:oc * 1024 + half * 512 + 512],
                        scr[:, 0:512], pp[bk][:, :]))
                    vop(lambda e, bk=bk: e.tensor_copy(
                        scr_bf[:, 1024:1536], pp[bk][:, :]))
                    for sub in range(4):
                        tb = 3 if sub % 2 == 0 else 4
                        transpose(ppb[tb][:, 0:128],
                                  scr_bf[:, 1024 + sub * 128:1024 + sub * 128 + 128])
                        tt = half * 4 + sub
                        vop(lambda e, tb=tb, tt=tt, oc=oc, c=c: e.tensor_copy(
                            vtbuf[c % 2][:, tt * 512 + oc * 128:
                                         tt * 512 + oc * 128 + 128],
                            ppb[tb][:, 0:128]))
            vop(lambda e, half=half: e.tensor_copy(
                rows[:1, half * 512:half * 512 + 512], pp[6][:1, :]))
            vop(lambda e, half=half: e.tensor_copy(
                rows[:1, 1024 + half * 512:1024 + half * 512 + 512], pp[7][:1, :]))

    def l0_stats_ar(c):
        C.fence(("gpsimd",))
        gv = dma("gpsimd", ar_in[c][:, :], rows[:1, 0:2048])
        C.wait("gpsimd", "G", gv)
        cv = collective("AllReduce", ALU.add, ar_in[c], ar_out[c], "CCR")
        C.wait("gpsimd", "CCR", cv)
        gv = dma("gpsimd", rows[:1, 0:2048], ar_out[c][:, :])
        for e_ in ("scalar", "vector", "tensor"):
            C.wait(e_, "G", gv)
        sop(lambda e: e.activation(rows2[:1, 0:2048], rows[:1, 0:2048],
                                   AF.Ln, bias=1e-6, scale=1.0 / D))
        sop(lambda e: e.activation(rows2[:1, 0:2048], rows2[:1, 0:2048],
                                   AF.Exp, bias=0.0, scale=-0.5))
        for half in range(2):
            h0 = half * 512
            bcast(pp[0][:, :], rows2[:1, h0:h0 + 512])
            bcast(pp[1][:, :], rows2[:1, 1024 + h0:1024 + h0 + 512])
            vop(lambda e, h0=h0: e.tensor_copy(rqbc[:, h0:h0 + 512], pp[0][:, :]))
            vop(lambda e, h0=h0: e.tensor_copy(rkbc[:, h0:h0 + 512], pp[1][:, :]))

    def l0_rope_ak(c):
        C.wait("vector", "LE", lv_tab[0])
        for fc in range(4):
            vop(lambda e, fc=fc: e.tensor_scalar(
                scr_bf[:, 0:1024], k_f[:, fc * 1024:(fc + 1) * 1024],
                pvecs[:, 4 + fc:4 + fc + 1], None, ALU.mult))
            vop(lambda e: e.tensor_mul(scr_bf[:, 0:1024], scr_bf[:, 0:1024],
                                       rkbc[:, 0:1024]))
            for half in range(2):
                mm(pp[half][:, :], PT_bf[:, :],
                   scr_bf[:, half * 512:half * 512 + 512])
            vop(lambda e: e.tensor_mul(scr[:, 0:1024], scr_bf[:, 0:1024], ctab[:, :]))
            vop(lambda e: e.tensor_mul(scrB[:, 0:512], pp[0][:, :], stab[:, 0:512]))
            vop(lambda e: e.tensor_mul(scrB[:, 512:1024], pp[1][:, :],
                                       stab[:, 512:1024]))
            vop(lambda e, fc=fc, c=c: e.tensor_add(
                akbuf[c % 2][:, fc * 1024:(fc + 1) * 1024], scr[:, 0:1024],
                scrB[:, 0:1024]))

    def l0_attention(c):
        for h in range(4):
            vop(lambda e, h=h: e.tensor_scalar(
                scr_bf[:, 0:1024], q_f[:, h * 1024:(h + 1) * 1024],
                pvecs[:, 0 + h:0 + h + 1], None, ALU.mult))
            vop(lambda e: e.tensor_mul(scr_bf[:, 0:1024], scr_bf[:, 0:1024],
                                       rqbc[:, 0:1024]))
            for half in range(2):
                mm(pp[half][:, :], PT_bf[:, :],
                   scr_bf[:, half * 512:half * 512 + 512])
            vop(lambda e: e.tensor_mul(scr[:, 0:1024], scr_bf[:, 0:1024], ctab[:, :]))
            vop(lambda e: e.tensor_mul(scrB[:, 0:512], pp[0][:, :], stab[:, 0:512]))
            vop(lambda e: e.tensor_mul(scrB[:, 512:1024], pp[1][:, :],
                                       stab[:, 512:1024]))
            vop(lambda e: e.tensor_add(aqh[:, :], scr[:, 0:1024], scrB[:, 0:1024]))
            for qt in range(2):
                q0g = c * CH + qt * 512
                subs = []
                for k0 in range(q0g - 1024, q0g + 512, 128):
                    if k0 < 0:
                        continue
                    delta = k0 - q0g
                    if delta <= -640:
                        subs.append((k0, maskW_bf, (delta + 1024) // 128))
                    elif delta < 0:
                        subs.append((k0, None, 0))
                    else:
                        subs.append((k0, maskM_bf, delta // 128))
                nsub = len(subs)
                for i, (k0, mk, j) in enumerate(subs):
                    kc = (k0 // CH) % 2
                    ko = k0 % CH
                    bk = i % 2
                    mm(pp[bk][:, :],
                       akbuf[kc][:, h * 1024 + ko:h * 1024 + ko + 128],
                       aqh[:, qt * 512:qt * 512 + 512])
                    eb = expt[i % 2]
                    sop(lambda e, eb=eb, bk=bk: e.activation(
                        eb[:, :], pp[bk][:, :], AF.Exp, bias=-C_L0, scale=SC0))
                    if mk is not None:
                        vop(lambda e, eb=eb, mk=mk, j=j: e.tensor_mul(
                            eb[:, :], eb[:, :],
                            mk[:, 512 - 128 * j:1024 - 128 * j]))
                    colsum(pp[3][:1, :], eb[:, :], i == 0, i == nsub - 1)
                    mm(pp[2][:, :],
                       vtbuf[kc][:, (ko // 128) * 512 + h * 128:
                                  (ko // 128) * 512 + h * 128 + 128],
                       eb[:, :], i == 0, i == nsub - 1)
                vop(lambda e: e.reciprocal(rows[:1, 0:512], pp[3][:1, :]))
                vop(lambda e: e.tensor_copy(scr[:, 0:512], pp[2][:, :]))
                bcast(pp[4][:, :], rows[:1, 0:512])
                vop(lambda e, h=h, qt=qt: e.tensor_mul(
                    attnT[:, h * 1024 + qt * 512:h * 1024 + qt * 512 + 512],
                    scr[:, 0:512], pp[4][:, :]))

    def l0_derived(c):
        for which, src in [(0, q_f), (1, k_f)]:
            sc_col, of_col = (8, 12) if which == 0 else (16, 20)
            for fc in range(4):
                sop(lambda e, s=src, fc=fc: e.activation(
                    scr[:, 0:1024], s[:, fc * 1024:(fc + 1) * 1024], AF.Sigmoid))
                vop(lambda e, s=src, fc=fc: e.tensor_mul(
                    scr[:, 1024:2048], scr[:, 0:1024],
                    s[:, fc * 1024:(fc + 1) * 1024]))
                vop(lambda e, s=src, fc=fc, sc=sc_col, of=of_col: e.tensor_scalar(
                    s[:, fc * 1024:(fc + 1) * 1024], scr[:, 1024:2048],
                    pvecs[:, sc + fc:sc + fc + 1], pvecs[:, of + fc:of + fc + 1],
                    ALU.mult, ALU.add))
                sop(lambda e, s=src, fc=fc: e.activation(
                    scr_bf[:, 0:1024], s[:, fc * 1024:(fc + 1) * 1024], AF.Square))
                b0 = 2 + which
                colsum(pp[b0][:1, :], scr_bf[:, 0:512], fc == 0, fc == 3)
                colsum(pp[b0 + 2][:1, :], scr_bf[:, 512:1024], fc == 0, fc == 3)
        vop(lambda e: e.tensor_copy(rows[:1, 0:512], pp[2][:1, :]))
        vop(lambda e: e.tensor_copy(rows[:1, 512:1024], pp[4][:1, :]))
        vop(lambda e: e.tensor_copy(rows[:1, 1024:1536], pp[3][:1, :]))
        vop(lambda e: e.tensor_copy(rows[:1, 1536:2048], pp[5][:1, :]))
        sop(lambda e: e.activation(rows[:1, 0:2048], rows[:1, 0:2048],
                                   AF.Ln, bias=1e-24, scale=1.0))
        sop(lambda e: e.activation(rows[:1, 0:2048], rows[:1, 0:2048],
                                   AF.Exp, bias=0.0, scale=0.5))
        sop(lambda e: e.add(rows[:1, 0:2048], rows[:1, 0:2048], 1e-12))
        vop(lambda e: e.reciprocal(rows2[:1, 0:2048], rows[:1, 0:2048]))
        for which, src in [(0, q_f), (1, k_f)]:
            for half in range(2):
                bcast(pp[half][:, :],
                      rows2[:1, which * 1024 + half * 512:
                            which * 1024 + half * 512 + 512])
            for fc in range(4):
                for half in range(2):
                    o = fc * 1024 + half * 512
                    vop(lambda e, s=src, o=o, half=half: e.tensor_mul(
                        s[:, o:o + 512], s[:, o:o + 512], pp[half][:, :]))
        for dc in range(4):
            for tt in range(8):
                tb = 3 if tt % 2 == 0 else 4
                transpose(ppb[tb][:, 0:128],
                          k_f[:, dc * 1024 + tt * 128:dc * 1024 + tt * 128 + 128])
                vop(lambda e, tb=tb, tt=tt, dc=dc: e.tensor_copy(
                    fkt[:, tt * 512 + dc * 128:tt * 512 + dc * 128 + 128],
                    ppb[tb][:, 0:128]))
        for tt in range(8):
            sop(lambda e, tt=tt, c=c: e.activation(
                scr[:, 0:512], vtbuf[c % 2][:, tt * 512:(tt + 1) * 512], AF.Sigmoid))
            vop(lambda e, tt=tt, c=c: e.scalar_tensor_tensor(
                scrB[:, 0:512], scr[:, 0:512], lr_sb[:, tt * 3 + 1:tt * 3 + 2],
                vtbuf[c % 2][:, tt * 512:(tt + 1) * 512], ALU.mult, ALU.mult))
            vop(lambda e, tt=tt: e.tensor_copy(fvl1[:, tt * 512:(tt + 1) * 512],
                                               scrB[:, 0:512]))

    def l0_ttt(c):
        for dst, src in [(sbw0, st_w0), (sbw2, st_w2), (sbw1, st_w1)]:
            vop(lambda e, d=dst, s=src: e.tensor_copy(d, s))
        for tt in range(8):
            for bk, lhs_buf, rhs in [(0, k_f, sbw0), (1, k_f, sbw2), (2, fvf, sbw1)]:
                for dc in range(4):
                    mm(pp[bk][:, :],
                       lhs_buf[:, dc * 1024 + tt * 128:dc * 1024 + tt * 128 + 128],
                       rhs[:, dc * 512:(dc + 1) * 512], dc == 0, dc == 3)
            sop(lambda e: e.activation(scr[:, 0:512], pp[0][:, :], AF.Sigmoid))
            vop(lambda e: e.tensor_mul(scr[:, 512:1024], scr[:, 0:512], pp[0][:, :]))
            vop(lambda e: e.tensor_sub(scr[:, 1024:1536], pp[0][:, :],
                                       scr[:, 512:1024]))
            vop(lambda e: e.tensor_mul(scr[:, 1536:2048], scr[:, 0:512],
                                       scr[:, 1024:1536]))
            vop(lambda e: e.tensor_add(scr[:, 1536:2048], scr[:, 1536:2048],
                                       scr[:, 0:512]))
            vop(lambda e: e.tensor_copy(scrB[:, 0:512], pp[1][:, :]))
            vop(lambda e: e.tensor_mul(scrB[:, 512:1024], pp[2][:, :], scrB[:, 0:512]))
            vop(lambda e, tt=tt: e.tensor_mul(hid[:, tt * 512:(tt + 1) * 512],
                                              scr[:, 512:1024], scrB[:, 0:512]))
            vop(lambda e, tt=tt: e.scalar_tensor_tensor(
                dgate0[:, tt * 512:(tt + 1) * 512], scrB[:, 512:1024],
                lr_sb[:, tt * 3:tt * 3 + 1], scr[:, 1536:2048], ALU.mult, ALU.mult))
            vop(lambda e, tt=tt: e.scalar_tensor_tensor(
                dhpre2[:, tt * 512:(tt + 1) * 512], pp[2][:, :],
                lr_sb[:, tt * 3 + 2:tt * 3 + 3], scr[:, 512:1024],
                ALU.mult, ALU.mult))
        for st_f, lhs_buf, rhs_buf in [(st_w0, fkt, dgate0), (st_w2, fkt, dhpre2),
                                       (st_w1, fvl1, hid)]:
            for mc in range(4):
                bk = mc % 2
                for tt in range(8):
                    mm(pp[bk][:, :],
                       lhs_buf[:, tt * 512 + mc * 128:tt * 512 + mc * 128 + 128],
                       rhs_buf[:, tt * 512:(tt + 1) * 512], tt == 0, tt == 7)
                vop(lambda e, s=st_f, mc=mc, bk=bk: e.tensor_add(
                    s[:, mc * 512:(mc + 1) * 512], s[:, mc * 512:(mc + 1) * 512],
                    pp[bk][:, :]))
        for dst, src in [(sbw0, st_w0), (sbw2, st_w2), (sbw1, st_w1)]:
            vop(lambda e, d=dst, s=src: e.tensor_copy(d, s))
        for ocn in range(4):
            for hcn in range(4):
                tb = 3 if (ocn * 4 + hcn) % 2 == 0 else 4
                transpose(ppb[tb][:, 0:128],
                          sbw1[:, ocn * 512 + hcn * 128:ocn * 512 + hcn * 128 + 128])
                vop(lambda e, tb=tb, ocn=ocn, hcn=hcn: e.tensor_copy(
                    w1NT[:, hcn * 512 + ocn * 128:hcn * 512 + ocn * 128 + 128],
                    ppb[tb][:, 0:128]))
        for mh in range(4):
            for nh in range(2):
                for dc in range(4):
                    mm(pp[0][:, :],
                       sbw0[:, dc * 512 + mh * 128:dc * 512 + mh * 128 + 128],
                       q_f[:, dc * 1024 + nh * 512:dc * 1024 + nh * 512 + 512],
                       dc == 0, dc == 3)
                    mm(pp[1][:, :],
                       sbw2[:, dc * 512 + mh * 128:dc * 512 + mh * 128 + 128],
                       q_f[:, dc * 1024 + nh * 512:dc * 1024 + nh * 512 + 512],
                       dc == 0, dc == 3)
                sop(lambda e: e.activation(scr[:, 0:512], pp[0][:, :], AF.Sigmoid))
                vop(lambda e: e.tensor_mul(scr[:, 512:1024], scr[:, 0:512],
                                           pp[0][:, :]))
                vop(lambda e, mh=mh, nh=nh: e.tensor_mul(
                    hq[:, mh * 1024 + nh * 512:mh * 1024 + nh * 512 + 512],
                    scr[:, 512:1024], pp[1][:, :]))
        for nh in range(2):
            for oc in range(4):
                bk = oc % 2
                for hc in range(4):
                    mm(pp[bk][:, :],
                       w1NT[:, hc * 512 + oc * 128:hc * 512 + oc * 128 + 128],
                       hq[:, hc * 1024 + nh * 512:hc * 1024 + nh * 512 + 512],
                       hc == 0, hc == 3)
                sop(lambda e, bk=bk: e.activation(scr_bf[:, 0:512], pp[bk][:, :],
                                                  AF.Square))
                vop(lambda e, oc=oc, nh=nh, bk=bk: e.tensor_copy(
                    oisb[:, oc * 1024 + nh * 512:oc * 1024 + nh * 512 + 512],
                    pp[bk][:, :]))
                colsum(pp[5][:1, :], scr_bf[:, 0:512], oc == 0, oc == 3)
            sop(lambda e: e.activation(rows[:1, 0:512], pp[5][:1, :],
                                       AF.Ln, bias=1e-6, scale=1.0 / 512.0))
            sop(lambda e: e.activation(rows[:1, 0:512], rows[:1, 0:512],
                                       AF.Exp, bias=0.0, scale=-0.5))
            bcast(pp[6][:, :], rows[:1, 0:512])
            for oc in range(4):
                vop(lambda e, oc=oc, nh=nh: e.scalar_tensor_tensor(
                    scr[:, 0:512],
                    oisb[:, oc * 1024 + nh * 512:oc * 1024 + nh * 512 + 512],
                    pvecs[:, 24 + oc:24 + oc + 1], pp[6][:, :], ALU.mult, ALU.mult))
                vv = vop(lambda e, oc=oc, nh=nh: e.tensor_add(
                    scr_bf[:, 1024:1536], scr[:, 0:512],
                    attnT[:, oc * 1024 + nh * 512:oc * 1024 + nh * 512 + 512]))
                C.wait("gpsimd", "V", vv)
                gv = dma("gpsimd", ag_mix_in[c][oc * 128:(oc + 1) * 128,
                                                nh * 512:nh * 512 + 512],
                         scr_bf[:, 1024:1536])
                C.wait("vector", "G", gv)
        C.wait("gpsimd", "G", C.cnt["G"])
        return collective("AllGather", ALU.bypass, ag_mix_in[c], ag_mix_out[c])

    cv_ags = [0, 0, 0, 0]
    for c in range(NCH):
        l0_qkv(c)
        if stop_after == f"l0q{c}":
            break
        if "stats" not in skip:
            l0_stats_ar(c)
        if "rope" not in skip:
            l0_rope_ak(c)
        if "attn" not in skip:
            l0_attention(c)
        if "derived" not in skip:
            l0_derived(c)
        if "ttt" not in skip:
            cv_ags[c] = l0_ttt(c)
        if stop_after == f"l0c{c}":
            break

    def finish():
        C.fence()
        gv = dma("gpsimd", y_out[0:128, :], ARN[:, 0:1024])
        C.wait("gpsimd", "G", gv)
        C.replay(es.enter_context(nc.Block()))
        es.close()
        return nc

    if stop_after.startswith("l0"):
        dump("x0bf", x0bf, 2048, 512)
        dump("qf", q_f, 512, 1024)
        dump("kf", k_f, 512, 1024)
        dump("fvf", fvf, 512, 1024)
        dump("attnT", attnT, 512, 1024)
        dump("ak", akbuf[(NCH - 1) % 2] if stop_after == "l0c3" else akbuf[0],
             512, 1024)
        dump("lr", lr_sb, 128, 24)
        dump("oisb", oisb, 512, 1024)
        dump("stw0", st_w0, 128, 2048)
        return finish()

    AL.pop()
    AL.push()
    slotA = AL.alloc(16 * 1024, BF16)
    slotB = AL.alloc(16 * 1024, BF16)
    slotB_at = AL.last_at
    AL.push()
    mixT = AL.alloc(16 * 1024, BF16)
    mstage = [AL.alloc(1024, BF16), AL.alloc(1024, BF16)]
    wstage = [AL.alloc(2048), AL.alloc(2048)]
    w_bf2 = [AL.alloc(2048, BF16), AL.alloc(2048, BF16)]
    x0own = [AL.alloc(1024), AL.alloc(1024)]
    sq_bf = AL.alloc(1024, BF16)
    mbc = AL.alloc(1024)
    rbc = AL.alloc(1024)
    scrA = AL.alloc(1024)
    W_PEAK = AL.off

    def phase_w():
        C.fence(("sync",))
        msel_t = {0: 0, 1: 0}
        for r in range(4):
            C.wait("sync", "CCA", cv_ags[r])
            for k in range(16):
                sb2 = mstage[k % 2]
                if r > 0 or k >= 2:
                    C.wait("sync", "V", msel_t[k % 2])
                lv = dma("sync", sb2, ag_mix_out[r][k * 128:(k + 1) * 128, :],
                         "LM%d" % (k % 2))
                if r == 0:
                    msel_t[k % 2] = vopx(lambda e, k=k, s=sb2: e.tensor_scalar(
                        mixT[:, k * 1024:(k + 1) * 1024], s, msel[:, 0:1],
                        None, ALU.mult), [("LM%d" % (k % 2), lv)])
                else:
                    msel_t[k % 2] = vopx(lambda e, k=k, r=r, s=sb2:
                                         e.scalar_tensor_tensor(
                        mixT[:, k * 1024:(k + 1) * 1024], s, msel[:, r:r + 1],
                        mixT[:, k * 1024:(k + 1) * 1024], ALU.mult, ALU.add),
                        [("LM%d" % (k % 2), lv)])
        C.wait("sync", "T", C.cnt["T"])
        cast_w, mmg, addm = {}, {}, {}
        addt = {0: 0, 1: 0}
        for m_ in range(16):
            e_ = m_ % 2
            sl = wstage[e_]
            if m_ >= 2:
                C.wait("sync", "V", cast_w[m_ - 2])
                C.wait("sync", "V", addm[(m_ - 2, 1)])
            dma("sync", sl, din["WoT_p"][:, m_ * 2048:(m_ + 1) * 2048],
                "LW%d" % e_)
            lv = dma("sync", x0own[e_], din["x0T_own"][m_ * 128:(m_ + 1) * 128, :],
                     "LW%d" % e_)
            cwt = [("LW%d" % e_, lv)]
            if m_ >= 2:
                cwt.append(("T", mmg[(m_ - 2, 1)]))
            cast_w[m_] = vopx(lambda e, sl=sl, w=w_bf2[e_]: e.tensor_copy(w, sl),
                              cwt)
            for nh in range(2):
                wt0 = [("V", cast_w[m_])]
                if m_ >= 1:
                    wt0.append(("V", addt[nh]))
                else:
                    wt0.append(("V", max(msel_t[0], msel_t[1])))
                for k in range(16):
                    mmg[(m_, nh)] = mmx(
                        pp[nh][:, :], w_bf2[e_][:, k * 128:(k + 1) * 128],
                        mixT[:, k * 1024 + nh * 512:k * 1024 + nh * 512 + 512],
                        k == 0, k == 15, wt0 if k == 0 else ())
            for nh in range(2):
                addt[nh] = vopx(lambda e, m_=m_, nh=nh, e_=e_: e.tensor_add(
                    slotA[:, m_ * 1024 + nh * 512:m_ * 1024 + nh * 512 + 512],
                    pp[nh][:, :], x0own[e_][:, nh * 512:nh * 512 + 512]),
                    [("T", mmg[(m_, nh)])])
                addm[(m_, nh)] = addt[nh]
        layer_norm(slotA, sq_bf, mbc, rbc, scrA)

    phase_w()
    if stop_after == "xA":
        dump("xA", slotA, 2048, 1024)
        return finish()

    AL.pop()
    AL.push()
    stage4 = [AL.alloc(2048), AL.alloc(2048), AL.alloc(2048), AL.alloc(2048)]
    w13 = AL.alloc(4 * 2048, BF16)
    w2sl = AL.alloc(4 * 2048, BF16)
    hgrp = AL.alloc(4 * 1024, BF16)
    f2t = [AL.alloc(1024, BF16), AL.alloc(1024, BF16)]
    sq_bf0 = AL.alloc(1024, BF16)
    mbc0 = AL.alloc(1024)
    rbc0 = AL.alloc(1024)
    scr0 = AL.alloc(2048)
    scr0bf = AL.alloc(2048, BF16)
    FFN0_PEAK = AL.off

    def phase_ffn0():
        C.fence(("sync",))
        vop(lambda e: e.memset(slotB[:, :], 0.0))
        C.fence()
        cast2, mmg, sig, evac, h128t, cpy = {}, {}, {}, {}, {}, {}
        gv_prev = {0: 0, 1: 0}
        for m_ in range(64):
            e_ = m_ % 2
            s1, s3 = stage4[e_ * 2], stage4[e_ * 2 + 1]
            wA = w13[:, e_ * 4096:e_ * 4096 + 2048]
            wB = w13[:, e_ * 4096 + 2048:e_ * 4096 + 4096]
            if m_ >= 2:
                C.wait("sync", "V", cast2[m_ - 2])
            dma("sync", s1, din["W1_p"][:, m_ * 2048:(m_ + 1) * 2048],
                "LW%d" % e_)
            lv = dma("sync", s3, din["W3_p"][:, m_ * 2048:(m_ + 1) * 2048],
                     "LW%d" % e_)
            wts = [("LW%d" % e_, lv)]
            if m_ >= 2:
                wts.append(("T", mmg[(m_ - 2, 1)]))
            vopx(lambda e, s=s1, w=wA: e.tensor_copy(w, s), wts)
            cast2[m_] = vopx(lambda e, s=s3, w=wB: e.tensor_copy(w, s))
            for nh in range(2):
                b0, b1, bh = nh * 2, nh * 2 + 1, 4 + nh
                o5 = nh * 1024
                wt0 = [("V", cast2[m_])]
                if m_ >= 1:
                    wt0 += [("V", evac[(m_ - 1, nh)]), ("S", sig[(m_ - 1, nh)])]
                for k in range(16):
                    mmx(pp[b0][:, :], wA[:, k * 128:(k + 1) * 128],
                        slotA[:, k * 1024 + nh * 512:k * 1024 + nh * 512 + 512],
                        k == 0, k == 15, wt0 if k == 0 else ())
                for k in range(16):
                    mmg[(m_, nh)] = mmx(
                        pp[b1][:, :], wB[:, k * 128:(k + 1) * 128],
                        slotA[:, k * 1024 + nh * 512:k * 1024 + nh * 512 + 512],
                        k == 0, k == 15)
                swt = [("T", mmg[(m_, nh)])]
                if m_ >= 1:
                    swt.append(("V", evac[(m_ - 1, nh)]))
                sig[(m_, nh)] = sopx(lambda e, b=b0, o5=o5: e.activation(
                    scr0[:, o5:o5 + 512], pp[b][:, :], AF.Sigmoid), swt)
                vopx(lambda e, b=b0, o5=o5: e.tensor_mul(
                    scr0[:, o5 + 512:o5 + 1024], scr0[:, o5:o5 + 512],
                    pp[b][:, :]), [("T", mmg[(m_, nh)]), ("S", sig[(m_, nh)])])
                evac[(m_, nh)] = vopx(lambda e, b=b1, o5=o5: e.tensor_mul(
                    scr0bf[:, o5:o5 + 512], scr0[:, o5 + 512:o5 + 1024],
                    pp[b][:, :]))
                hwt = [("V", evac[(m_, nh)])]
                if m_ >= 1:
                    hwt.append(("V", cpy[(m_ - 1, nh)]))
                h128t[(m_, nh)] = mmx(pp[bh][:, :], H128_bf[:, :],
                                      scr0bf[:, o5:o5 + 512], True, True, hwt)
                gs = "GA" if nh == 0 else "GB"
                cpy[(m_, nh)] = vopx(lambda e, b=bh, o5=o5: e.tensor_copy(
                    scr0bf[:, o5 + 512:o5 + 1024], pp[b][:, :]),
                    [("T", h128t[(m_, nh)]), (gs, gv_prev[nh])])
                C.wait("gpsimd", "V", cpy[(m_, nh)])
                for l in range(2):
                    def g_sh(e, m_=m_, l=l, nh=nh, o5=o5):
                        dstap = shufd[m_ * 2 + l:m_ * 2 + l + 1,
                                      0:64 * 1024].rearrange(
                            "p (u hh t) -> (p u) hh t", hh=2, t=512)
                        return e.dma_start(out=dstap[:, nh, :],
                                           in_=scr0bf[l:128:2, o5 + 512:o5 + 1024])
                    C.op("gpsimd", g_sh)
                    gv = C.inc("gpsimd", gs, 16)
                gv_prev[nh] = gv
        C.fence()
        w2c, hgt, h64 = {}, {}, {}
        odlast = {}
        oduse = {0: 0, 1: 0, 2: 0, 3: 0}
        f2use = {0: 0, 1: 0}
        for grp in range(16):
            for j in range(4):
                m2 = grp * 4 + j
                jj = j % 2
                fb = f2t[jj]
                prev = (grp, j - 2) if j >= 2 else (grp - 1, j + 2)
                C.wait("sync", "T", f2use[jj])
                C.wait("sync", "V", w2c.get(prev, 0))
                lv = dma("sync", fb, shufd[:, m2 * 1024:(m2 + 1) * 1024],
                         "LX%d" % jj)
                lv2 = dma("sync", stage4[2 + jj],
                          din["W2Tp"][m2 * 128:(m2 + 1) * 128, :],
                          "LW%d" % jj)
                cwt = [("LW%d" % jj, lv2)]
                if grp > 0:
                    cwt.append(("T", odlast[grp - 1]))
                w2c[(grp, j)] = vopx(lambda e, j=j, s=stage4[2 + jj]:
                                     e.tensor_copy(
                    w2sl[:, j * 2048:(j + 1) * 2048], s), cwt)
                t0 = mmx(pp[6 + jj][:, :], H64I2_bf[:, :], fb[:, 0:512],
                         True, True,
                         [("LX%d" % jj, lv), ("V", hgt.get(prev, 0))])
                hwt = [("T", t0)]
                if grp > 0:
                    hwt.append(("T", odlast[grp - 1]))
                g0 = vopx(lambda e, j=j, jj=jj: e.tensor_scalar(
                    hgrp[:, j * 1024:j * 1024 + 512],
                    pp[6 + jj][:, :], R8192, None, ALU.mult), hwt)
                t1 = mmx(pp[6 + jj][:, :], H64I2_bf[:, :], fb[:, 512:1024],
                         True, True, [("V", g0)])
                hgt[(grp, j)] = vopx(lambda e, j=j, jj=jj: e.tensor_scalar(
                    hgrp[:, j * 1024 + 512:j * 1024 + 1024],
                    pp[6 + jj][:, :], R8192, None, ALU.mult), [("T", t1)])
                f2use[jj] = t1
            for od in range(16):
                for nh in range(2):
                    bk = (od * 2 + nh) % 4
                    gwt = [("V", oduse[bk])]
                    if od == 0 and nh == 0:
                        gwt += [("V", w2c[(grp, 3)]), ("V", hgt[(grp, 3)])]
                    for j in range(4):
                        tvx = mmx(
                            pp[bk][:, :],
                            w2sl[:, j * 2048 + od * 128:j * 2048 + od * 128 + 128],
                            hgrp[:, j * 1024 + nh * 512:j * 1024 + nh * 512 + 512],
                            j == 0, j == 3, gwt if j == 0 else ())
                    odlast[grp] = tvx
                    oduse[bk] = vopx(lambda e, od=od, nh=nh, bk=bk: e.tensor_add(
                        slotB[:, od * 1024 + nh * 512:od * 1024 + nh * 512 + 512],
                        slotB[:, od * 1024 + nh * 512:od * 1024 + nh * 512 + 512],
                        pp[bk][:, :]), [("T", tvx)])
        C.fence()
        for fc in range(16):
            vop(lambda e, fc=fc: e.tensor_add(
                slotB[:, fc * 1024:(fc + 1) * 1024],
                slotB[:, fc * 1024:(fc + 1) * 1024],
                slotA[:, fc * 1024:(fc + 1) * 1024]))
        layer_norm(slotB, sq_bf0, mbc0, rbc0, scr0[:, 0:1024])

    phase_ffn0()
    if stop_after == "xB":
        dump("xB", slotB, 2048, 1024)
        return finish()

    AL.pop()
    AL.push()
    q1 = AL.alloc(8 * 1024, BF16)
    k1 = AL.alloc(4 * 1536, BF16)
    v1t = AL.alloc(4 * 1536, BF16)
    wstage1 = [AL.alloc(2048), AL.alloc(2048)]
    w1bf2 = [AL.alloc(2048, BF16), AL.alloc(2048, BF16)]
    c1tab = AL.alloc(1024)
    s1tab = AL.alloc(1024)
    s1tab_at = AL.last_at
    kvstage = [AL.alloc(512, BF16), AL.alloc(512, BF16)]
    scr1 = AL.alloc(2048)
    scr1b = AL.alloc(1024)
    scr1bf = AL.alloc(1536, BF16)
    expt1 = [AL.alloc(512, BF16), AL.alloc(512, BF16)]
    sq_bf1 = AL.alloc(1024, BF16)
    mbc1 = AL.alloc(1024)
    rbc1 = AL.alloc(1024)
    L1A_PEAK = AL.off

    def phase_l1a():
        C.fence(("sync",))
        dma("sync", c1tab, din["cos1"][:, :], "LE")
        lv = dma("sync", s1tab, din["sin1"][:, :], "LE")
        C.wait("vector", "LE", lv)
        C.wait("tensor", "LE", lv)
        qst = {"hist": [], "cast": {}, "mmg": {}, "evac": {}, "ptv": 0,
               "sbfT": 0, "tpv": {3: 0, 4: 0}}

        def slab_load(m_, dname):
            h = qst["hist"]
            uid = len(h)
            p2 = h[-2] if uid >= 2 else None
            p1 = h[-1] if uid >= 1 else None
            h.append(uid)
            e_ = uid % 2
            sl = wstage1[e_]
            if p2 is not None:
                C.wait("sync", "V", qst["cast"][p2])
            lv = dma("sync", sl, din[dname][:, m_ * 2048:(m_ + 1) * 2048],
                     "LW%d" % e_)
            cwt = [("LW%d" % e_, lv)]
            if p2 is not None:
                cwt.append(("T", qst["mmg"][p2]))
            qst["cast"][uid] = vopx(lambda e, s=sl, w=w1bf2[e_]:
                                    e.tensor_copy(w, s), cwt)
            return uid, e_, p1

        def qkv1_slab(m_):
            uid, e_, p1 = slab_load(m_, "wqkv1_p")
            for nh in range(2):
                wt0 = [("V", qst["cast"][uid])]
                if p1 is not None:
                    wt0.append(("V", qst["evac"][p1]))
                for k in range(16):
                    qst["mmg"][uid] = mmx(
                        pp[nh][:, :], w1bf2[e_][:, k * 128:(k + 1) * 128],
                        slotB[:, k * 1024 + nh * 512:k * 1024 + nh * 512 + 512],
                        k == 0, k == 15, wt0 if k == 0 else ())
            return uid

        def rope_post(uid, dst_ap):
            cwt = [("T", qst["mmg"][uid]), ("T", qst["sbfT"])]
            vopx(lambda e: e.tensor_copy(scr1bf[:, 0:512], pp[0][:, :]), cwt)
            ev = vopx(lambda e: e.tensor_copy(scr1bf[:, 512:1024], pp[1][:, :]))
            qst["evac"][uid] = ev
            ptl = 0
            for half in range(2):
                ptl = mmx(pp[2 + half][:, :], PT_bf[:, :],
                          scr1bf[:, half * 512:half * 512 + 512], True, True,
                          [("V", ev), ("V", qst["ptv"])])
            qst["sbfT"] = ptl
            vopx(lambda e: e.tensor_mul(scr1[:, 0:1024], scr1bf[:, 0:1024],
                                        c1tab[:, :]))
            vopx(lambda e: e.tensor_mul(scr1b[:, 0:512], pp[2][:, :],
                                        s1tab[:, 0:512]), [("T", ptl)])
            vopx(lambda e: e.tensor_mul(scr1b[:, 512:1024], pp[3][:, :],
                                        s1tab[:, 512:1024]))
            qst["ptv"] = vopx(lambda e, d=dst_ap: e.tensor_add(
                d, scr1[:, 0:1024], scr1b[:, 0:1024]))

        C.fence()
        for m_ in range(16, 24):
            uid = qkv1_slab(m_)
            kvh = (m_ - 16) % 4
            if m_ < 20:
                rope_post(uid, k1[:, kvh * 1536 + 512:kvh * 1536 + 1536])
            else:
                cwt = [("T", qst["mmg"][uid]), ("T", qst["sbfT"])]
                vopx(lambda e: e.tensor_copy(scr1bf[:, 0:512], pp[0][:, :]), cwt)
                ev = vopx(lambda e: e.tensor_copy(scr1bf[:, 512:1024],
                                                  pp[1][:, :]))
                qst["evac"][uid] = ev
                tpt = 0
                for sub in range(8):
                    tb = 3 if sub % 2 == 0 else 4
                    tpt = C.opx("tensor", lambda e, tb=tb, sub=sub: e.transpose(
                        ppb[tb][:, 0:128],
                        scr1bf[:, sub * 128:sub * 128 + 128], id_bf[:, :]),
                        [("V", ev), ("V", qst["tpv"][tb])])
                    qst["tpv"][tb] = vopx(lambda e, tb=tb, kvh=kvh, sub=sub:
                                          e.tensor_copy(
                        v1t[:, kvh * 1536 + 512 + sub * 128:
                            kvh * 1536 + 512 + sub * 128 + 128],
                        ppb[tb][:, 0:128]), [("T", tpt)])
                qst["sbfT"] = tpt
        C.fence(("gpsimd",))
        for kvh in range(4):
            dma("gpsimd", ag_kv_in[kvh * 128:(kvh + 1) * 128, :],
                k1[:, kvh * 1536 + 1024:kvh * 1536 + 1536])
            dma("gpsimd", ag_kv_in[(4 + kvh) * 128:(5 + kvh) * 128, :],
                v1t[:, kvh * 1536 + 1024:kvh * 1536 + 1536])
        C.wait("gpsimd", "G", C.cnt["G"])
        cv_kv = collective("AllGather", ALU.bypass, ag_kv_in, ag_kv_out, "CCA")

        def q_group(g2):
            for hl in range(8):
                uid = qkv1_slab(g2 * 8 + hl)
                rope_post(uid, q1[:, hl * 1024:(hl + 1) * 1024])

        def halo_select():
            C.wait("sync", "CCA", cv_kv)
            ks_t = {0: 0, 1: 0}
            for r in range(4):
                for ch in range(8):
                    stg = kvstage[ch % 2]
                    if r > 0 or ch >= 2:
                        C.wait("sync", "V", ks_t[ch % 2])
                    lv = dma("sync", stg,
                             ag_kv_out[r * 1024 + ch * 128:
                                       r * 1024 + (ch + 1) * 128, :],
                             "LX%d" % (ch % 2))
                    C.wait("vector", "LX%d" % (ch % 2), lv)
                    if ch < 4:
                        dst = k1[:, ch * 1536:ch * 1536 + 512]
                    else:
                        dst = v1t[:, (ch - 4) * 1536:(ch - 4) * 1536 + 512]
                    if r == 0:
                        ks_t[ch % 2] = vop(lambda e, d=dst, s=stg: e.tensor_scalar(
                            d, s, hsel[:, 0:1], None, ALU.mult))
                    else:
                        ks_t[ch % 2] = vop(
                            lambda e, d=dst, s=stg, r=r: e.scalar_tensor_tensor(
                                d, s, hsel[:, r:r + 1], d, ALU.mult, ALU.add))

        def l1_attention(g2):
            for hl in range(8):
                h = g2 * 8 + hl
                kvh = h // 4
                for qt in range(2):
                    q0l = qt * 512
                    subs = []
                    for k0 in range(q0l - 512, q0l + 512, 128):
                        delta = k0 - q0l
                        if delta < 0:
                            mk = maskWc_bf if k0 < 0 else maskW_bf
                            subs.append((k0, mk, (delta + 512) // 128))
                        else:
                            subs.append((k0, maskM_bf, delta // 128))
                    nsub = len(subs)
                    for i, (k0, mk, j) in enumerate(subs):
                        kb = k0 + 512
                        bk = i % 2
                        mm(pp[bk][:, :],
                           k1[:, kvh * 1536 + kb:kvh * 1536 + kb + 128],
                           q1[:, hl * 1024 + q0l:hl * 1024 + q0l + 512])
                        eb = expt1[i % 2]
                        sop(lambda e, eb=eb, bk=bk: e.activation(
                            eb[:, :], pp[bk][:, :], AF.Exp, bias=-C_L1, scale=SC0))
                        vop(lambda e, eb=eb, mk=mk, j=j: e.tensor_mul(
                            eb[:, :], eb[:, :], mk[:, 512 - 128 * j:1024 - 128 * j]))
                        colsum(pp[3][:1, :], eb[:, :], i == 0, i == nsub - 1)
                        mm(pp[2][:, :],
                           v1t[:, kvh * 1536 + (kb // 128) * 128:
                               kvh * 1536 + (kb // 128) * 128 + 128],
                           eb[:, :], i == 0, i == nsub - 1)
                    vop(lambda e: e.reciprocal(rows[:1, 0:512], pp[3][:1, :]))
                    vop(lambda e: e.tensor_copy(scr1[:, 0:512], pp[2][:, :]))
                    bcast(pp[4][:, :], rows[:1, 0:512])
                    vop(lambda e, hl=hl, q0l=q0l: e.tensor_mul(
                        q1[:, hl * 1024 + q0l:hl * 1024 + q0l + 512],
                        scr1[:, 0:512], pp[4][:, :]))

        def wo1_partial(g2):
            addw = {}
            for m_ in range(16):
                uid, e_, p1 = slab_load(m_, "wo1_p")
                for nh in range(2):
                    wt0 = [("V", qst["cast"][uid]), ("V", C.cnt["V"])]
                    if p1 is not None:
                        wt0.append(("V", qst["evac"][p1]))
                    for kl in range(8):
                        k = g2 * 8 + kl
                        qst["mmg"][uid] = mmx(
                            pp[nh][:, :], w1bf2[e_][:, k * 128:(k + 1) * 128],
                            q1[:, kl * 1024 + nh * 512:kl * 1024 + nh * 512 + 512],
                            kl == 0, kl == 7, wt0 if kl == 0 else ())
                for nh in range(2):
                    if g2 == 0:
                        addw[nh] = vopx(lambda e, m_=m_, nh=nh: e.tensor_add(
                            slotA[:, m_ * 1024 + nh * 512:
                                  m_ * 1024 + nh * 512 + 512],
                            pp[nh][:, :],
                            slotB[:, m_ * 1024 + nh * 512:
                                  m_ * 1024 + nh * 512 + 512]),
                            [("T", qst["mmg"][uid])])
                    else:
                        addw[nh] = vopx(lambda e, m_=m_, nh=nh: e.tensor_add(
                            slotA[:, m_ * 1024 + nh * 512:
                                  m_ * 1024 + nh * 512 + 512],
                            slotA[:, m_ * 1024 + nh * 512:
                                  m_ * 1024 + nh * 512 + 512],
                            pp[nh][:, :]), [("T", qst["mmg"][uid])])
                qst["evac"][uid] = addw[1]

        q_group(0)
        halo_select()
        l1_attention(0)
        wo1_partial(0)
        q_group(1)
        marks["wm"] = dict(C.cnt)
        l1_attention(1)
        wo1_partial(1)
        layer_norm(slotA, sq_bf1, mbc1, rbc1, scr1[:, 0:1024])

    phase_l1a()
    if stop_after == "xC":
        dump("xC", slotA, 2048, 1024)
        return finish()

    AL.pop()
    AL.push()
    xq = AL.alloc(16 * 1024, BF16)
    stageF = [AL.alloc(2048), AL.alloc(2048)]
    wslbf = AL.alloc(4 * 2048, BF16)
    scFbf = AL.alloc(2048, BF16)
    hgrp2 = AL.alloc(4 * 1024, BF16)
    f2b = [AL.alloc(1024, BF16), AL.alloc(1024, BF16)]
    scbc = AL.alloc(1024)
    invbc = AL.alloc(1024)
    scF = AL.alloc(2048)
    mxtab = AL.alloc(256)
    i32t = AL.alloc(1024, I32)
    sq_bf2 = AL.alloc(1024, BF16)
    mbc2 = AL.alloc(1024)
    rbc2 = AL.alloc(1024)
    outf = [AL.alloc(1024), AL.alloc(1024)]
    L1F_PEAK = AL.off

    wm_stage = [AL.view(slotB_at, 2048, F32),
                AL.view(slotB_at + 8192, 2048, F32)]
    wm_cols = AL.view(s1tab_at, 64, F32)

    def wmean_scale(dram, nld, slot, entry=None):
        wm_t = {0: 0, 1: 0}
        for r in range(nld):
            buf = wm_stage[r % 2]
            if r >= 2:
                C.wait("sync", "V", wm_t[r % 2])
            elif entry is not None:
                for sem in ("T", "V", "S"):
                    C.wait("sync", sem, entry[sem])
            lv = dma("sync", buf, dram[r], "LM%d" % (r % 2))
            wm_t[r % 2] = vopx(lambda e, b=buf, r=r: e.tensor_reduce(
                wm_cols[:, r:r + 1], b, AX.X, ALU.add,
                apply_absolute_value=True), [("LM%d" % (r % 2), lv)])
        sop(lambda e, nld=nld: e.add(wm_cols[:, 0:nld], wm_cols[:, 0:nld], 0.0))
        vop(lambda e, nld=nld: e.tensor_reduce(
            wm_cols[:, 40:41], wm_cols[:, 0:nld], AX.X, ALU.add))
        mm(pp[0][:1, 0:1], onesc_f[:, 0:1], wm_cols[:, 40:41])
        vop(lambda e: e.tensor_copy(rows[:1, 1024:1025], pp[0][:1, 0:1]))
        sop(lambda e: e.activation(rows[:1, 1025:1026], rows[:1, 1024:1025],
                                   AF.Ln, bias=0.0, scale=1.0 / 8388608.0))
        vop(lambda e: e.tensor_scalar(rows[:1, 1026:1027], rows[:1, 1025:1026],
                                      math.log(1e-5), None, ALU.max))
        sop(lambda e: e.activation(rows[:1, 4:5], rows[:1, 1026:1027],
                                   AF.Exp, bias=0.0, scale=-1.0))
        sop(lambda e: e.activation(rows[:1, 5:6], rows[:1, 1026:1027],
                                   AF.Exp, bias=math.log(0.25), scale=2.0))
        sop(lambda e: e.activation(rows[:1, 6:7], rows[:1, 1026:1027],
                                   AF.Exp, bias=math.log(0.5), scale=1.0))
        bcast(pp[0][:, 0:3], rows[:1, 4:7])
        vop(lambda e, slot=slot: e.tensor_copy(srow[:, slot:slot + 1],
                                               pp[0][:, 0:1]))
        vop(lambda e, slot=slot: e.tensor_copy(srow[:, slot + 1:slot + 2],
                                               pp[0][:, 1:2]))
        vop(lambda e, slot=slot: e.tensor_copy(srow[:, slot + 2:slot + 3],
                                               pp[0][:, 2:3]))

    def fwht_rows_from_maxcols(nsc, stride):
        for tt in range(8):
            vop(lambda e, tt=tt, st=stride: e.tensor_reduce(
                mxacc[:, tt:tt + 1], mxtab[:, tt * st:(tt + 1) * st],
                AX.X, ALU.max))
        for tt in range(8):
            sop(lambda e, tt=tt: e.add(scFbf[:, tt:tt + 1],
                                       mxacc[:, tt:tt + 1], 0.0))
        for tt in range(8):
            tb = 5 if tt % 2 == 0 else 6
            transpose(ppb[tb][:1, 0:128], scFbf[:, tt:tt + 1])
            vop(lambda e, tb=tb, tt=tt: e.tensor_copy(
                rows[:1, tt * 128:(tt + 1) * 128], ppb[tb][:1, 0:128]))
        sop(lambda e: e.add(rows[:1, 0:1024], rows[:1, 0:1024], 0.0))
        vop(lambda e, nsc=nsc: e.tensor_scalar(rows[:1, 0:1024], rows[:1, 0:1024],
                                               nsc, None, ALU.mult))
        vop(lambda e: e.tensor_scalar(rows[:1, 0:1024], rows[:1, 0:1024],
                                      1e-5, None, ALU.max))
        vop(lambda e: e.reciprocal(rows2[:1, 0:1024], rows[:1, 0:1024]))
        vop(lambda e: e.tensor_scalar(rows2[:1, 0:1024], rows2[:1, 0:1024],
                                      127.0, None, ALU.mult))
        vop(lambda e: e.tensor_scalar(rows[:1, 0:1024], rows[:1, 0:1024],
                                      1.0 / 127.0, None, ALU.mult))
        for half in range(2):
            h0 = half * 512
            bcast(pp[0][:, :], rows2[:1, h0:h0 + 512])
            bcast(pp[1][:, :], rows[:1, h0:h0 + 512])
            vop(lambda e, h0=h0: e.tensor_copy(scbc[:, h0:h0 + 512], pp[0][:, :]))
            vop(lambda e, h0=h0: e.tensor_copy(invbc[:, h0:h0 + 512], pp[1][:, :]))

    def phase_l1f():
        wmean_scale([din["wup_p"][:, r * 2048:(r + 1) * 2048] for r in range(32)],
                    32, 0, entry=marks["wm"])
        if stop_after == "wsc":
            dump("srow", srow, 128, 8)
            return True
        C.fence(("gpsimd",))
        for u in range(8):
            def g_pre(e, u=u):
                src = slotA[u * 16:(u + 1) * 16, :].rearrange(
                    "cp (c t) -> cp c t", t=1024)
                dstap = shufd[:, 0:16 * 1024].rearrange("p (cp t) -> p cp t", t=1024)
                return e.dma_start(out=dstap[u:128:8, :, :].transpose([1, 0, 2]),
                                   in_=src)
            C.op("gpsimd", g_pre)
            C.inc("gpsimd", "G", 16)
        C.wait("gpsimd", "G", C.cnt["G"])
        gv = dma("gpsimd", xq[:, :], shufd[:, 0:16 * 1024])
        C.wait("tensor", "G", gv)
        C.wait("vector", "G", gv)
        for fc in range(16):
            for nh in range(2):
                mm(pp[nh][:, :], H128_bf[:, :],
                   xq[:, fc * 1024 + nh * 512:fc * 1024 + nh * 512 + 512])
            for nh in range(2):
                vop(lambda e, fc=fc, nh=nh: e.tensor_copy(
                    xq[:, fc * 1024 + nh * 512:fc * 1024 + nh * 512 + 512],
                    pp[nh][:, :]))
        C.fence(("gpsimd",))
        for l in range(8):
            def g_mid(e, l=l):
                src = xq[l:128:8, :].rearrange("hg (c t) -> hg c t", t=1024)
                dstap = shufd[:, 0:16 * 1024].rearrange("p (hg t) -> p hg t", t=1024)
                return e.dma_start(out=dstap[l:128:8, :, :].transpose([1, 0, 2]),
                                   in_=src)
            C.op("gpsimd", g_mid)
            C.inc("gpsimd", "G", 16)
        C.wait("gpsimd", "G", C.cnt["G"])
        gv = dma("gpsimd", xq[:, :], shufd[:, 0:16 * 1024])
        C.wait("tensor", "G", gv)
        C.wait("vector", "G", gv)
        for fc in range(16):
            for tt in range(8):
                bb = 2 + (tt % 4)
                mm(pp[bb][:, 0:128],
                   xq[:, fc * 1024 + tt * 128:fc * 1024 + tt * 128 + 128],
                   H16I8_bf[:, :])
                vop(lambda e, bb=bb, tt=tt, fc=fc: e.tensor_reduce(
                    mxtab[:, tt * 16 + fc:tt * 16 + fc + 1], pp[bb][:, 0:128],
                    AX.X, ALU.max, apply_absolute_value=True))
            for nh in range(2):
                mm(pp[nh][:, :], H16I8_bf[:, :],
                   xq[:, fc * 1024 + nh * 512:fc * 1024 + nh * 512 + 512])
            for nh in range(2):
                vop(lambda e, fc=fc, nh=nh: e.tensor_scalar(
                    xq[:, fc * 1024 + nh * 512:fc * 1024 + nh * 512 + 512],
                    pp[nh][:, :], R2048, None, ALU.mult))
        fwht_rows_from_maxcols(R2048, 16)
        if stop_after == "xh1":
            dump("xh", xq, 2048, 1024)
            dump("scbc", scbc, 128, 1024)
            dump("invbc", invbc, 128, 1024)
            return True
        for fc in range(16):
            vop(lambda e, fc=fc: e.tensor_mul(
                scF[:, 0:1024], xq[:, fc * 1024:(fc + 1) * 1024], scbc[:, :]))
            vop(lambda e: e.tensor_copy(i32t[:, :], scF[:, 0:1024]))
            vop(lambda e, fc=fc: e.scalar_tensor_tensor(
                xq[:, fc * 1024:(fc + 1) * 1024], i32t[:, :], 1.0, invbc[:, :],
                ALU.mult, ALU.mult))
        if stop_after == "xq1":
            dump("xq", xq, 2048, 1024)
            return True
        C.fence()
        sg2, addt, mmg, rel, stt, h128t, cpy = {}, {}, {}, {}, {}, {}, {}
        gv_prev = {0: C.cnt["GA"], 1: C.cnt["GB"]}
        for m_ in range(32):
            e_ = m_ % 2
            sl = stageF[e_]
            sA = wslbf[:, e_ * 4096:e_ * 4096 + 2048]
            sB = wslbf[:, e_ * 4096 + 2048:e_ * 4096 + 4096]
            if m_ >= 2:
                C.wait("sync", "S", sg2[m_ - 2])
            lv = dma("sync", sl, din["wup_p"][:, m_ * 2048:(m_ + 1) * 2048],
                     "LW%d" % e_)
            swt = [("LW%d" % e_, lv)]
            if m_ >= 2:
                swt.append(("T", mmg[(m_ - 2, 1)]))
            sopx(lambda e, s=sl, w=sA: e.activation(
                w, s, AF.Sign, bias=-0.5, scale=srow[:, 0:1]), swt)
            sg2[m_] = sopx(lambda e, s=sl, w=sB: e.activation(
                w, s, AF.Sign, bias=0.5, scale=srow[:, 0:1]),
                [("V", addt.get(m_ - 2, 0))])
            addt[m_] = vopx(lambda e, a=sA, b=sB: e.tensor_add(a, a, b),
                            [("S", sg2[m_])])
            for nh in range(2):
                o5 = nh * 1024
                wt0 = [("V", addt[m_])]
                if m_ >= 1:
                    wt0.append(("S", rel[(m_ - 1, nh)]))
                for k in range(16):
                    mmg[(m_, nh)] = mmx(
                        pp[nh][:, :], sA[:, k * 128:(k + 1) * 128],
                        xq[:, k * 1024 + nh * 512:k * 1024 + nh * 512 + 512],
                        k == 0, k == 15, wt0 if k == 0 else ())
                rwt = [("T", mmg[(m_, nh)])]
                if m_ >= 1:
                    rwt.append(("V", stt[(m_ - 1, nh)]))
                rel[(m_, nh)] = sopx(lambda e, nh=nh: e.activation(
                    scF[:, nh * 512:nh * 512 + 512], pp[nh][:, :], AF.Relu), rwt)
                stt[(m_, nh)] = vopx(lambda e, nh=nh, o5=o5:
                                     e.scalar_tensor_tensor(
                    scFbf[:, o5:o5 + 512], scF[:, nh * 512:nh * 512 + 512],
                    srow[:, 1:2], scF[:, nh * 512:nh * 512 + 512],
                    ALU.mult, ALU.mult), [("S", rel[(m_, nh)])])
                hwt = [("V", stt[(m_, nh)])]
                if m_ >= 1:
                    hwt.append(("V", cpy[(m_ - 1, nh)]))
                h128t[(m_, nh)] = mmx(pp[2 + nh][:, :], H128_bf[:, :],
                                      scFbf[:, o5:o5 + 512], True, True, hwt)
                gs = "GA" if nh == 0 else "GB"
                cpy[(m_, nh)] = vopx(lambda e, nh=nh, o5=o5: e.tensor_copy(
                    scFbf[:, o5 + 512:o5 + 1024], pp[2 + nh][:, :]),
                    [("T", h128t[(m_, nh)]), (gs, gv_prev[nh])])
                C.wait("gpsimd", "V", cpy[(m_, nh)])
                for l in range(4):
                    def g_sh4(e, m_=m_, l=l, nh=nh, o5=o5):
                        dstap = shufd[m_ * 4 + l:m_ * 4 + l + 1,
                                      0:32 * 1024].rearrange(
                            "p (u hh t) -> (p u) hh t", hh=2, t=512)
                        return e.dma_start(out=dstap[:, nh, :],
                                           in_=scFbf[l:128:4, o5 + 512:o5 + 1024])
                    C.op("gpsimd", g_sh4)
                    gv = C.inc("gpsimd", gs, 16)
                gv_prev[nh] = gv
        wmean_scale([din["wdnTp"][r * 128:(r + 1) * 128, :] for r in range(32)],
                    32, 4)
        C.fence()
        if stop_after == "h4s":
            if "h4s" in tap_t:
                stg = AL.view(AL.nbytes - 8192, 2048, F32)
                for m2 in range(32):
                    lv = dma("sync", f2b[0], shufd[:, m2 * 1024:(m2 + 1) * 1024],
                             "LX0")
                    C.wait("vector", "LX0", lv)
                    vv = vop(lambda e: e.tensor_copy(stg[:, 0:1024], f2b[0]))
                    C.wait("gpsimd", "V", vv)
                    gv = dma("gpsimd", tap_t["h4s"][m2 * 128:(m2 + 1) * 128, :],
                             stg[:, 0:1024])
                    C.wait("vector", "G", gv)
                    C.wait("sync", "G", gv)
            return True
        mx_t = {0: 0, 1: 0}
        for m2 in range(32):
            fb = f2b[m2 % 2]
            if m2 >= 2:
                C.wait("sync", "T", mx_t[m2 % 2])
            lv = dma("sync", fb, shufd[:, m2 * 1024:(m2 + 1) * 1024],
                     "LX%d" % (m2 % 2))
            C.wait("tensor", "LX%d" % (m2 % 2), lv)
            for tt in range(8):
                bb = 2 + (tt % 4)
                tvx = mm(pp[bb][:, 0:128], fb[:, tt * 128:(tt + 1) * 128],
                         H32I4_bf[:, :])
                vop(lambda e, bb=bb, tt=tt, m2=m2: e.tensor_reduce(
                    mxtab[:, tt * 32 + m2:tt * 32 + m2 + 1], pp[bb][:, 0:128],
                    AX.X, ALU.max, apply_absolute_value=True))
            mx_t[m2 % 2] = tvx
        fwht_rows_from_maxcols(R4096, 32)
        if stop_after == "fw4m":
            dump("scbc", scbc, 128, 1024)
            dump("invbc", invbc, 128, 1024)
            return True
        vop(lambda e: e.memset(slotB[:, :], 0.0))
        C.fence()
        sg1d, sg2d, addd, qtt, bankev = {}, {}, {}, {}, {}
        odlast = {}
        oduse = {0: 0, 1: 0, 2: 0, 3: 0}
        f2use = {0: 0, 1: 0}
        for grp in range(8):
            for j in range(4):
                m2 = grp * 4 + j
                jj = j % 2
                fb = f2b[jj]
                prev = (grp, j - 2) if j >= 2 else (grp - 1, j + 2)
                prevj = (grp, j - 1) if j >= 1 else (grp - 1, 3)
                C.wait("sync", "T", f2use[jj])
                C.wait("sync", "S", sg2d.get(prev, 0))
                lv = dma("sync", fb, shufd[:, m2 * 1024:(m2 + 1) * 1024],
                         "LX%d" % jj)
                sl = stageF[jj]
                lv2 = dma("sync", sl, din["wdnTp"][m2 * 128:(m2 + 1) * 128, :],
                          "LW%d" % jj)
                swt = [("LW%d" % jj, lv2), ("V", addd.get(prevj, 0))]
                sg1d[(grp, j)] = sopx(lambda e, s=sl: e.activation(
                    scFbf[:, 0:2048], s, AF.Sign, bias=-0.5,
                    scale=srow[:, 4:5]), swt)
                s2wt = []
                if grp > 0:
                    s2wt.append(("T", odlast[grp - 1]))
                sg2d[(grp, j)] = sopx(lambda e, s=sl, j=j: e.activation(
                    wslbf[:, j * 2048:(j + 1) * 2048], s, AF.Sign, bias=0.5,
                    scale=srow[:, 4:5]), s2wt)
                addd[(grp, j)] = vopx(lambda e, j=j: e.tensor_add(
                    wslbf[:, j * 2048:(j + 1) * 2048],
                    wslbf[:, j * 2048:(j + 1) * 2048],
                    scFbf[:, 0:2048]), [("S", sg2d[(grp, j)])])
                for nh in range(2):
                    o6 = nh * 1024
                    mwt = [("V", bankev.get(jj, 0))]
                    if nh == 0:
                        mwt.append(("LX%d" % jj, lv))
                    tvx = mmx(pp[6 + jj][:, :], H32I4_bf[:, :],
                              fb[:, nh * 512:nh * 512 + 512], True, True, mwt)
                    bankev[jj] = vopx(lambda e, o6=o6, jj=jj: e.tensor_scalar(
                        scF[:, o6:o6 + 512], pp[6 + jj][:, :], R4096, None,
                        ALU.mult), [("T", tvx)])
                    vopx(lambda e, o6=o6, nh=nh: e.tensor_mul(
                        scF[:, o6 + 512:o6 + 1024], scF[:, o6:o6 + 512],
                        scbc[:, nh * 512:nh * 512 + 512]))
                    vopx(lambda e, nh=nh, o6=o6: e.tensor_copy(
                        i32t[:, nh * 512:nh * 512 + 512],
                        scF[:, o6 + 512:o6 + 1024]))
                    hwt = []
                    if grp > 0:
                        hwt.append(("T", odlast[grp - 1]))
                    qtt[(grp, j)] = vopx(lambda e, j=j, nh=nh:
                                         e.scalar_tensor_tensor(
                        hgrp2[:, j * 1024 + nh * 512:j * 1024 + nh * 512 + 512],
                        i32t[:, nh * 512:nh * 512 + 512], 1.0,
                        invbc[:, nh * 512:nh * 512 + 512],
                        ALU.mult, ALU.mult), hwt)
                    f2use[jj] = tvx
            for od in range(16):
                for nh in range(2):
                    bk = (od * 2 + nh) % 4
                    gwt = [("V", oduse[bk])]
                    if od == 0 and nh == 0:
                        gwt += [("V", addd[(grp, 3)]), ("V", qtt[(grp, 3)])]
                    for j in range(4):
                        tvx = mmx(
                            pp[bk][:, :],
                            wslbf[:, j * 2048 + od * 128:j * 2048 + od * 128 + 128],
                            hgrp2[:, j * 1024 + nh * 512:j * 1024 + nh * 512 + 512],
                            j == 0, j == 3, gwt if j == 0 else ())
                    odlast[grp] = tvx
                    oduse[bk] = vopx(lambda e, od=od, nh=nh, bk=bk:
                                     e.scalar_tensor_tensor(
                        slotB[:, od * 1024 + nh * 512:od * 1024 + nh * 512 + 512],
                        pp[bk][:, :], srow[:, 6:7],
                        slotB[:, od * 1024 + nh * 512:od * 1024 + nh * 512 + 512],
                        ALU.mult, ALU.add), [("T", tvx)])
        C.fence()
        for fc in range(16):
            vop(lambda e, fc=fc: e.tensor_add(
                slotB[:, fc * 1024:(fc + 1) * 1024],
                slotB[:, fc * 1024:(fc + 1) * 1024],
                slotA[:, fc * 1024:(fc + 1) * 1024]))
        out_g = {0: 0, 1: 0}

        def out_fn(fc, scrA_, rbc_):
            ob = outf[fc % 2]
            gs = "GA" if fc % 2 == 0 else "GB"
            C.wait("vector", gs, out_g[fc % 2])
            vv = vop(lambda e, ob=ob: e.tensor_mul(ob[:, :], scrA_[:, 0:1024],
                                                   rbc_[:, :]))
            C.wait("gpsimd", "V", vv)
            out_g[fc % 2] = dma("gpsimd", y_out[fc * 128:(fc + 1) * 128, :],
                                ob[:, :], gs)

        layer_norm(slotB, sq_bf2, mbc2, rbc2, scF[:, 0:1024], out_fn=out_fn)
        C.wait("gpsimd", "GA", C.cnt["GA"])
        C.wait("gpsimd", "GB", C.cnt["GB"])

    if phase_l1f():
        return finish()
    C.fence()
    LAST_WAITED.clear()
    LAST_WAITED.update({k: set(v) for k, v in C.waited.items()})
    C.replay(es.enter_context(nc.Block()))
    es.close()
    return nc


_NC_CACHE = {}


def kernel(**inputs):
    in_maps = host_prepare(inputs)
    key = "full"
    if key not in _NC_CACHE:
        build_nc(stop_after="out")
        _NC_CACHE[key] = build_nc(stop_after="out",
                                  want=dict(LAST_WAITED))
    nc = _NC_CACHE[key]
    res = run_bass_kernel_spmd(nc, in_maps, core_ids=list(range(8)))
    out = np.zeros((B, S, D), np.float32)
    for core in range(8):
        g, t = core // 4, core % 4
        out[g, t * TOK:(t + 1) * TOK, :] = np.asarray(res.results[core]["y_out"]).T
    return out


# version 12
# speedup vs baseline: 1.0154x; 1.0126x over previous
import bisect
import math
import numpy as np
import ml_dtypes
from contextlib import ExitStack
import concourse.bass as bass
import concourse.mybir as mybir
from concourse.bass_utils import run_bass_kernel_spmd

F32 = mybir.dt.float32
BF16 = mybir.dt.bfloat16
I32 = mybir.dt.int32
AF = mybir.ActivationFunctionType
ALU = mybir.AluOpType
AX = mybir.AxisListType

B, S, D = 2, 4096, 2048
CH, NCH, TOK = 1024, 4, 1024
BASE_LR_INV = 0.001 + math.log(-math.expm1(-0.001))
C_L0, C_L1 = 24.0, 16.0
SC0 = 1.0 / math.sqrt(128.0)
R2048 = 1.0 / math.sqrt(2048.0)
R4096 = 1.0 / math.sqrt(4096.0)
R8192 = 1.0 / math.sqrt(8192.0)
ENGINES = ("tensor", "vector", "scalar", "sync", "gpsimd")
ARN_ELEMS = 53200

CONST_VALS = sorted(set([
    -C_L0, -C_L1, 0.0, SC0, -0.5, 0.5, 1.0, 1e-6, 1e-5, 1e-24, 1e-12,
    1.0 / D, 1.0 / 512.0, R8192, R4096, R2048,
    -1.0, 2.0, math.log(0.5), math.log(0.25), 1.0 / 8388608.0,
]))


def _hadamard(n):
    h = np.array([[1.0]])
    while h.shape[0] < n:
        h = np.block([[h, h], [h, -h]])
    return h


def _f1_perm(n, nch):
    lb = int(math.log2(nch))
    out = np.zeros(n, np.int64)
    for m in range(nch):
        for p in range(128):
            out[m * 128 + p] = (p << lb) | m
    return out


def _f3_perm(n, nch):
    lb_hi = int(math.log2(nch))
    k = 128 // nch
    lbk = int(math.log2(k))
    out = np.zeros(n, np.int64)
    for m2 in range(nch):
        for p2 in range(128):
            i_hi = (m2 << lbk) | (p2 & (k - 1))
            q = p2 >> lbk
            out[m2 * 128 + p2] = (i_hi << lb_hi) | q
    return out


def _pack_w(W):
    K, N = W.shape
    return np.ascontiguousarray(
        W.reshape(K // 128, 128, N // 128, 128)
        .transpose(1, 2, 0, 3).reshape(128, -1))


def host_prepare(inputs):
    f32 = np.float32
    x0 = np.ascontiguousarray(np.asarray(inputs["L0_attn_hidden_states"], f32))
    WqkvT = np.asarray(inputs["L0_attn_Wqkv"], f32).T
    WoT_p = _pack_w(np.ascontiguousarray(np.asarray(inputs["L0_attn_Wo"], f32).T))
    lr_wT = np.asarray(inputs["L0_attn_lr_w"], f32).T
    f1_8192 = _f1_perm(8192, 64)
    f3_8192 = _f3_perm(8192, 64)
    f1_4096 = _f1_perm(4096, 32)
    f3_4096 = _f3_perm(4096, 32)
    f3_2048 = _f3_perm(2048, 16)
    W1_p = _pack_w(np.asarray(inputs["L0_ffn_W1"], f32).T[:, f1_8192])
    W3_p = _pack_w(np.asarray(inputs["L0_ffn_W3"], f32).T[:, f1_8192])
    W2Tp = np.ascontiguousarray(np.asarray(inputs["L0_ffn_W2"], f32).T[f3_8192, :])
    wup_p = _pack_w(np.asarray(inputs["L1_ffn_w_up"], f32).T[f3_2048, :][:, f1_4096])
    wdnTp = np.ascontiguousarray(np.asarray(inputs["L1_ffn_w_down"], f32).T[f3_4096, :])
    wqkv1_p = _pack_w(np.asarray(inputs["L1_attn_w_qkv"], f32))
    wo1_p = _pack_w(np.asarray(inputs["L1_attn_w_o"], f32))

    ang0 = np.arange(S, dtype=np.float64)[:, None] / (
        500000.0 ** (np.arange(0, 128, 2, dtype=np.float64) / 128))[None, :]
    cos0f = np.ascontiguousarray(np.concatenate([np.cos(ang0).T] * 2, 0).astype(f32))
    sin0f = np.ascontiguousarray(np.concatenate([np.sin(ang0).T] * 2, 0).astype(f32))
    ang1 = np.arange(S, dtype=np.float64)[:, None] / (
        1000000.0 ** (np.arange(0, 128, 2, dtype=np.float64) / 128))[None, :]
    cos1f = np.concatenate([np.cos(ang1).T] * 2, 0).astype(f32)
    sin1f = np.concatenate([np.sin(ang1).T] * 2, 0).astype(f32)

    P = np.zeros((128, 128), f32)
    for i in range(64):
        P[i, i + 64] = -1.0
        P[i + 64, i] = 1.0
    PT = np.ascontiguousarray(P.T)
    ident = np.eye(128, dtype=f32)
    kk = np.arange(128)[:, None]
    cc = np.arange(1024)[None, :]
    maskM = (kk <= cc - 512).astype(f32)
    maskW = np.ascontiguousarray(1.0 - maskM)

    def chunkmajor(vec):
        return np.ascontiguousarray(np.asarray(vec, f32).reshape(4, 128).T)

    in_maps = []
    for core in range(8):
        g, t = core // 4, core % 4
        cols = slice(512 * t, 512 * t + 512)
        m = {}
        m["x0T"] = np.ascontiguousarray(x0[g].T)
        m["x0T_own"] = np.ascontiguousarray(x0[g].T[:, t * TOK:(t + 1) * TOK])
        m["wqkv_p"] = _pack_w(np.concatenate(
            [WqkvT[:, 512 * t:512 * t + 512],
             WqkvT[:, 2048 + 512 * t:2048 + 512 * t + 512],
             WqkvT[:, 4096 + 512 * t:4096 + 512 * t + 512]], 1))
        m["lrw_p"] = np.ascontiguousarray(
            lr_wT[:, [t, 4 + t, 8 + t]].reshape(16, 128, 3)
            .transpose(1, 0, 2).reshape(128, 48))
        m["lrb"] = np.ascontiguousarray(
            np.asarray(inputs["L0_attn_lr_b"], f32)[[t, 4 + t, 8 + t]].reshape(1, 3))
        m["pvecs"] = np.ascontiguousarray(np.concatenate([
            chunkmajor(np.asarray(inputs["L0_attn_q_norm_w"])[cols]),
            chunkmajor(np.asarray(inputs["L0_attn_k_norm_w"])[cols]),
            chunkmajor(np.asarray(inputs["L0_attn_qk_scale"])[cols, 0]),
            chunkmajor(np.asarray(inputs["L0_attn_qk_offset"])[cols, 0]),
            chunkmajor(np.asarray(inputs["L0_attn_qk_scale"])[cols, 1]),
            chunkmajor(np.asarray(inputs["L0_attn_qk_offset"])[cols, 1]),
            chunkmajor(np.asarray(inputs["L0_attn_ttt_norm_w"])),
        ], axis=1))
        m["w0T"] = np.ascontiguousarray(
            np.asarray(inputs["L0_attn_w0"][t], f32).T.reshape(4, 128, 512)
            .transpose(1, 0, 2).reshape(128, 2048))
        m["w2T"] = np.ascontiguousarray(
            np.asarray(inputs["L0_attn_w2"][t], f32).T.reshape(4, 128, 512)
            .transpose(1, 0, 2).reshape(128, 2048))
        m["w1N"] = np.ascontiguousarray(
            np.asarray(inputs["L0_attn_w1"][t], f32).reshape(4, 128, 512)
            .transpose(1, 0, 2).reshape(128, 2048))
        m["WoT_p"] = WoT_p
        m["W1_p"] = W1_p
        m["W3_p"] = W3_p
        m["W2Tp"] = W2Tp
        m["wqkv1_p"] = wqkv1_p
        m["wo1_p"] = wo1_p
        m["wup_p"] = wup_p
        m["wdnTp"] = wdnTp
        m["cos0"] = cos0f
        m["sin0"] = sin0f
        m["cos1"] = np.ascontiguousarray(cos1f[:, t * TOK:(t + 1) * TOK])
        m["sin1"] = np.ascontiguousarray(sin1f[:, t * TOK:(t + 1) * TOK])
        m["PT"] = PT
        m["ident"] = ident
        m["maskM"] = maskM
        m["maskW"] = maskW
        m["maskWc"] = np.ascontiguousarray(
            maskW if t > 0 else np.zeros((128, 1024), f32))
        m["H128"] = _hadamard(128).astype(f32)
        m["H64I2"] = np.kron(_hadamard(64), np.eye(2)).astype(f32)
        m["H32I4"] = np.kron(_hadamard(32), np.eye(4)).astype(f32)
        m["H16I8"] = np.kron(_hadamard(16), np.eye(8)).astype(f32)
        msel = np.zeros((128, 4), f32)
        msel[:, t] = 1.0
        m["msel"] = msel
        hsel = np.zeros((128, 4), f32)
        hsel[:, (t + 3) % 4] = 1.0
        m["hsel"] = hsel
        in_maps.append(m)
    return in_maps


class Ctx:
    BL = {"tensor": ("V", "S"), "vector": ("T", "S"), "scalar": ("T", "V")}
    TICK = {"tensor": "T", "vector": "V", "scalar": "S"}

    THIN = ("T", "V", "S")

    def __init__(self, nc, serial=True, want=None):
        self.nc = nc
        self.serial = serial
        self.prog = {e: [] for e in ENGINES}
        self.cnt = {}
        self.lastwait = {}
        self.S = {}
        self.want = want
        self.waited = {}
        self.attached = {}

    def addsem(self, es, name):
        self.S[name] = es.enter_context(self.nc.semaphore(name))
        self.cnt[name] = 0

    def wait(self, eng, sem, val):
        if val is None or val <= 0:
            return
        key = (eng, sem)
        if self.lastwait.get(key, 0) >= val:
            return
        self.lastwait[key] = val
        self.waited.setdefault(sem, set()).add(int(val))
        if self.want is not None and sem in self.THIN:
            att = self.attached.get(sem, [])
            rv = bisect.bisect_right(att, int(val))
            assert rv > 0 and att[rv - 1] == int(val), \
                f"wait on unattached tick {sem}>={val}"
            self.prog[eng].append([None, (sem, rv)])
        else:
            self.prog[eng].append([None, (sem, int(val))])

    def inc(self, eng, sem, by):
        self.cnt[sem] += by
        if (self.want is not None and sem in self.THIN
                and self.cnt[sem] not in self.want.get(sem, ())):
            return self.cnt[sem]
        self.attached.setdefault(sem, []).append(self.cnt[sem])
        slot = None
        for item in reversed(self.prog[eng]):
            if item[0] is not None:
                slot = item
                break
        assert slot is not None
        fn = slot[0]
        sh = self.S[sem]
        slot[0] = (lambda e, fn=fn, sh=sh, by=by: fn(e).then_inc(sh, by))
        return self.cnt[sem]

    def op(self, eng, fn):
        if self.serial and eng in self.BL:
            for s in self.BL[eng]:
                self.wait(eng, s, self.cnt[s])
        self.prog[eng].append([fn, None])
        if eng in self.TICK:
            return self.inc(eng, self.TICK[eng], 1)
        return None

    def opx(self, eng, fn, waits=()):
        for s, v in waits:
            self.wait(eng, s, v)
        self.prog[eng].append([fn, None])
        if eng in self.TICK:
            return self.inc(eng, self.TICK[eng], 1)
        return None

    def fence(self, engines=ENGINES):
        marks = dict(self.cnt)
        for e in engines:
            for s, v in marks.items():
                self.wait(e, s, v)

    def replay(self, block):
        def mk(eng):
            def body(e):
                for fn, w in self.prog[eng]:
                    if fn is not None:
                        fn(e)
                    else:
                        e.wait_ge(self.S[w[0]], w[1])
            return body
        for eng in ENGINES:
            getattr(block, eng)(mk(eng))


class Alloc:
    def __init__(self, arena, nbytes):
        self.arena = arena
        self.nbytes = nbytes
        self.off = 0
        self.peak = 0
        self.marks = []
        self.last_at = 0

    def view(self, at, elems, dt=F32):
        bs = 2 if dt == BF16 else 4
        assert at % bs == 0 and at + elems * bs <= self.nbytes
        if dt == F32:
            return self.arena[:, at // 4: at // 4 + elems]
        return self.arena[:, :].bitcast(dt)[:, at // bs: at // bs + elems]

    def alloc(self, elems, dt=F32):
        self.off = (self.off + 3) & ~3
        at = self.off
        self.off += elems * (2 if dt == BF16 else 4)
        self.peak = max(self.peak, self.off)
        assert self.off <= self.nbytes, f"arena overflow {self.off} > {self.nbytes}"
        self.last_at = at
        return self.view(at, elems, dt)

    def push(self):
        self.marks.append(self.off)

    def pop(self):
        self.off = self.marks.pop()


LAST_WAITED = {}


def build_nc(stop_after="out", taps=(), skip=(), want=None):
    nc = bass.Bass("TRN2", target_bir_lowering=False, debug=False)
    din = {}
    for name, shape in [
        ("x0T", [D, S]), ("x0T_own", [D, TOK]),
        ("wqkv_p", [128, 12 * 2048]), ("lrw_p", [128, 48]), ("lrb", [1, 3]),
        ("pvecs", [128, 28]),
        ("w0T", [128, 2048]), ("w2T", [128, 2048]), ("w1N", [128, 2048]),
        ("WoT_p", [128, 16 * 2048]),
        ("W1_p", [128, 64 * 2048]), ("W3_p", [128, 64 * 2048]), ("W2Tp", [8192, D]),
        ("wqkv1_p", [128, 24 * 2048]), ("wo1_p", [128, 16 * 2048]),
        ("wup_p", [128, 32 * 2048]), ("wdnTp", [4096, D]),
        ("cos0", [128, S]), ("sin0", [128, S]),
        ("cos1", [128, TOK]), ("sin1", [128, TOK]),
        ("PT", [128, 128]), ("ident", [128, 128]),
        ("maskM", [128, 1024]), ("maskW", [128, 1024]), ("maskWc", [128, 1024]),
        ("H128", [128, 128]), ("H64I2", [128, 128]), ("H32I4", [128, 128]),
        ("H16I8", [128, 128]), ("msel", [128, 4]), ("hsel", [128, 4]),
    ]:
        din[name] = nc.dram_tensor(name, shape, F32, kind="ExternalInput")

    y_out = nc.dram_tensor("y_out", [D, TOK], F32, kind="ExternalOutput")
    tap_t = {n: nc.dram_tensor("tap_" + n, list(sh), F32, kind="ExternalOutput")
             for n, sh in dict(taps).items()}

    wqkv_bfd = nc.dram_tensor("wqkv_bfd", [128, 12 * 2048], BF16)
    ag_mix_in = [nc.dram_tensor(f"ag_mix_in{c}", [512, CH], BF16) for c in range(NCH)]
    ag_mix_out = [nc.dram_tensor(f"ag_mix_out{c}", [D, CH], BF16) for c in range(NCH)]
    ar_in = [nc.dram_tensor(f"ar_in{c}", [1, 2048], F32) for c in range(NCH)]
    ar_out = [nc.dram_tensor(f"ar_out{c}", [1, 2048], F32) for c in range(NCH)]
    ag_kv_in = nc.dram_tensor("ag_kv_in", [1024, 512], BF16)
    ag_kv_out = nc.dram_tensor("ag_kv_out", [4096, 512], BF16)
    shufd = nc.dram_tensor("shufd", [128, 64 * 1024], BF16)
    RG = [[0, 1, 2, 3], [4, 5, 6, 7]]

    marks = {}
    es = ExitStack()
    C = Ctx(nc, serial=True, want=want)
    for s in ("T", "V", "S", "L", "G", "GA", "GB", "LS", "LE",
              "LX0", "LX1", "LW0", "LW1", "LM0", "LM1", "CCR", "CCA"):
        C.addsem(es, s)

    def sbt(name, shape, dt=F32):
        return es.enter_context(nc.sbuf_tensor(name, shape, dt))

    ARN = sbt("ARN", [128, ARN_ELEMS])
    AL = Alloc(ARN, ARN_ELEMS * 4)
    cpool = AL.alloc(len(CONST_VALS))
    for i, cv in enumerate(CONST_VALS):
        nc.const_aps.register(F32, float(cv), cpool[:, i:i + 1])
    pvecs = AL.alloc(28)
    ones_bf = AL.alloc(2, BF16)
    onesc_f = AL.alloc(1)
    onesr_f = AL.alloc(128)
    PT_bf = AL.alloc(128, BF16)
    id_bf = AL.alloc(128, BF16)
    H128_bf = AL.alloc(128, BF16)
    H64I2_bf = AL.alloc(128, BF16)
    H32I4_bf = AL.alloc(128, BF16)
    H16I8_bf = AL.alloc(128, BF16)
    maskM_bf = AL.alloc(1024, BF16)
    maskW_bf = AL.alloc(1024, BF16)
    maskWc_bf = AL.alloc(1024, BF16)
    rows = AL.alloc(2048)
    rows2 = AL.alloc(2048)
    lrb_bc = AL.alloc(3)
    lrw_bf = AL.alloc(48, BF16)
    lr_sb = AL.alloc(24)
    srow = AL.alloc(8)
    msel = AL.alloc(4)
    hsel = AL.alloc(4)
    mxacc = AL.alloc(8)
    pp = [es.enter_context(nc.psum_tensor(f"pb{i}", [128, 512], F32)) for i in range(8)]
    ppb = [pp[i][:, :].bitcast(BF16) for i in range(8)]

    def dma(qeng, out, in_, sem=None):
        if sem is None:
            sem = "G" if qeng == "gpsimd" else "L"
        C.op(qeng, lambda e, o=out, i=in_: e.dma_start(out=o, in_=i))
        return C.inc(qeng, sem, 16)

    def mm(out, lhsT, rhs, start=True, stop=True):
        return C.op("tensor", lambda e, o=out, l=lhsT, r=rhs, a=start, b=stop:
                    e.matmul(o, l, r, start=a, stop=b))

    def vop(fn):
        return C.op("vector", fn)

    def sop(fn):
        return C.op("scalar", fn)

    def mmx(out, lhsT, rhs, start=True, stop=True, waits=()):
        return C.opx("tensor", lambda e, o=out, l=lhsT, r=rhs, a=start, b=stop:
                     e.matmul(o, l, r, start=a, stop=b), waits)

    def vopx(fn, waits=()):
        return C.opx("vector", fn, waits)

    def sopx(fn, waits=()):
        return C.opx("scalar", fn, waits)

    def colsum(dst_ps, src_bf, start=True, stop=True):
        return mm(dst_ps, ones_bf[:, 0:1], src_bf, start, stop)

    def bcast(dst_ps, row_ap):
        return mm(dst_ps, onesr_f[:1, :], row_ap, True, True)

    def transpose(dst_ps, src_bf):
        return C.op("tensor", lambda e, o=dst_ps, i=src_bf:
                    e.transpose(o, i, id_bf[:, :]))

    def collective(kind, op_, in_t, out_t, sem="CCA"):
        C.op("gpsimd", lambda e, k=kind, o=op_, i=in_t, u=out_t: e.collective_compute(
            k, o, replica_groups=RG, ins=[i.ap().opt()], outs=[u.ap().opt()]))
        return C.inc("gpsimd", sem, 1)

    AL.push()
    ist = [AL.alloc(2048), AL.alloc(2048)]
    ist_bf = AL.alloc(2048, BF16)
    cast_v = {0: 0, 1: 0}
    st_g = {0: 0}

    def phase_init():
        for i, cv in enumerate(CONST_VALS):
            vop(lambda e, i=i, cv=float(cv): e.memset(cpool[:, i:i + 1], cv))
        vop(lambda e: e.memset(ones_bf[:, 0:1], 1.0))
        vop(lambda e: e.memset(onesc_f[:, :], 1.0))
        vop(lambda e: e.memset(onesr_f[:, :], 1.0))
        dma("sync", pvecs[:, :], din["pvecs"][:, :])
        dma("sync", msel[:, :], din["msel"][:, :])
        dma("sync", hsel[:, :], din["hsel"][:, :])
        lv = dma("sync", ist[0][:, 0:48], din["lrw_p"][:, :], "LX0")
        C.wait("vector", "LX0", lv)
        vv = vop(lambda e: e.tensor_copy(lrw_bf[:, :], ist[0][:, 0:48]))
        C.wait("sync", "V", vv)
        lv = dma("sync", rows[:1, 0:3], din["lrb"][:, :])
        C.wait("vector", "L", lv)
        vop(lambda e: e.tensor_scalar(rows[:1, 0:3], rows[:1, 0:3],
                                      BASE_LR_INV, None, ALU.add))
        bcast(pp[0][:, 0:3], rows[:1, 0:3])
        vop(lambda e: e.tensor_copy(lrb_bc[:, :], pp[0][:, 0:3]))
        for nm, dst, n in [("PT", PT_bf, 128), ("ident", id_bf, 128),
                           ("H128", H128_bf, 128), ("H64I2", H64I2_bf, 128),
                           ("H32I4", H32I4_bf, 128), ("H16I8", H16I8_bf, 128),
                           ("maskM", maskM_bf, 1024), ("maskW", maskW_bf, 1024),
                           ("maskWc", maskWc_bf, 1024)]:
            lv = dma("sync", ist[0][:, 0:n], din[nm][:, :], "LX0")
            C.wait("vector", "LX0", lv)
            vv = vop(lambda e, d=dst, s=ist[0][:, 0:n]: e.tensor_copy(d[:, :], s))
            C.wait("sync", "V", vv)
        for m_ in range(12):
            buf = ist[m_ % 2]
            if m_ >= 2:
                C.wait("sync", "V", cast_v[m_ % 2])
            lv = dma("sync", buf, din["wqkv_p"][:, m_ * 2048:(m_ + 1) * 2048],
                     "LX%d" % (m_ % 2))
            C.wait("vector", "LX%d" % (m_ % 2), lv)
            C.wait("vector", "G", 0 if m_ == 0 else st_g[0])
            vv = vop(lambda e, b=buf: e.tensor_copy(ist_bf[:, :], b))
            cast_v[m_ % 2] = vv
            C.wait("gpsimd", "V", vv)
            gv = dma("gpsimd", wqkv_bfd[:, m_ * 2048:(m_ + 1) * 2048], ist_bf[:, :])
            st_g[0] = gv
            C.wait("vector", "G", gv)
        C.fence()

    phase_init()

    def layer_norm(src_bf, sq_bf, mbc, rbc, scrA, dst_bf=None, out_fn=None):
        for fc in range(16):
            sop(lambda e, fc=fc: e.activation(
                sq_bf[:, 0:1024], src_bf[:, fc * 1024:(fc + 1) * 1024], AF.Square))
            for half in range(2):
                h0 = half * 512
                colsum(pp[6 + half][:1, :],
                       src_bf[:, fc * 1024 + h0:fc * 1024 + h0 + 512],
                       fc == 0, fc == 15)
                colsum(pp[4 + half][:1, :], sq_bf[:, h0:h0 + 512], fc == 0, fc == 15)
        vop(lambda e: e.tensor_copy(rows[:1, 0:512], pp[6][:1, :]))
        vop(lambda e: e.tensor_copy(rows[:1, 512:1024], pp[7][:1, :]))
        vop(lambda e: e.tensor_copy(rows[:1, 1024:1536], pp[4][:1, :]))
        vop(lambda e: e.tensor_copy(rows[:1, 1536:2048], pp[5][:1, :]))
        vop(lambda e: e.tensor_scalar(rows[:1, 0:2048], rows[:1, 0:2048],
                                      1.0 / D, None, ALU.mult))
        vop(lambda e: e.tensor_mul(rows2[:1, 0:1024], rows[:1, 0:1024],
                                   rows[:1, 0:1024]))
        vop(lambda e: e.tensor_sub(rows[:1, 1024:2048], rows[:1, 1024:2048],
                                   rows2[:1, 0:1024]))
        sop(lambda e: e.activation(rows[:1, 1024:2048], rows[:1, 1024:2048],
                                   AF.Ln, bias=1e-5, scale=1.0))
        sop(lambda e: e.activation(rows[:1, 1024:2048], rows[:1, 1024:2048],
                                   AF.Exp, bias=0.0, scale=-0.5))
        for half in range(2):
            h0 = half * 512
            bcast(pp[0][:, :], rows[:1, h0:h0 + 512])
            bcast(pp[1][:, :], rows[:1, 1024 + h0:1024 + h0 + 512])
            vop(lambda e, h0=h0: e.tensor_copy(mbc[:, h0:h0 + 512], pp[0][:, :]))
            vop(lambda e, h0=h0: e.tensor_copy(rbc[:, h0:h0 + 512], pp[1][:, :]))
        if dst_bf is None:
            dst_bf = src_bf
        for fc in range(16):
            vop(lambda e, fc=fc: e.tensor_sub(
                scrA[:, 0:1024], src_bf[:, fc * 1024:(fc + 1) * 1024], mbc[:, :]))
            if out_fn is not None:
                out_fn(fc, scrA, rbc)
            else:
                vop(lambda e, fc=fc: e.tensor_mul(
                    dst_bf[:, fc * 1024:(fc + 1) * 1024], scrA[:, 0:1024], rbc[:, :]))

    def dump(name, src_ap, rows_n, cols_n):
        if name not in tap_t:
            return
        C.fence()
        stg = AL.view(AL.nbytes - 8192, 2048, F32)
        for r in range(rows_n // 128):
            vv = vop(lambda e, r=r: e.tensor_copy(
                stg[:, 0:cols_n], src_ap[:, r * cols_n:(r + 1) * cols_n]))
            C.wait("gpsimd", "V", vv)
            gv = dma("gpsimd", tap_t[name][r * 128:(r + 1) * 128, :], stg[:, 0:cols_n])
            C.wait("vector", "G", gv)
        C.fence()

    AL.pop()
    AL.push()
    x0bf = AL.alloc(16 * 512, BF16)
    at_x0bf = AL.last_at
    wslab = [AL.alloc(2048, BF16), AL.alloc(2048, BF16)]
    x0f = [AL.alloc(512), AL.alloc(512)]
    q_f = AL.alloc(4096, BF16)
    k_f = AL.alloc(4096, BF16)
    fvf = AL.alloc(4096, BF16)
    fvl1 = AL.alloc(4096, BF16)
    fkt = AL.alloc(4096, BF16)
    akbuf = [AL.alloc(4096, BF16), AL.alloc(4096, BF16)]
    vtbuf = [AL.alloc(4096, BF16), AL.alloc(4096, BF16)]
    attnT = AL.alloc(4096, BF16)
    ctab = AL.alloc(1024)
    at_ctab = AL.last_at
    stab = AL.alloc(1024)
    rqbc = AL.alloc(1024)
    at_rqbc = AL.last_at
    rkbc = AL.alloc(1024)
    aqh = AL.alloc(1024, BF16)
    at_aqh = AL.last_at
    expt = [AL.alloc(512, BF16), AL.alloc(512, BF16)]
    st_w0 = AL.alloc(2048)
    st_w2 = AL.alloc(2048)
    st_w1 = AL.alloc(2048)
    st_bf = AL.alloc(3 * 2048, BF16)
    scr = AL.alloc(2048)
    scrB = AL.alloc(1024)
    scr_bf = AL.alloc(2048, BF16)
    sbw0, sbw2, sbw1 = st_bf[:, 0:2048], st_bf[:, 2048:4096], st_bf[:, 4096:6144]
    hid = AL.view(at_x0bf, 4096, BF16)
    dgate0 = AL.view(at_x0bf + 8192, 4096, BF16)
    dhpre2 = AL.view(at_x0bf + 16384, 4096, BF16)
    hq = AL.view(at_ctab, 4096, BF16)
    oisb = AL.view(at_rqbc, 4096, BF16)
    w1NT = AL.view(at_aqh, 2048, BF16)
    L0_PEAK = AL.off

    lv = dma("sync", st_w0, din["w0T"][:, :], "LS")
    lv = dma("sync", st_w2, din["w2T"][:, :], "LS")
    lv = dma("sync", st_w1, din["w1N"][:, :], "LS")
    C.wait("vector", "LS", lv)

    slab_t = {0: 0, 1: 0}
    cast_t = {0: 0, 1: 0}
    lv_tab = {0: 0}

    def l0_qkv(c):
        C.wait("sync", "T", C.cnt["T"])
        C.wait("sync", "V", C.cnt["V"])
        dma("sync", ctab, din["cos0"][:, c * CH:c * CH + 1024], "LE")
        lv_tab[0] = dma("sync", stab, din["sin0"][:, c * CH:c * CH + 1024], "LE")
        for half in range(2):
            tok0 = c * CH + half * 512
            for k in range(16):
                buf = x0f[k % 2]
                if k >= 2:
                    C.wait("sync", "V", cast_t[k % 2])
                lv = dma("sync", buf, din["x0T"][k * 128:(k + 1) * 128,
                                                 tok0:tok0 + 512],
                         "LX%d" % (k % 2))
                C.wait("vector", "LX%d" % (k % 2), lv)
                cast_t[k % 2] = vop(lambda e, k=k, b=buf: e.tensor_copy(
                    x0bf[:, k * 512:(k + 1) * 512], b))
            for ttl in range(4):
                tt = half * 4 + ttl
                for k in range(16):
                    mm(pp[5][:, 0:3],
                       x0bf[:, k * 512 + ttl * 128:k * 512 + ttl * 128 + 128],
                       lrw_bf[:, k * 3:(k + 1) * 3], k == 0, k == 15)
                vop(lambda e: e.tensor_add(scr[:, 0:3], pp[5][:, 0:3], lrb_bc[:, :]))
                sop(lambda e: e.activation(scr[:, 4:7], scr[:, 0:3], AF.Exp))
                sop(lambda e, tt=tt: e.activation(
                    lr_sb[:, tt * 3:tt * 3 + 3], scr[:, 4:7], AF.Ln,
                    bias=1.0, scale=1.0))
            for m_ in range(12):
                if m_ == 8:
                    vop(lambda e, half=half: e.tensor_copy(
                        rows[:1, half * 512:half * 512 + 512], pp[6][:1, :]))
                    vop(lambda e, half=half: e.tensor_copy(
                        rows[:1, 1024 + half * 512:1024 + half * 512 + 512],
                        pp[7][:1, :]))
                    if half == 1:
                        l0_stats_issue(c)
                wb = wslab[m_ % 2]
                if m_ >= 2 or half > 0 or c > 0:
                    C.wait("sync", "T", slab_t[m_ % 2])
                lv = dma("sync", wb, wqkv_bfd[:, m_ * 2048:(m_ + 1) * 2048],
                         "LW%d" % (m_ % 2))
                C.wait("tensor", "LW%d" % (m_ % 2), lv)
                bk = m_ % 2
                for k in range(16):
                    tv = mm(pp[bk][:, :], wb[:, k * 128:(k + 1) * 128],
                            x0bf[:, k * 512:(k + 1) * 512], k == 0, k == 15)
                slab_t[m_ % 2] = tv
                if m_ < 8:
                    fc = m_ % 4
                    dst = q_f if m_ < 4 else k_f
                    vop(lambda e, d=dst, fc=fc, half=half, bk=bk: e.tensor_copy(
                        d[:, fc * 1024 + half * 512:fc * 1024 + half * 512 + 512],
                        pp[bk][:, :]))
                    sop(lambda e, bk=bk: e.activation(
                        scr_bf[:, 0:512], pp[bk][:, :], AF.Square))
                    colsum(pp[6 if m_ < 4 else 7][:1, :], scr_bf[:, 0:512],
                           fc == 0, fc == 3)
                else:
                    oc = m_ - 8
                    sop(lambda e, bk=bk: e.activation(
                        scr[:, 0:512], pp[bk][:, :], AF.Sigmoid))
                    vop(lambda e, oc=oc, half=half, bk=bk: e.tensor_mul(
                        fvf[:, oc * 1024 + half * 512:oc * 1024 + half * 512 + 512],
                        scr[:, 0:512], pp[bk][:, :]))
                    vop(lambda e, bk=bk: e.tensor_copy(
                        scr_bf[:, 1024:1536], pp[bk][:, :]))
                    for sub in range(4):
                        tb = 3 if sub % 2 == 0 else 4
                        transpose(ppb[tb][:, 0:128],
                                  scr_bf[:, 1024 + sub * 128:1024 + sub * 128 + 128])
                        tt = half * 4 + sub
                        vop(lambda e, tb=tb, tt=tt, oc=oc, c=c: e.tensor_copy(
                            vtbuf[c % 2][:, tt * 512 + oc * 128:
                                         tt * 512 + oc * 128 + 128],
                            ppb[tb][:, 0:128]))

    def l0_stats_issue(c):
        C.fence(("gpsimd",))
        gv = dma("gpsimd", ar_in[c][:, :], rows[:1, 0:2048])
        C.wait("gpsimd", "G", gv)
        collective("AllReduce", ALU.add, ar_in[c], ar_out[c], "CCR")

    def l0_stats_ar(c):
        cv = C.cnt["CCR"]
        C.wait("gpsimd", "CCR", cv)
        gv = dma("gpsimd", rows[:1, 0:2048], ar_out[c][:, :])
        for e_ in ("scalar", "vector", "tensor"):
            C.wait(e_, "G", gv)
        sop(lambda e: e.activation(rows2[:1, 0:2048], rows[:1, 0:2048],
                                   AF.Ln, bias=1e-6, scale=1.0 / D))
        sop(lambda e: e.activation(rows2[:1, 0:2048], rows2[:1, 0:2048],
                                   AF.Exp, bias=0.0, scale=-0.5))
        for half in range(2):
            h0 = half * 512
            bcast(pp[0][:, :], rows2[:1, h0:h0 + 512])
            bcast(pp[1][:, :], rows2[:1, 1024 + h0:1024 + h0 + 512])
            vop(lambda e, h0=h0: e.tensor_copy(rqbc[:, h0:h0 + 512], pp[0][:, :]))
            vop(lambda e, h0=h0: e.tensor_copy(rkbc[:, h0:h0 + 512], pp[1][:, :]))

    def l0_rope_ak(c):
        C.wait("vector", "LE", lv_tab[0])
        for fc in range(4):
            vop(lambda e, fc=fc: e.tensor_scalar(
                scr_bf[:, 0:1024], k_f[:, fc * 1024:(fc + 1) * 1024],
                pvecs[:, 4 + fc:4 + fc + 1], None, ALU.mult))
            vop(lambda e: e.tensor_mul(scr_bf[:, 0:1024], scr_bf[:, 0:1024],
                                       rkbc[:, 0:1024]))
            for half in range(2):
                mm(pp[half][:, :], PT_bf[:, :],
                   scr_bf[:, half * 512:half * 512 + 512])
            vop(lambda e: e.tensor_mul(scr[:, 0:1024], scr_bf[:, 0:1024], ctab[:, :]))
            vop(lambda e: e.tensor_mul(scrB[:, 0:512], pp[0][:, :], stab[:, 0:512]))
            vop(lambda e: e.tensor_mul(scrB[:, 512:1024], pp[1][:, :],
                                       stab[:, 512:1024]))
            vop(lambda e, fc=fc, c=c: e.tensor_add(
                akbuf[c % 2][:, fc * 1024:(fc + 1) * 1024], scr[:, 0:1024],
                scrB[:, 0:1024]))

    def l0_attention(c):
        for h in range(4):
            vop(lambda e, h=h: e.tensor_scalar(
                scr_bf[:, 0:1024], q_f[:, h * 1024:(h + 1) * 1024],
                pvecs[:, 0 + h:0 + h + 1], None, ALU.mult))
            vop(lambda e: e.tensor_mul(scr_bf[:, 0:1024], scr_bf[:, 0:1024],
                                       rqbc[:, 0:1024]))
            for half in range(2):
                mm(pp[half][:, :], PT_bf[:, :],
                   scr_bf[:, half * 512:half * 512 + 512])
            vop(lambda e: e.tensor_mul(scr[:, 0:1024], scr_bf[:, 0:1024], ctab[:, :]))
            vop(lambda e: e.tensor_mul(scrB[:, 0:512], pp[0][:, :], stab[:, 0:512]))
            vop(lambda e: e.tensor_mul(scrB[:, 512:1024], pp[1][:, :],
                                       stab[:, 512:1024]))
            vop(lambda e: e.tensor_add(aqh[:, :], scr[:, 0:1024], scrB[:, 0:1024]))
            for qt in range(2):
                q0g = c * CH + qt * 512
                subs = []
                for k0 in range(q0g - 1024, q0g + 512, 128):
                    if k0 < 0:
                        continue
                    delta = k0 - q0g
                    if delta <= -640:
                        subs.append((k0, maskW_bf, (delta + 1024) // 128))
                    elif delta < 0:
                        subs.append((k0, None, 0))
                    else:
                        subs.append((k0, maskM_bf, delta // 128))
                nsub = len(subs)
                for i, (k0, mk, j) in enumerate(subs):
                    kc = (k0 // CH) % 2
                    ko = k0 % CH
                    bk = i % 2
                    mm(pp[bk][:, :],
                       akbuf[kc][:, h * 1024 + ko:h * 1024 + ko + 128],
                       aqh[:, qt * 512:qt * 512 + 512])
                    eb = expt[i % 2]
                    sop(lambda e, eb=eb, bk=bk: e.activation(
                        eb[:, :], pp[bk][:, :], AF.Exp, bias=-C_L0, scale=SC0))
                    if mk is not None:
                        vop(lambda e, eb=eb, mk=mk, j=j: e.tensor_mul(
                            eb[:, :], eb[:, :],
                            mk[:, 512 - 128 * j:1024 - 128 * j]))
                    colsum(pp[3][:1, :], eb[:, :], i == 0, i == nsub - 1)
                    mm(pp[2][:, :],
                       vtbuf[kc][:, (ko // 128) * 512 + h * 128:
                                  (ko // 128) * 512 + h * 128 + 128],
                       eb[:, :], i == 0, i == nsub - 1)
                vop(lambda e: e.reciprocal(rows[:1, 0:512], pp[3][:1, :]))
                vop(lambda e: e.tensor_copy(scr[:, 0:512], pp[2][:, :]))
                bcast(pp[4][:, :], rows[:1, 0:512])
                vop(lambda e, h=h, qt=qt: e.tensor_mul(
                    attnT[:, h * 1024 + qt * 512:h * 1024 + qt * 512 + 512],
                    scr[:, 0:512], pp[4][:, :]))

    def l0_derived(c):
        for which, src in [(0, q_f), (1, k_f)]:
            sc_col, of_col = (8, 12) if which == 0 else (16, 20)
            for fc in range(4):
                sop(lambda e, s=src, fc=fc: e.activation(
                    scr[:, 0:1024], s[:, fc * 1024:(fc + 1) * 1024], AF.Sigmoid))
                vop(lambda e, s=src, fc=fc: e.tensor_mul(
                    scr[:, 1024:2048], scr[:, 0:1024],
                    s[:, fc * 1024:(fc + 1) * 1024]))
                vop(lambda e, s=src, fc=fc, sc=sc_col, of=of_col: e.tensor_scalar(
                    s[:, fc * 1024:(fc + 1) * 1024], scr[:, 1024:2048],
                    pvecs[:, sc + fc:sc + fc + 1], pvecs[:, of + fc:of + fc + 1],
                    ALU.mult, ALU.add))
                sop(lambda e, s=src, fc=fc: e.activation(
                    scr_bf[:, 0:1024], s[:, fc * 1024:(fc + 1) * 1024], AF.Square))
                b0 = 2 + which
                colsum(pp[b0][:1, :], scr_bf[:, 0:512], fc == 0, fc == 3)
                colsum(pp[b0 + 2][:1, :], scr_bf[:, 512:1024], fc == 0, fc == 3)
        vop(lambda e: e.tensor_copy(rows[:1, 0:512], pp[2][:1, :]))
        vop(lambda e: e.tensor_copy(rows[:1, 512:1024], pp[4][:1, :]))
        vop(lambda e: e.tensor_copy(rows[:1, 1024:1536], pp[3][:1, :]))
        vop(lambda e: e.tensor_copy(rows[:1, 1536:2048], pp[5][:1, :]))
        sop(lambda e: e.activation(rows[:1, 0:2048], rows[:1, 0:2048],
                                   AF.Ln, bias=1e-24, scale=1.0))
        sop(lambda e: e.activation(rows[:1, 0:2048], rows[:1, 0:2048],
                                   AF.Exp, bias=0.0, scale=0.5))
        sop(lambda e: e.add(rows[:1, 0:2048], rows[:1, 0:2048], 1e-12))
        vop(lambda e: e.reciprocal(rows2[:1, 0:2048], rows[:1, 0:2048]))
        for which, src in [(0, q_f), (1, k_f)]:
            for half in range(2):
                bcast(pp[half][:, :],
                      rows2[:1, which * 1024 + half * 512:
                            which * 1024 + half * 512 + 512])
            for fc in range(4):
                for half in range(2):
                    o = fc * 1024 + half * 512
                    vop(lambda e, s=src, o=o, half=half: e.tensor_mul(
                        s[:, o:o + 512], s[:, o:o + 512], pp[half][:, :]))
        for dc in range(4):
            for tt in range(8):
                tb = 3 if tt % 2 == 0 else 4
                transpose(ppb[tb][:, 0:128],
                          k_f[:, dc * 1024 + tt * 128:dc * 1024 + tt * 128 + 128])
                vop(lambda e, tb=tb, tt=tt, dc=dc: e.tensor_copy(
                    fkt[:, tt * 512 + dc * 128:tt * 512 + dc * 128 + 128],
                    ppb[tb][:, 0:128]))
        for tt in range(8):
            sop(lambda e, tt=tt, c=c: e.activation(
                scr[:, 0:512], vtbuf[c % 2][:, tt * 512:(tt + 1) * 512], AF.Sigmoid))
            vop(lambda e, tt=tt, c=c: e.scalar_tensor_tensor(
                scrB[:, 0:512], scr[:, 0:512], lr_sb[:, tt * 3 + 1:tt * 3 + 2],
                vtbuf[c % 2][:, tt * 512:(tt + 1) * 512], ALU.mult, ALU.mult))
            vop(lambda e, tt=tt: e.tensor_copy(fvl1[:, tt * 512:(tt + 1) * 512],
                                               scrB[:, 0:512]))

    def l0_ttt(c):
        for dst, src in [(sbw0, st_w0), (sbw2, st_w2), (sbw1, st_w1)]:
            vop(lambda e, d=dst, s=src: e.tensor_copy(d, s))
        for tt in range(8):
            for bk, lhs_buf, rhs in [(0, k_f, sbw0), (1, k_f, sbw2), (2, fvf, sbw1)]:
                for dc in range(4):
                    mm(pp[bk][:, :],
                       lhs_buf[:, dc * 1024 + tt * 128:dc * 1024 + tt * 128 + 128],
                       rhs[:, dc * 512:(dc + 1) * 512], dc == 0, dc == 3)
            sop(lambda e: e.activation(scr[:, 0:512], pp[0][:, :], AF.Sigmoid))
            vop(lambda e: e.tensor_mul(scr[:, 512:1024], scr[:, 0:512], pp[0][:, :]))
            vop(lambda e: e.tensor_sub(scr[:, 1024:1536], pp[0][:, :],
                                       scr[:, 512:1024]))
            vop(lambda e: e.tensor_mul(scr[:, 1536:2048], scr[:, 0:512],
                                       scr[:, 1024:1536]))
            vop(lambda e: e.tensor_add(scr[:, 1536:2048], scr[:, 1536:2048],
                                       scr[:, 0:512]))
            vop(lambda e: e.tensor_copy(scrB[:, 0:512], pp[1][:, :]))
            vop(lambda e: e.tensor_mul(scrB[:, 512:1024], pp[2][:, :], scrB[:, 0:512]))
            vop(lambda e, tt=tt: e.tensor_mul(hid[:, tt * 512:(tt + 1) * 512],
                                              scr[:, 512:1024], scrB[:, 0:512]))
            vop(lambda e, tt=tt: e.scalar_tensor_tensor(
                dgate0[:, tt * 512:(tt + 1) * 512], scrB[:, 512:1024],
                lr_sb[:, tt * 3:tt * 3 + 1], scr[:, 1536:2048], ALU.mult, ALU.mult))
            vop(lambda e, tt=tt: e.scalar_tensor_tensor(
                dhpre2[:, tt * 512:(tt + 1) * 512], pp[2][:, :],
                lr_sb[:, tt * 3 + 2:tt * 3 + 3], scr[:, 512:1024],
                ALU.mult, ALU.mult))
        for st_f, lhs_buf, rhs_buf in [(st_w0, fkt, dgate0), (st_w2, fkt, dhpre2),
                                       (st_w1, fvl1, hid)]:
            for mc in range(4):
                bk = mc % 2
                for tt in range(8):
                    mm(pp[bk][:, :],
                       lhs_buf[:, tt * 512 + mc * 128:tt * 512 + mc * 128 + 128],
                       rhs_buf[:, tt * 512:(tt + 1) * 512], tt == 0, tt == 7)
                vop(lambda e, s=st_f, mc=mc, bk=bk: e.tensor_add(
                    s[:, mc * 512:(mc + 1) * 512], s[:, mc * 512:(mc + 1) * 512],
                    pp[bk][:, :]))
        for dst, src in [(sbw0, st_w0), (sbw2, st_w2), (sbw1, st_w1)]:
            vop(lambda e, d=dst, s=src: e.tensor_copy(d, s))
        for ocn in range(4):
            for hcn in range(4):
                tb = 3 if (ocn * 4 + hcn) % 2 == 0 else 4
                transpose(ppb[tb][:, 0:128],
                          sbw1[:, ocn * 512 + hcn * 128:ocn * 512 + hcn * 128 + 128])
                vop(lambda e, tb=tb, ocn=ocn, hcn=hcn: e.tensor_copy(
                    w1NT[:, hcn * 512 + ocn * 128:hcn * 512 + ocn * 128 + 128],
                    ppb[tb][:, 0:128]))
        for mh in range(4):
            for nh in range(2):
                for dc in range(4):
                    mm(pp[0][:, :],
                       sbw0[:, dc * 512 + mh * 128:dc * 512 + mh * 128 + 128],
                       q_f[:, dc * 1024 + nh * 512:dc * 1024 + nh * 512 + 512],
                       dc == 0, dc == 3)
                    mm(pp[1][:, :],
                       sbw2[:, dc * 512 + mh * 128:dc * 512 + mh * 128 + 128],
                       q_f[:, dc * 1024 + nh * 512:dc * 1024 + nh * 512 + 512],
                       dc == 0, dc == 3)
                sop(lambda e: e.activation(scr[:, 0:512], pp[0][:, :], AF.Sigmoid))
                vop(lambda e: e.tensor_mul(scr[:, 512:1024], scr[:, 0:512],
                                           pp[0][:, :]))
                vop(lambda e, mh=mh, nh=nh: e.tensor_mul(
                    hq[:, mh * 1024 + nh * 512:mh * 1024 + nh * 512 + 512],
                    scr[:, 512:1024], pp[1][:, :]))
        for nh in range(2):
            for oc in range(4):
                bk = oc % 2
                for hc in range(4):
                    mm(pp[bk][:, :],
                       w1NT[:, hc * 512 + oc * 128:hc * 512 + oc * 128 + 128],
                       hq[:, hc * 1024 + nh * 512:hc * 1024 + nh * 512 + 512],
                       hc == 0, hc == 3)
                sop(lambda e, bk=bk: e.activation(scr_bf[:, 0:512], pp[bk][:, :],
                                                  AF.Square))
                vop(lambda e, oc=oc, nh=nh, bk=bk: e.tensor_copy(
                    oisb[:, oc * 1024 + nh * 512:oc * 1024 + nh * 512 + 512],
                    pp[bk][:, :]))
                colsum(pp[5][:1, :], scr_bf[:, 0:512], oc == 0, oc == 3)
            sop(lambda e: e.activation(rows[:1, 0:512], pp[5][:1, :],
                                       AF.Ln, bias=1e-6, scale=1.0 / 512.0))
            sop(lambda e: e.activation(rows[:1, 0:512], rows[:1, 0:512],
                                       AF.Exp, bias=0.0, scale=-0.5))
            bcast(pp[6][:, :], rows[:1, 0:512])
            for oc in range(4):
                vop(lambda e, oc=oc, nh=nh: e.scalar_tensor_tensor(
                    scr[:, 0:512],
                    oisb[:, oc * 1024 + nh * 512:oc * 1024 + nh * 512 + 512],
                    pvecs[:, 24 + oc:24 + oc + 1], pp[6][:, :], ALU.mult, ALU.mult))
                vv = vop(lambda e, oc=oc, nh=nh: e.tensor_add(
                    scr_bf[:, 1024:1536], scr[:, 0:512],
                    attnT[:, oc * 1024 + nh * 512:oc * 1024 + nh * 512 + 512]))
                C.wait("gpsimd", "V", vv)
                gv = dma("gpsimd", ag_mix_in[c][oc * 128:(oc + 1) * 128,
                                                nh * 512:nh * 512 + 512],
                         scr_bf[:, 1024:1536])
                C.wait("vector", "G", gv)
        C.wait("gpsimd", "G", C.cnt["G"])
        return collective("AllGather", ALU.bypass, ag_mix_in[c], ag_mix_out[c])

    cv_ags = [0, 0, 0, 0]
    for c in range(NCH):
        l0_qkv(c)
        if stop_after == f"l0q{c}":
            break
        if "stats" not in skip:
            l0_stats_ar(c)
        if "rope" not in skip:
            l0_rope_ak(c)
        if "attn" not in skip:
            l0_attention(c)
        if "derived" not in skip:
            l0_derived(c)
        if "ttt" not in skip:
            cv_ags[c] = l0_ttt(c)
        if stop_after == f"l0c{c}":
            break

    def finish():
        C.fence()
        gv = dma("gpsimd", y_out[0:128, :], ARN[:, 0:1024])
        C.wait("gpsimd", "G", gv)
        C.replay(es.enter_context(nc.Block()))
        es.close()
        return nc

    if stop_after.startswith("l0"):
        dump("x0bf", x0bf, 2048, 512)
        dump("qf", q_f, 512, 1024)
        dump("kf", k_f, 512, 1024)
        dump("fvf", fvf, 512, 1024)
        dump("attnT", attnT, 512, 1024)
        dump("ak", akbuf[(NCH - 1) % 2] if stop_after == "l0c3" else akbuf[0],
             512, 1024)
        dump("lr", lr_sb, 128, 24)
        dump("oisb", oisb, 512, 1024)
        dump("stw0", st_w0, 128, 2048)
        return finish()

    AL.pop()
    AL.push()
    slotA = AL.alloc(16 * 1024, BF16)
    slotB = AL.alloc(16 * 1024, BF16)
    slotB_at = AL.last_at
    AL.push()
    mixT = AL.alloc(16 * 1024, BF16)
    mstage = [AL.alloc(1024, BF16), AL.alloc(1024, BF16)]
    wstage = [AL.alloc(2048), AL.alloc(2048)]
    w_bf2 = [AL.alloc(2048, BF16), AL.alloc(2048, BF16)]
    x0own = [AL.alloc(1024), AL.alloc(1024)]
    sq_bf = AL.alloc(1024, BF16)
    mbc = AL.alloc(1024)
    rbc = AL.alloc(1024)
    scrA = AL.alloc(1024)
    W_PEAK = AL.off

    def phase_w():
        C.fence(("sync",))
        msel_t = {0: 0, 1: 0}
        for r in range(4):
            C.wait("sync", "CCA", cv_ags[r])
            for k in range(16):
                sb2 = mstage[k % 2]
                if r > 0 or k >= 2:
                    C.wait("sync", "V", msel_t[k % 2])
                lv = dma("sync", sb2, ag_mix_out[r][k * 128:(k + 1) * 128, :],
                         "LM%d" % (k % 2))
                if r == 0:
                    msel_t[k % 2] = vopx(lambda e, k=k, s=sb2: e.tensor_scalar(
                        mixT[:, k * 1024:(k + 1) * 1024], s, msel[:, 0:1],
                        None, ALU.mult), [("LM%d" % (k % 2), lv)])
                else:
                    msel_t[k % 2] = vopx(lambda e, k=k, r=r, s=sb2:
                                         e.scalar_tensor_tensor(
                        mixT[:, k * 1024:(k + 1) * 1024], s, msel[:, r:r + 1],
                        mixT[:, k * 1024:(k + 1) * 1024], ALU.mult, ALU.add),
                        [("LM%d" % (k % 2), lv)])
        C.wait("sync", "T", C.cnt["T"])
        cast_w, mmg, addm = {}, {}, {}
        addt = {0: 0, 1: 0}
        for m_ in range(16):
            e_ = m_ % 2
            sl = wstage[e_]
            if m_ >= 2:
                C.wait("sync", "V", cast_w[m_ - 2])
                C.wait("sync", "V", addm[(m_ - 2, 1)])
            dma("sync", sl, din["WoT_p"][:, m_ * 2048:(m_ + 1) * 2048],
                "LW%d" % e_)
            lv = dma("sync", x0own[e_], din["x0T_own"][m_ * 128:(m_ + 1) * 128, :],
                     "LW%d" % e_)
            cwt = [("LW%d" % e_, lv)]
            if m_ >= 2:
                cwt.append(("T", mmg[(m_ - 2, 1)]))
            cast_w[m_] = vopx(lambda e, sl=sl, w=w_bf2[e_]: e.tensor_copy(w, sl),
                              cwt)
            for nh in range(2):
                wt0 = [("V", cast_w[m_])]
                if m_ >= 1:
                    wt0.append(("V", addt[nh]))
                else:
                    wt0.append(("V", max(msel_t[0], msel_t[1])))
                for k in range(16):
                    mmg[(m_, nh)] = mmx(
                        pp[nh][:, :], w_bf2[e_][:, k * 128:(k + 1) * 128],
                        mixT[:, k * 1024 + nh * 512:k * 1024 + nh * 512 + 512],
                        k == 0, k == 15, wt0 if k == 0 else ())
            for nh in range(2):
                addt[nh] = vopx(lambda e, m_=m_, nh=nh, e_=e_: e.tensor_add(
                    slotA[:, m_ * 1024 + nh * 512:m_ * 1024 + nh * 512 + 512],
                    pp[nh][:, :], x0own[e_][:, nh * 512:nh * 512 + 512]),
                    [("T", mmg[(m_, nh)])])
                addm[(m_, nh)] = addt[nh]
        layer_norm(slotA, sq_bf, mbc, rbc, scrA)

    phase_w()
    if stop_after == "xA":
        dump("xA", slotA, 2048, 1024)
        return finish()

    AL.pop()
    AL.push()
    stage4 = [AL.alloc(2048), AL.alloc(2048), AL.alloc(2048), AL.alloc(2048)]
    w13 = AL.alloc(4 * 2048, BF16)
    w2sl = AL.alloc(4 * 2048, BF16)
    hgrp = AL.alloc(4 * 1024, BF16)
    f2t = [AL.alloc(1024, BF16), AL.alloc(1024, BF16)]
    sq_bf0 = AL.alloc(1024, BF16)
    mbc0 = AL.alloc(1024)
    rbc0 = AL.alloc(1024)
    scr0 = AL.alloc(2048)
    scr0bf = AL.alloc(2048, BF16)
    FFN0_PEAK = AL.off

    def phase_ffn0():
        C.fence(("sync",))
        vop(lambda e: e.memset(slotB[:, :], 0.0))
        C.fence()
        cast2, mmg, sig, evac, h128t, cpy = {}, {}, {}, {}, {}, {}
        gv_prev = {0: 0, 1: 0}
        for m_ in range(64):
            e_ = m_ % 2
            s1, s3 = stage4[e_ * 2], stage4[e_ * 2 + 1]
            wA = w13[:, e_ * 4096:e_ * 4096 + 2048]
            wB = w13[:, e_ * 4096 + 2048:e_ * 4096 + 4096]
            if m_ >= 2:
                C.wait("sync", "V", cast2[m_ - 2])
            dma("sync", s1, din["W1_p"][:, m_ * 2048:(m_ + 1) * 2048],
                "LW%d" % e_)
            lv = dma("sync", s3, din["W3_p"][:, m_ * 2048:(m_ + 1) * 2048],
                     "LW%d" % e_)
            wts = [("LW%d" % e_, lv)]
            if m_ >= 2:
                wts.append(("T", mmg[(m_ - 2, 1)]))
            vopx(lambda e, s=s1, w=wA: e.tensor_copy(w, s), wts)
            cast2[m_] = vopx(lambda e, s=s3, w=wB: e.tensor_copy(w, s))
            for nh in range(2):
                b0, b1, bh = nh * 2, nh * 2 + 1, 4 + nh
                o5 = nh * 1024
                wt0 = [("V", cast2[m_])]
                if m_ >= 1:
                    wt0 += [("V", evac[(m_ - 1, nh)]), ("S", sig[(m_ - 1, nh)])]
                for k in range(16):
                    mmx(pp[b0][:, :], wA[:, k * 128:(k + 1) * 128],
                        slotA[:, k * 1024 + nh * 512:k * 1024 + nh * 512 + 512],
                        k == 0, k == 15, wt0 if k == 0 else ())
                for k in range(16):
                    mmg[(m_, nh)] = mmx(
                        pp[b1][:, :], wB[:, k * 128:(k + 1) * 128],
                        slotA[:, k * 1024 + nh * 512:k * 1024 + nh * 512 + 512],
                        k == 0, k == 15)
                swt = [("T", mmg[(m_, nh)])]
                if m_ >= 1:
                    swt.append(("V", evac[(m_ - 1, nh)]))
                sig[(m_, nh)] = sopx(lambda e, b=b0, o5=o5: e.activation(
                    scr0[:, o5:o5 + 512], pp[b][:, :], AF.Sigmoid), swt)
                vopx(lambda e, b=b0, o5=o5: e.tensor_mul(
                    scr0[:, o5 + 512:o5 + 1024], scr0[:, o5:o5 + 512],
                    pp[b][:, :]), [("T", mmg[(m_, nh)]), ("S", sig[(m_, nh)])])
                evac[(m_, nh)] = vopx(lambda e, b=b1, o5=o5: e.tensor_mul(
                    scr0bf[:, o5:o5 + 512], scr0[:, o5 + 512:o5 + 1024],
                    pp[b][:, :]))
                hwt = [("V", evac[(m_, nh)])]
                if m_ >= 1:
                    hwt.append(("V", cpy[(m_ - 1, nh)]))
                h128t[(m_, nh)] = mmx(pp[bh][:, :], H128_bf[:, :],
                                      scr0bf[:, o5:o5 + 512], True, True, hwt)
                gs = "GA" if nh == 0 else "GB"
                cpy[(m_, nh)] = vopx(lambda e, b=bh, o5=o5: e.tensor_copy(
                    scr0bf[:, o5 + 512:o5 + 1024], pp[b][:, :]),
                    [("T", h128t[(m_, nh)]), (gs, gv_prev[nh])])
                C.wait("gpsimd", "V", cpy[(m_, nh)])
                for l in range(2):
                    def g_sh(e, m_=m_, l=l, nh=nh, o5=o5):
                        dstap = shufd[m_ * 2 + l:m_ * 2 + l + 1,
                                      0:64 * 1024].rearrange(
                            "p (u hh t) -> (p u) hh t", hh=2, t=512)
                        return e.dma_start(out=dstap[:, nh, :],
                                           in_=scr0bf[l:128:2, o5 + 512:o5 + 1024])
                    C.op("gpsimd", g_sh)
                    gv = C.inc("gpsimd", gs, 16)
                gv_prev[nh] = gv
        C.fence()
        w2c, hgt, h64 = {}, {}, {}
        odlast = {}
        oduse = {0: 0, 1: 0, 2: 0, 3: 0}
        f2use = {0: 0, 1: 0}
        for grp in range(16):
            for j in range(4):
                m2 = grp * 4 + j
                jj = j % 2
                fb = f2t[jj]
                prev = (grp, j - 2) if j >= 2 else (grp - 1, j + 2)
                C.wait("sync", "T", f2use[jj])
                C.wait("sync", "V", w2c.get(prev, 0))
                lv = dma("sync", fb, shufd[:, m2 * 1024:(m2 + 1) * 1024],
                         "LX%d" % jj)
                lv2 = dma("sync", stage4[2 + jj],
                          din["W2Tp"][m2 * 128:(m2 + 1) * 128, :],
                          "LW%d" % jj)
                cwt = [("LW%d" % jj, lv2)]
                if grp > 0:
                    cwt.append(("T", odlast[grp - 1]))
                w2c[(grp, j)] = vopx(lambda e, j=j, s=stage4[2 + jj]:
                                     e.tensor_copy(
                    w2sl[:, j * 2048:(j + 1) * 2048], s), cwt)
                t0 = mmx(pp[6 + jj][:, :], H64I2_bf[:, :], fb[:, 0:512],
                         True, True,
                         [("LX%d" % jj, lv), ("V", hgt.get(prev, 0))])
                hwt = [("T", t0)]
                if grp > 0:
                    hwt.append(("T", odlast[grp - 1]))
                g0 = vopx(lambda e, j=j, jj=jj: e.tensor_scalar(
                    hgrp[:, j * 1024:j * 1024 + 512],
                    pp[6 + jj][:, :], R8192, None, ALU.mult), hwt)
                t1 = mmx(pp[6 + jj][:, :], H64I2_bf[:, :], fb[:, 512:1024],
                         True, True, [("V", g0)])
                hgt[(grp, j)] = vopx(lambda e, j=j, jj=jj: e.tensor_scalar(
                    hgrp[:, j * 1024 + 512:j * 1024 + 1024],
                    pp[6 + jj][:, :], R8192, None, ALU.mult), [("T", t1)])
                f2use[jj] = t1
            for od in range(16):
                for nh in range(2):
                    bk = (od * 2 + nh) % 4
                    gwt = [("V", oduse[bk])]
                    if od == 0 and nh == 0:
                        gwt += [("V", w2c[(grp, 3)]), ("V", hgt[(grp, 3)])]
                    for j in range(4):
                        tvx = mmx(
                            pp[bk][:, :],
                            w2sl[:, j * 2048 + od * 128:j * 2048 + od * 128 + 128],
                            hgrp[:, j * 1024 + nh * 512:j * 1024 + nh * 512 + 512],
                            j == 0, j == 3, gwt if j == 0 else ())
                    odlast[grp] = tvx
                    oduse[bk] = vopx(lambda e, od=od, nh=nh, bk=bk: e.tensor_add(
                        slotB[:, od * 1024 + nh * 512:od * 1024 + nh * 512 + 512],
                        slotB[:, od * 1024 + nh * 512:od * 1024 + nh * 512 + 512],
                        pp[bk][:, :]), [("T", tvx)])
        C.fence()
        for fc in range(16):
            vop(lambda e, fc=fc: e.tensor_add(
                slotB[:, fc * 1024:(fc + 1) * 1024],
                slotB[:, fc * 1024:(fc + 1) * 1024],
                slotA[:, fc * 1024:(fc + 1) * 1024]))
        layer_norm(slotB, sq_bf0, mbc0, rbc0, scr0[:, 0:1024])

    phase_ffn0()
    if stop_after == "xB":
        dump("xB", slotB, 2048, 1024)
        return finish()

    AL.pop()
    AL.push()
    q1 = AL.alloc(8 * 1024, BF16)
    k1 = AL.alloc(4 * 1536, BF16)
    v1t = AL.alloc(4 * 1536, BF16)
    wstage1 = [AL.alloc(2048), AL.alloc(2048)]
    w1bf2 = [AL.alloc(2048, BF16), AL.alloc(2048, BF16)]
    c1tab = AL.alloc(1024)
    s1tab = AL.alloc(1024)
    s1tab_at = AL.last_at
    kvstage = [AL.alloc(512, BF16), AL.alloc(512, BF16)]
    scr1 = AL.alloc(2048)
    scr1b = AL.alloc(1024)
    scr1bf = AL.alloc(1536, BF16)
    expt1 = [AL.alloc(512, BF16), AL.alloc(512, BF16)]
    sq_bf1 = AL.alloc(1024, BF16)
    mbc1 = AL.alloc(1024)
    rbc1 = AL.alloc(1024)
    L1A_PEAK = AL.off

    def phase_l1a():
        C.fence(("sync",))
        dma("sync", c1tab, din["cos1"][:, :], "LE")
        lv = dma("sync", s1tab, din["sin1"][:, :], "LE")
        C.wait("vector", "LE", lv)
        C.wait("tensor", "LE", lv)
        qst = {"hist": [], "cast": {}, "mmg": {}, "evac": {}, "ptv": 0,
               "sbfT": 0, "tpv": {3: 0, 4: 0}}

        def slab_load(m_, dname):
            h = qst["hist"]
            uid = len(h)
            p2 = h[-2] if uid >= 2 else None
            p1 = h[-1] if uid >= 1 else None
            h.append(uid)
            e_ = uid % 2
            sl = wstage1[e_]
            if p2 is not None:
                C.wait("sync", "V", qst["cast"][p2])
            lv = dma("sync", sl, din[dname][:, m_ * 2048:(m_ + 1) * 2048],
                     "LW%d" % e_)
            cwt = [("LW%d" % e_, lv)]
            if p2 is not None:
                cwt.append(("T", qst["mmg"][p2]))
            qst["cast"][uid] = vopx(lambda e, s=sl, w=w1bf2[e_]:
                                    e.tensor_copy(w, s), cwt)
            return uid, e_, p1

        def qkv1_slab(m_):
            uid, e_, p1 = slab_load(m_, "wqkv1_p")
            for nh in range(2):
                wt0 = [("V", qst["cast"][uid])]
                if p1 is not None:
                    wt0.append(("V", qst["evac"][p1]))
                for k in range(16):
                    qst["mmg"][uid] = mmx(
                        pp[nh][:, :], w1bf2[e_][:, k * 128:(k + 1) * 128],
                        slotB[:, k * 1024 + nh * 512:k * 1024 + nh * 512 + 512],
                        k == 0, k == 15, wt0 if k == 0 else ())
            return uid

        def rope_post(uid, dst_ap):
            cwt = [("T", qst["mmg"][uid]), ("T", qst["sbfT"])]
            vopx(lambda e: e.tensor_copy(scr1bf[:, 0:512], pp[0][:, :]), cwt)
            ev = vopx(lambda e: e.tensor_copy(scr1bf[:, 512:1024], pp[1][:, :]))
            qst["evac"][uid] = ev
            ptl = 0
            for half in range(2):
                ptl = mmx(pp[2 + half][:, :], PT_bf[:, :],
                          scr1bf[:, half * 512:half * 512 + 512], True, True,
                          [("V", ev), ("V", qst["ptv"])])
            qst["sbfT"] = ptl
            vopx(lambda e: e.tensor_mul(scr1[:, 0:1024], scr1bf[:, 0:1024],
                                        c1tab[:, :]))
            vopx(lambda e: e.tensor_mul(scr1b[:, 0:512], pp[2][:, :],
                                        s1tab[:, 0:512]), [("T", ptl)])
            vopx(lambda e: e.tensor_mul(scr1b[:, 512:1024], pp[3][:, :],
                                        s1tab[:, 512:1024]))
            qst["ptv"] = vopx(lambda e, d=dst_ap: e.tensor_add(
                d, scr1[:, 0:1024], scr1b[:, 0:1024]))

        C.fence()
        for m_ in range(16, 24):
            uid = qkv1_slab(m_)
            kvh = (m_ - 16) % 4
            if m_ < 20:
                rope_post(uid, k1[:, kvh * 1536 + 512:kvh * 1536 + 1536])
            else:
                cwt = [("T", qst["mmg"][uid]), ("T", qst["sbfT"])]
                vopx(lambda e: e.tensor_copy(scr1bf[:, 0:512], pp[0][:, :]), cwt)
                ev = vopx(lambda e: e.tensor_copy(scr1bf[:, 512:1024],
                                                  pp[1][:, :]))
                qst["evac"][uid] = ev
                tpt = 0
                for sub in range(8):
                    tb = 3 if sub % 2 == 0 else 4
                    tpt = C.opx("tensor", lambda e, tb=tb, sub=sub: e.transpose(
                        ppb[tb][:, 0:128],
                        scr1bf[:, sub * 128:sub * 128 + 128], id_bf[:, :]),
                        [("V", ev), ("V", qst["tpv"][tb])])
                    qst["tpv"][tb] = vopx(lambda e, tb=tb, kvh=kvh, sub=sub:
                                          e.tensor_copy(
                        v1t[:, kvh * 1536 + 512 + sub * 128:
                            kvh * 1536 + 512 + sub * 128 + 128],
                        ppb[tb][:, 0:128]), [("T", tpt)])
                qst["sbfT"] = tpt
        C.fence(("gpsimd",))
        for kvh in range(4):
            dma("gpsimd", ag_kv_in[kvh * 128:(kvh + 1) * 128, :],
                k1[:, kvh * 1536 + 1024:kvh * 1536 + 1536])
            dma("gpsimd", ag_kv_in[(4 + kvh) * 128:(5 + kvh) * 128, :],
                v1t[:, kvh * 1536 + 1024:kvh * 1536 + 1536])
        C.wait("gpsimd", "G", C.cnt["G"])
        cv_kv = collective("AllGather", ALU.bypass, ag_kv_in, ag_kv_out, "CCA")

        def q_group(g2):
            for hl in range(8):
                uid = qkv1_slab(g2 * 8 + hl)
                rope_post(uid, q1[:, hl * 1024:(hl + 1) * 1024])

        def halo_select():
            C.wait("sync", "CCA", cv_kv)
            ks_t = {0: 0, 1: 0}
            for r in range(4):
                for ch in range(8):
                    stg = kvstage[ch % 2]
                    if r > 0 or ch >= 2:
                        C.wait("sync", "V", ks_t[ch % 2])
                    lv = dma("sync", stg,
                             ag_kv_out[r * 1024 + ch * 128:
                                       r * 1024 + (ch + 1) * 128, :],
                             "LX%d" % (ch % 2))
                    C.wait("vector", "LX%d" % (ch % 2), lv)
                    if ch < 4:
                        dst = k1[:, ch * 1536:ch * 1536 + 512]
                    else:
                        dst = v1t[:, (ch - 4) * 1536:(ch - 4) * 1536 + 512]
                    if r == 0:
                        ks_t[ch % 2] = vop(lambda e, d=dst, s=stg: e.tensor_scalar(
                            d, s, hsel[:, 0:1], None, ALU.mult))
                    else:
                        ks_t[ch % 2] = vop(
                            lambda e, d=dst, s=stg, r=r: e.scalar_tensor_tensor(
                                d, s, hsel[:, r:r + 1], d, ALU.mult, ALU.add))

        def l1_attention(g2):
            for hl in range(8):
                h = g2 * 8 + hl
                kvh = h // 4
                for qt in range(2):
                    q0l = qt * 512
                    subs = []
                    for k0 in range(q0l - 512, q0l + 512, 128):
                        delta = k0 - q0l
                        if delta < 0:
                            mk = maskWc_bf if k0 < 0 else maskW_bf
                            subs.append((k0, mk, (delta + 512) // 128))
                        else:
                            subs.append((k0, maskM_bf, delta // 128))
                    nsub = len(subs)
                    for i, (k0, mk, j) in enumerate(subs):
                        kb = k0 + 512
                        bk = i % 2
                        mm(pp[bk][:, :],
                           k1[:, kvh * 1536 + kb:kvh * 1536 + kb + 128],
                           q1[:, hl * 1024 + q0l:hl * 1024 + q0l + 512])
                        eb = expt1[i % 2]
                        sop(lambda e, eb=eb, bk=bk: e.activation(
                            eb[:, :], pp[bk][:, :], AF.Exp, bias=-C_L1, scale=SC0))
                        vop(lambda e, eb=eb, mk=mk, j=j: e.tensor_mul(
                            eb[:, :], eb[:, :], mk[:, 512 - 128 * j:1024 - 128 * j]))
                        colsum(pp[3][:1, :], eb[:, :], i == 0, i == nsub - 1)
                        mm(pp[2][:, :],
                           v1t[:, kvh * 1536 + (kb // 128) * 128:
                               kvh * 1536 + (kb // 128) * 128 + 128],
                           eb[:, :], i == 0, i == nsub - 1)
                    vop(lambda e: e.reciprocal(rows[:1, 0:512], pp[3][:1, :]))
                    vop(lambda e: e.tensor_copy(scr1[:, 0:512], pp[2][:, :]))
                    bcast(pp[4][:, :], rows[:1, 0:512])
                    vop(lambda e, hl=hl, q0l=q0l: e.tensor_mul(
                        q1[:, hl * 1024 + q0l:hl * 1024 + q0l + 512],
                        scr1[:, 0:512], pp[4][:, :]))

        def wo1_partial(g2):
            addw = {}
            for m_ in range(16):
                uid, e_, p1 = slab_load(m_, "wo1_p")
                for nh in range(2):
                    wt0 = [("V", qst["cast"][uid]), ("V", C.cnt["V"])]
                    if p1 is not None:
                        wt0.append(("V", qst["evac"][p1]))
                    for kl in range(8):
                        k = g2 * 8 + kl
                        qst["mmg"][uid] = mmx(
                            pp[nh][:, :], w1bf2[e_][:, k * 128:(k + 1) * 128],
                            q1[:, kl * 1024 + nh * 512:kl * 1024 + nh * 512 + 512],
                            kl == 0, kl == 7, wt0 if kl == 0 else ())
                for nh in range(2):
                    if g2 == 0:
                        addw[nh] = vopx(lambda e, m_=m_, nh=nh: e.tensor_add(
                            slotA[:, m_ * 1024 + nh * 512:
                                  m_ * 1024 + nh * 512 + 512],
                            pp[nh][:, :],
                            slotB[:, m_ * 1024 + nh * 512:
                                  m_ * 1024 + nh * 512 + 512]),
                            [("T", qst["mmg"][uid])])
                    else:
                        addw[nh] = vopx(lambda e, m_=m_, nh=nh: e.tensor_add(
                            slotA[:, m_ * 1024 + nh * 512:
                                  m_ * 1024 + nh * 512 + 512],
                            slotA[:, m_ * 1024 + nh * 512:
                                  m_ * 1024 + nh * 512 + 512],
                            pp[nh][:, :]), [("T", qst["mmg"][uid])])
                qst["evac"][uid] = addw[1]

        q_group(0)
        halo_select()
        l1_attention(0)
        wo1_partial(0)
        q_group(1)
        marks["wm"] = dict(C.cnt)
        l1_attention(1)
        wo1_partial(1)
        layer_norm(slotA, sq_bf1, mbc1, rbc1, scr1[:, 0:1024])

    phase_l1a()
    if stop_after == "xC":
        dump("xC", slotA, 2048, 1024)
        return finish()

    AL.pop()
    AL.push()
    xq = AL.alloc(16 * 1024, BF16)
    stageF = [AL.alloc(2048), AL.alloc(2048)]
    wslbf = AL.alloc(4 * 2048, BF16)
    scFbf = AL.alloc(2048, BF16)
    hgrp2 = AL.alloc(4 * 1024, BF16)
    f2b = [AL.alloc(1024, BF16), AL.alloc(1024, BF16)]
    scbc = AL.alloc(1024)
    invbc = AL.alloc(1024)
    scF = AL.alloc(2048)
    mxtab = AL.alloc(256)
    i32t = AL.alloc(1024, I32)
    sq_bf2 = AL.alloc(1024, BF16)
    mbc2 = AL.alloc(1024)
    rbc2 = AL.alloc(1024)
    outf = [AL.alloc(1024), AL.alloc(1024)]
    L1F_PEAK = AL.off

    wm_stage = [AL.view(slotB_at, 2048, F32),
                AL.view(slotB_at + 8192, 2048, F32)]
    wm_cols = AL.view(s1tab_at, 64, F32)

    def wmean_scale(dram, nld, slot, entry=None):
        wm_t = {0: 0, 1: 0}
        for r in range(nld):
            buf = wm_stage[r % 2]
            if r >= 2:
                C.wait("sync", "V", wm_t[r % 2])
            elif entry is not None:
                for sem in ("T", "V", "S"):
                    C.wait("sync", sem, entry[sem])
            lv = dma("sync", buf, dram[r], "LM%d" % (r % 2))
            wm_t[r % 2] = vopx(lambda e, b=buf, r=r: e.tensor_reduce(
                wm_cols[:, r:r + 1], b, AX.X, ALU.add,
                apply_absolute_value=True), [("LM%d" % (r % 2), lv)])
        sop(lambda e, nld=nld: e.add(wm_cols[:, 0:nld], wm_cols[:, 0:nld], 0.0))
        vop(lambda e, nld=nld: e.tensor_reduce(
            wm_cols[:, 40:41], wm_cols[:, 0:nld], AX.X, ALU.add))
        mm(pp[0][:1, 0:1], onesc_f[:, 0:1], wm_cols[:, 40:41])
        vop(lambda e: e.tensor_copy(rows[:1, 1024:1025], pp[0][:1, 0:1]))
        sop(lambda e: e.activation(rows[:1, 1025:1026], rows[:1, 1024:1025],
                                   AF.Ln, bias=0.0, scale=1.0 / 8388608.0))
        vop(lambda e: e.tensor_scalar(rows[:1, 1026:1027], rows[:1, 1025:1026],
                                      math.log(1e-5), None, ALU.max))
        sop(lambda e: e.activation(rows[:1, 4:5], rows[:1, 1026:1027],
                                   AF.Exp, bias=0.0, scale=-1.0))
        sop(lambda e: e.activation(rows[:1, 5:6], rows[:1, 1026:1027],
                                   AF.Exp, bias=math.log(0.25), scale=2.0))
        sop(lambda e: e.activation(rows[:1, 6:7], rows[:1, 1026:1027],
                                   AF.Exp, bias=math.log(0.5), scale=1.0))
        bcast(pp[0][:, 0:3], rows[:1, 4:7])
        vop(lambda e, slot=slot: e.tensor_copy(srow[:, slot:slot + 1],
                                               pp[0][:, 0:1]))
        vop(lambda e, slot=slot: e.tensor_copy(srow[:, slot + 1:slot + 2],
                                               pp[0][:, 1:2]))
        vop(lambda e, slot=slot: e.tensor_copy(srow[:, slot + 2:slot + 3],
                                               pp[0][:, 2:3]))

    def fwht_rows_from_maxcols(nsc, stride):
        for tt in range(8):
            vop(lambda e, tt=tt, st=stride: e.tensor_reduce(
                mxacc[:, tt:tt + 1], mxtab[:, tt * st:(tt + 1) * st],
                AX.X, ALU.max))
        for tt in range(8):
            sop(lambda e, tt=tt: e.add(scFbf[:, tt:tt + 1],
                                       mxacc[:, tt:tt + 1], 0.0))
        for tt in range(8):
            tb = 5 if tt % 2 == 0 else 6
            transpose(ppb[tb][:1, 0:128], scFbf[:, tt:tt + 1])
            vop(lambda e, tb=tb, tt=tt: e.tensor_copy(
                rows[:1, tt * 128:(tt + 1) * 128], ppb[tb][:1, 0:128]))
        sop(lambda e: e.add(rows[:1, 0:1024], rows[:1, 0:1024], 0.0))
        vop(lambda e, nsc=nsc: e.tensor_scalar(rows[:1, 0:1024], rows[:1, 0:1024],
                                               nsc, None, ALU.mult))
        vop(lambda e: e.tensor_scalar(rows[:1, 0:1024], rows[:1, 0:1024],
                                      1e-5, None, ALU.max))
        vop(lambda e: e.reciprocal(rows2[:1, 0:1024], rows[:1, 0:1024]))
        vop(lambda e: e.tensor_scalar(rows2[:1, 0:1024], rows2[:1, 0:1024],
                                      127.0, None, ALU.mult))
        vop(lambda e: e.tensor_scalar(rows[:1, 0:1024], rows[:1, 0:1024],
                                      1.0 / 127.0, None, ALU.mult))
        for half in range(2):
            h0 = half * 512
            bcast(pp[0][:, :], rows2[:1, h0:h0 + 512])
            bcast(pp[1][:, :], rows[:1, h0:h0 + 512])
            vop(lambda e, h0=h0: e.tensor_copy(scbc[:, h0:h0 + 512], pp[0][:, :]))
            vop(lambda e, h0=h0: e.tensor_copy(invbc[:, h0:h0 + 512], pp[1][:, :]))

    def phase_l1f():
        wmean_scale([din["wup_p"][:, r * 2048:(r + 1) * 2048] for r in range(32)],
                    32, 0, entry=marks["wm"])
        if stop_after == "wsc":
            dump("srow", srow, 128, 8)
            return True
        C.fence(("gpsimd",))
        for u in range(8):
            def g_pre(e, u=u):
                src = slotA[u * 16:(u + 1) * 16, :].rearrange(
                    "cp (c t) -> cp c t", t=1024)
                dstap = shufd[:, 0:16 * 1024].rearrange("p (cp t) -> p cp t", t=1024)
                return e.dma_start(out=dstap[u:128:8, :, :].transpose([1, 0, 2]),
                                   in_=src)
            C.op("gpsimd", g_pre)
            C.inc("gpsimd", "G", 16)
        C.wait("gpsimd", "G", C.cnt["G"])
        gv = dma("gpsimd", xq[:, :], shufd[:, 0:16 * 1024])
        C.wait("tensor", "G", gv)
        C.wait("vector", "G", gv)
        for fc in range(16):
            for nh in range(2):
                mm(pp[nh][:, :], H128_bf[:, :],
                   xq[:, fc * 1024 + nh * 512:fc * 1024 + nh * 512 + 512])
            for nh in range(2):
                vop(lambda e, fc=fc, nh=nh: e.tensor_copy(
                    xq[:, fc * 1024 + nh * 512:fc * 1024 + nh * 512 + 512],
                    pp[nh][:, :]))
        C.fence(("gpsimd",))
        for l in range(8):
            def g_mid(e, l=l):
                src = xq[l:128:8, :].rearrange("hg (c t) -> hg c t", t=1024)
                dstap = shufd[:, 0:16 * 1024].rearrange("p (hg t) -> p hg t", t=1024)
                return e.dma_start(out=dstap[l:128:8, :, :].transpose([1, 0, 2]),
                                   in_=src)
            C.op("gpsimd", g_mid)
            C.inc("gpsimd", "G", 16)
        C.wait("gpsimd", "G", C.cnt["G"])
        gv = dma("gpsimd", xq[:, :], shufd[:, 0:16 * 1024])
        C.wait("tensor", "G", gv)
        C.wait("vector", "G", gv)
        for fc in range(16):
            for tt in range(8):
                bb = 2 + (tt % 4)
                mm(pp[bb][:, 0:128],
                   xq[:, fc * 1024 + tt * 128:fc * 1024 + tt * 128 + 128],
                   H16I8_bf[:, :])
                vop(lambda e, bb=bb, tt=tt, fc=fc: e.tensor_reduce(
                    mxtab[:, tt * 16 + fc:tt * 16 + fc + 1], pp[bb][:, 0:128],
                    AX.X, ALU.max, apply_absolute_value=True))
            for nh in range(2):
                mm(pp[nh][:, :], H16I8_bf[:, :],
                   xq[:, fc * 1024 + nh * 512:fc * 1024 + nh * 512 + 512])
            for nh in range(2):
                vop(lambda e, fc=fc, nh=nh: e.tensor_scalar(
                    xq[:, fc * 1024 + nh * 512:fc * 1024 + nh * 512 + 512],
                    pp[nh][:, :], R2048, None, ALU.mult))
        fwht_rows_from_maxcols(R2048, 16)
        if stop_after == "xh1":
            dump("xh", xq, 2048, 1024)
            dump("scbc", scbc, 128, 1024)
            dump("invbc", invbc, 128, 1024)
            return True
        for fc in range(16):
            vop(lambda e, fc=fc: e.tensor_mul(
                scF[:, 0:1024], xq[:, fc * 1024:(fc + 1) * 1024], scbc[:, :]))
            vop(lambda e: e.tensor_copy(i32t[:, :], scF[:, 0:1024]))
            vop(lambda e, fc=fc: e.scalar_tensor_tensor(
                xq[:, fc * 1024:(fc + 1) * 1024], i32t[:, :], 1.0, invbc[:, :],
                ALU.mult, ALU.mult))
        if stop_after == "xq1":
            dump("xq", xq, 2048, 1024)
            return True
        C.fence()
        sg2, addt, mmg, rel, stt, h128t, cpy = {}, {}, {}, {}, {}, {}, {}
        gv_prev = {0: C.cnt["GA"], 1: C.cnt["GB"]}
        for m_ in range(32):
            e_ = m_ % 2
            sl = stageF[e_]
            sA = wslbf[:, e_ * 4096:e_ * 4096 + 2048]
            sB = wslbf[:, e_ * 4096 + 2048:e_ * 4096 + 4096]
            if m_ >= 2:
                C.wait("sync", "S", sg2[m_ - 2])
            lv = dma("sync", sl, din["wup_p"][:, m_ * 2048:(m_ + 1) * 2048],
                     "LW%d" % e_)
            swt = [("LW%d" % e_, lv)]
            if m_ >= 2:
                swt.append(("T", mmg[(m_ - 2, 1)]))
            sopx(lambda e, s=sl, w=sA: e.activation(
                w, s, AF.Sign, bias=-0.5, scale=srow[:, 0:1]), swt)
            sg2[m_] = sopx(lambda e, s=sl, w=sB: e.activation(
                w, s, AF.Sign, bias=0.5, scale=srow[:, 0:1]),
                [("V", addt.get(m_ - 2, 0))])
            addt[m_] = vopx(lambda e, a=sA, b=sB: e.tensor_add(a, a, b),
                            [("S", sg2[m_])])
            for nh in range(2):
                o5 = nh * 1024
                wt0 = [("V", addt[m_])]
                if m_ >= 1:
                    wt0.append(("S", rel[(m_ - 1, nh)]))
                for k in range(16):
                    mmg[(m_, nh)] = mmx(
                        pp[nh][:, :], sA[:, k * 128:(k + 1) * 128],
                        xq[:, k * 1024 + nh * 512:k * 1024 + nh * 512 + 512],
                        k == 0, k == 15, wt0 if k == 0 else ())
                rwt = [("T", mmg[(m_, nh)])]
                if m_ >= 1:
                    rwt.append(("V", stt[(m_ - 1, nh)]))
                rel[(m_, nh)] = sopx(lambda e, nh=nh: e.activation(
                    scF[:, nh * 512:nh * 512 + 512], pp[nh][:, :], AF.Relu), rwt)
                stt[(m_, nh)] = vopx(lambda e, nh=nh, o5=o5:
                                     e.scalar_tensor_tensor(
                    scFbf[:, o5:o5 + 512], scF[:, nh * 512:nh * 512 + 512],
                    srow[:, 1:2], scF[:, nh * 512:nh * 512 + 512],
                    ALU.mult, ALU.mult), [("S", rel[(m_, nh)])])
                hwt = [("V", stt[(m_, nh)])]
                if m_ >= 1:
                    hwt.append(("V", cpy[(m_ - 1, nh)]))
                h128t[(m_, nh)] = mmx(pp[2 + nh][:, :], H128_bf[:, :],
                                      scFbf[:, o5:o5 + 512], True, True, hwt)
                gs = "GA" if nh == 0 else "GB"
                cpy[(m_, nh)] = vopx(lambda e, nh=nh, o5=o5: e.tensor_copy(
                    scFbf[:, o5 + 512:o5 + 1024], pp[2 + nh][:, :]),
                    [("T", h128t[(m_, nh)]), (gs, gv_prev[nh])])
                C.wait("gpsimd", "V", cpy[(m_, nh)])
                for l in range(4):
                    def g_sh4(e, m_=m_, l=l, nh=nh, o5=o5):
                        dstap = shufd[m_ * 4 + l:m_ * 4 + l + 1,
                                      0:32 * 1024].rearrange(
                            "p (u hh t) -> (p u) hh t", hh=2, t=512)
                        return e.dma_start(out=dstap[:, nh, :],
                                           in_=scFbf[l:128:4, o5 + 512:o5 + 1024])
                    C.op("gpsimd", g_sh4)
                    gv = C.inc("gpsimd", gs, 16)
                gv_prev[nh] = gv
        wmean_scale([din["wdnTp"][r * 128:(r + 1) * 128, :] for r in range(32)],
                    32, 4)
        C.fence()
        if stop_after == "h4s":
            if "h4s" in tap_t:
                stg = AL.view(AL.nbytes - 8192, 2048, F32)
                for m2 in range(32):
                    lv = dma("sync", f2b[0], shufd[:, m2 * 1024:(m2 + 1) * 1024],
                             "LX0")
                    C.wait("vector", "LX0", lv)
                    vv = vop(lambda e: e.tensor_copy(stg[:, 0:1024], f2b[0]))
                    C.wait("gpsimd", "V", vv)
                    gv = dma("gpsimd", tap_t["h4s"][m2 * 128:(m2 + 1) * 128, :],
                             stg[:, 0:1024])
                    C.wait("vector", "G", gv)
                    C.wait("sync", "G", gv)
            return True
        mx_t = {0: 0, 1: 0}
        for m2 in range(32):
            fb = f2b[m2 % 2]
            if m2 >= 2:
                C.wait("sync", "T", mx_t[m2 % 2])
            lv = dma("sync", fb, shufd[:, m2 * 1024:(m2 + 1) * 1024],
                     "LX%d" % (m2 % 2))
            C.wait("tensor", "LX%d" % (m2 % 2), lv)
            for tt in range(8):
                bb = 2 + (tt % 4)
                tvx = mm(pp[bb][:, 0:128], fb[:, tt * 128:(tt + 1) * 128],
                         H32I4_bf[:, :])
                vop(lambda e, bb=bb, tt=tt, m2=m2: e.tensor_reduce(
                    mxtab[:, tt * 32 + m2:tt * 32 + m2 + 1], pp[bb][:, 0:128],
                    AX.X, ALU.max, apply_absolute_value=True))
            mx_t[m2 % 2] = tvx
        fwht_rows_from_maxcols(R4096, 32)
        if stop_after == "fw4m":
            dump("scbc", scbc, 128, 1024)
            dump("invbc", invbc, 128, 1024)
            return True
        vop(lambda e: e.memset(slotB[:, :], 0.0))
        C.fence()
        sg1d, sg2d, addd, qtt, bankev = {}, {}, {}, {}, {}
        odlast = {}
        oduse = {0: 0, 1: 0, 2: 0, 3: 0}
        f2use = {0: 0, 1: 0}
        for grp in range(8):
            for j in range(4):
                m2 = grp * 4 + j
                jj = j % 2
                fb = f2b[jj]
                prev = (grp, j - 2) if j >= 2 else (grp - 1, j + 2)
                prevj = (grp, j - 1) if j >= 1 else (grp - 1, 3)
                C.wait("sync", "T", f2use[jj])
                C.wait("sync", "S", sg2d.get(prev, 0))
                lv = dma("sync", fb, shufd[:, m2 * 1024:(m2 + 1) * 1024],
                         "LX%d" % jj)
                sl = stageF[jj]
                lv2 = dma("sync", sl, din["wdnTp"][m2 * 128:(m2 + 1) * 128, :],
                          "LW%d" % jj)
                swt = [("LW%d" % jj, lv2), ("V", addd.get(prevj, 0))]
                sg1d[(grp, j)] = sopx(lambda e, s=sl: e.activation(
                    scFbf[:, 0:2048], s, AF.Sign, bias=-0.5,
                    scale=srow[:, 4:5]), swt)
                s2wt = []
                if grp > 0:
                    s2wt.append(("T", odlast[grp - 1]))
                sg2d[(grp, j)] = sopx(lambda e, s=sl, j=j: e.activation(
                    wslbf[:, j * 2048:(j + 1) * 2048], s, AF.Sign, bias=0.5,
                    scale=srow[:, 4:5]), s2wt)
                addd[(grp, j)] = vopx(lambda e, j=j: e.tensor_add(
                    wslbf[:, j * 2048:(j + 1) * 2048],
                    wslbf[:, j * 2048:(j + 1) * 2048],
                    scFbf[:, 0:2048]), [("S", sg2d[(grp, j)])])
                for nh in range(2):
                    o6 = nh * 1024
                    mwt = [("V", bankev.get(jj, 0))]
                    if nh == 0:
                        mwt.append(("LX%d" % jj, lv))
                    tvx = mmx(pp[6 + jj][:, :], H32I4_bf[:, :],
                              fb[:, nh * 512:nh * 512 + 512], True, True, mwt)
                    bankev[jj] = vopx(lambda e, o6=o6, jj=jj: e.tensor_scalar(
                        scF[:, o6:o6 + 512], pp[6 + jj][:, :], R4096, None,
                        ALU.mult), [("T", tvx)])
                    vopx(lambda e, o6=o6, nh=nh: e.tensor_mul(
                        scF[:, o6 + 512:o6 + 1024], scF[:, o6:o6 + 512],
                        scbc[:, nh * 512:nh * 512 + 512]))
                    vopx(lambda e, nh=nh, o6=o6: e.tensor_copy(
                        i32t[:, nh * 512:nh * 512 + 512],
                        scF[:, o6 + 512:o6 + 1024]))
                    hwt = []
                    if grp > 0:
                        hwt.append(("T", odlast[grp - 1]))
                    qtt[(grp, j)] = vopx(lambda e, j=j, nh=nh:
                                         e.scalar_tensor_tensor(
                        hgrp2[:, j * 1024 + nh * 512:j * 1024 + nh * 512 + 512],
                        i32t[:, nh * 512:nh * 512 + 512], 1.0,
                        invbc[:, nh * 512:nh * 512 + 512],
                        ALU.mult, ALU.mult), hwt)
                    f2use[jj] = tvx
            for od in range(16):
                for nh in range(2):
                    bk = (od * 2 + nh) % 4
                    gwt = [("V", oduse[bk])]
                    if od == 0 and nh == 0:
                        gwt += [("V", addd[(grp, 3)]), ("V", qtt[(grp, 3)])]
                    for j in range(4):
                        tvx = mmx(
                            pp[bk][:, :],
                            wslbf[:, j * 2048 + od * 128:j * 2048 + od * 128 + 128],
                            hgrp2[:, j * 1024 + nh * 512:j * 1024 + nh * 512 + 512],
                            j == 0, j == 3, gwt if j == 0 else ())
                    odlast[grp] = tvx
                    oduse[bk] = vopx(lambda e, od=od, nh=nh, bk=bk:
                                     e.scalar_tensor_tensor(
                        slotB[:, od * 1024 + nh * 512:od * 1024 + nh * 512 + 512],
                        pp[bk][:, :], srow[:, 6:7],
                        slotB[:, od * 1024 + nh * 512:od * 1024 + nh * 512 + 512],
                        ALU.mult, ALU.add), [("T", tvx)])
        C.fence()
        for fc in range(16):
            vop(lambda e, fc=fc: e.tensor_add(
                slotB[:, fc * 1024:(fc + 1) * 1024],
                slotB[:, fc * 1024:(fc + 1) * 1024],
                slotA[:, fc * 1024:(fc + 1) * 1024]))
        out_g = {0: 0, 1: 0}

        def out_fn(fc, scrA_, rbc_):
            ob = outf[fc % 2]
            gs = "GA" if fc % 2 == 0 else "GB"
            C.wait("vector", gs, out_g[fc % 2])
            vv = vop(lambda e, ob=ob: e.tensor_mul(ob[:, :], scrA_[:, 0:1024],
                                                   rbc_[:, :]))
            C.wait("gpsimd", "V", vv)
            out_g[fc % 2] = dma("gpsimd", y_out[fc * 128:(fc + 1) * 128, :],
                                ob[:, :], gs)

        layer_norm(slotB, sq_bf2, mbc2, rbc2, scF[:, 0:1024], out_fn=out_fn)
        C.wait("gpsimd", "GA", C.cnt["GA"])
        C.wait("gpsimd", "GB", C.cnt["GB"])

    if phase_l1f():
        return finish()
    C.fence()
    LAST_WAITED.clear()
    LAST_WAITED.update({k: set(v) for k, v in C.waited.items()})
    C.replay(es.enter_context(nc.Block()))
    es.close()
    return nc


_NC_CACHE = {}


def kernel(**inputs):
    in_maps = host_prepare(inputs)
    key = "full"
    if key not in _NC_CACHE:
        build_nc(stop_after="out")
        _NC_CACHE[key] = build_nc(stop_after="out",
                                  want=dict(LAST_WAITED))
    nc = _NC_CACHE[key]
    res = run_bass_kernel_spmd(nc, in_maps, core_ids=list(range(8)))
    out = np.zeros((B, S, D), np.float32)
    for core in range(8):
        g, t = core // 4, core % 4
        out[g, t * TOK:(t + 1) * TOK, :] = np.asarray(res.results[core]["y_out"]).T
    return out


# version 13
# speedup vs baseline: 1.0376x; 1.0218x over previous
import bisect
import math
import numpy as np
import ml_dtypes
from contextlib import ExitStack
import concourse.bass as bass
import concourse.mybir as mybir
from concourse.bass_utils import run_bass_kernel_spmd

F32 = mybir.dt.float32
BF16 = mybir.dt.bfloat16
I32 = mybir.dt.int32
AF = mybir.ActivationFunctionType
ALU = mybir.AluOpType
AX = mybir.AxisListType

B, S, D = 2, 4096, 2048
CH, NCH, TOK = 1024, 4, 1024
BASE_LR_INV = 0.001 + math.log(-math.expm1(-0.001))
C_L0, C_L1 = 24.0, 16.0
SC0 = 1.0 / math.sqrt(128.0)
R2048 = 1.0 / math.sqrt(2048.0)
R4096 = 1.0 / math.sqrt(4096.0)
R8192 = 1.0 / math.sqrt(8192.0)
ENGINES = ("tensor", "vector", "scalar", "sync", "gpsimd")
ARN_ELEMS = 53200

CONST_VALS = sorted(set([
    -C_L0, -C_L1, 0.0, SC0, -0.5, 0.5, 1.0, 1e-6, 1e-5, 1e-24, 1e-12,
    1.0 / D, 1.0 / 512.0, R8192, R4096, R2048,
    -1.0, 2.0, math.log(0.5), math.log(0.25), 1.0 / 8388608.0,
]))


def _hadamard(n):
    h = np.array([[1.0]])
    while h.shape[0] < n:
        h = np.block([[h, h], [h, -h]])
    return h


def _f1_perm(n, nch):
    lb = int(math.log2(nch))
    out = np.zeros(n, np.int64)
    for m in range(nch):
        for p in range(128):
            out[m * 128 + p] = (p << lb) | m
    return out


def _f3_perm(n, nch):
    lb_hi = int(math.log2(nch))
    k = 128 // nch
    lbk = int(math.log2(k))
    out = np.zeros(n, np.int64)
    for m2 in range(nch):
        for p2 in range(128):
            i_hi = (m2 << lbk) | (p2 & (k - 1))
            q = p2 >> lbk
            out[m2 * 128 + p2] = (i_hi << lb_hi) | q
    return out


def _pack_w(W):
    K, N = W.shape
    return np.ascontiguousarray(
        W.reshape(K // 128, 128, N // 128, 128)
        .transpose(1, 2, 0, 3).reshape(128, -1))


def host_prepare(inputs):
    f32 = np.float32
    x0 = np.ascontiguousarray(np.asarray(inputs["L0_attn_hidden_states"], f32))
    WqkvT = np.asarray(inputs["L0_attn_Wqkv"], f32).T
    WoT_p = _pack_w(np.ascontiguousarray(np.asarray(inputs["L0_attn_Wo"], f32).T))
    lr_wT = np.asarray(inputs["L0_attn_lr_w"], f32).T
    f1_8192 = _f1_perm(8192, 64)
    f3_8192 = _f3_perm(8192, 64)
    f1_4096 = _f1_perm(4096, 32)
    f3_4096 = _f3_perm(4096, 32)
    f3_2048 = _f3_perm(2048, 16)
    W1_p = _pack_w(np.asarray(inputs["L0_ffn_W1"], f32).T[:, f1_8192])
    W3_p = _pack_w(np.asarray(inputs["L0_ffn_W3"], f32).T[:, f1_8192])
    W2Tp = np.ascontiguousarray(np.asarray(inputs["L0_ffn_W2"], f32).T[f3_8192, :])
    wup_p = _pack_w(np.asarray(inputs["L1_ffn_w_up"], f32).T[f3_2048, :][:, f1_4096])
    wdnTp = np.ascontiguousarray(np.asarray(inputs["L1_ffn_w_down"], f32).T[f3_4096, :])
    wqkv1_p = _pack_w(np.asarray(inputs["L1_attn_w_qkv"], f32))
    wo1_p = _pack_w(np.asarray(inputs["L1_attn_w_o"], f32))

    ang0 = np.arange(S, dtype=np.float64)[:, None] / (
        500000.0 ** (np.arange(0, 128, 2, dtype=np.float64) / 128))[None, :]
    cos0f = np.ascontiguousarray(np.concatenate([np.cos(ang0).T] * 2, 0).astype(f32))
    sin0f = np.ascontiguousarray(np.concatenate([np.sin(ang0).T] * 2, 0).astype(f32))
    ang1 = np.arange(S, dtype=np.float64)[:, None] / (
        1000000.0 ** (np.arange(0, 128, 2, dtype=np.float64) / 128))[None, :]
    cos1f = np.concatenate([np.cos(ang1).T] * 2, 0).astype(f32)
    sin1f = np.concatenate([np.sin(ang1).T] * 2, 0).astype(f32)

    P = np.zeros((128, 128), f32)
    for i in range(64):
        P[i, i + 64] = -1.0
        P[i + 64, i] = 1.0
    PT = np.ascontiguousarray(P.T)
    ident = np.eye(128, dtype=f32)
    kk = np.arange(128)[:, None]
    cc = np.arange(1024)[None, :]
    maskM = (kk <= cc - 512).astype(f32)
    maskW = np.ascontiguousarray(1.0 - maskM)

    def chunkmajor(vec):
        return np.ascontiguousarray(np.asarray(vec, f32).reshape(4, 128).T)

    in_maps = []
    for core in range(8):
        g, t = core // 4, core % 4
        cols = slice(512 * t, 512 * t + 512)
        m = {}
        m["x0T"] = np.ascontiguousarray(x0[g].T)
        m["x0T_own"] = np.ascontiguousarray(x0[g].T[:, t * TOK:(t + 1) * TOK])
        m["wqkv_p"] = _pack_w(np.concatenate(
            [WqkvT[:, 512 * t:512 * t + 512],
             WqkvT[:, 2048 + 512 * t:2048 + 512 * t + 512],
             WqkvT[:, 4096 + 512 * t:4096 + 512 * t + 512]], 1))
        m["lrw_p"] = np.ascontiguousarray(
            lr_wT[:, [t, 4 + t, 8 + t]].reshape(16, 128, 3)
            .transpose(1, 0, 2).reshape(128, 48))
        m["lrb"] = np.ascontiguousarray(
            np.asarray(inputs["L0_attn_lr_b"], f32)[[t, 4 + t, 8 + t]].reshape(1, 3))
        m["pvecs"] = np.ascontiguousarray(np.concatenate([
            chunkmajor(np.asarray(inputs["L0_attn_q_norm_w"])[cols]),
            chunkmajor(np.asarray(inputs["L0_attn_k_norm_w"])[cols]),
            chunkmajor(np.asarray(inputs["L0_attn_qk_scale"])[cols, 0]),
            chunkmajor(np.asarray(inputs["L0_attn_qk_offset"])[cols, 0]),
            chunkmajor(np.asarray(inputs["L0_attn_qk_scale"])[cols, 1]),
            chunkmajor(np.asarray(inputs["L0_attn_qk_offset"])[cols, 1]),
            chunkmajor(np.asarray(inputs["L0_attn_ttt_norm_w"])),
        ], axis=1))
        m["w0T"] = np.ascontiguousarray(
            np.asarray(inputs["L0_attn_w0"][t], f32).T.reshape(4, 128, 512)
            .transpose(1, 0, 2).reshape(128, 2048))
        m["w2T"] = np.ascontiguousarray(
            np.asarray(inputs["L0_attn_w2"][t], f32).T.reshape(4, 128, 512)
            .transpose(1, 0, 2).reshape(128, 2048))
        m["w1N"] = np.ascontiguousarray(
            np.asarray(inputs["L0_attn_w1"][t], f32).reshape(4, 128, 512)
            .transpose(1, 0, 2).reshape(128, 2048))
        m["WoT_p"] = WoT_p
        m["W1_p"] = W1_p
        m["W3_p"] = W3_p
        m["W2Tp"] = W2Tp
        m["wqkv1_p"] = wqkv1_p
        m["wo1_p"] = wo1_p
        m["wup_p"] = wup_p
        m["wdnTp"] = wdnTp
        m["cos0"] = cos0f
        m["sin0"] = sin0f
        m["cos1"] = np.ascontiguousarray(cos1f[:, t * TOK:(t + 1) * TOK])
        m["sin1"] = np.ascontiguousarray(sin1f[:, t * TOK:(t + 1) * TOK])
        m["PT"] = PT
        m["ident"] = ident
        m["maskM"] = maskM
        m["maskW"] = maskW
        m["maskWc"] = np.ascontiguousarray(
            maskW if t > 0 else np.zeros((128, 1024), f32))
        m["H128"] = _hadamard(128).astype(f32)
        m["H64I2"] = np.kron(_hadamard(64), np.eye(2)).astype(f32)
        m["H32I4"] = np.kron(_hadamard(32), np.eye(4)).astype(f32)
        m["H16I8"] = np.kron(_hadamard(16), np.eye(8)).astype(f32)
        msel = np.zeros((128, 4), f32)
        msel[:, t] = 1.0
        m["msel"] = msel
        hsel = np.zeros((128, 4), f32)
        hsel[:, (t + 3) % 4] = 1.0
        m["hsel"] = hsel
        in_maps.append(m)
    return in_maps


class Ctx:
    BL = {"tensor": ("V", "S"), "vector": ("T", "S"), "scalar": ("T", "V")}
    TICK = {"tensor": "T", "vector": "V", "scalar": "S"}

    THIN = ("T", "V", "S")

    def __init__(self, nc, serial=True, want=None):
        self.nc = nc
        self.serial = serial
        self.prog = {e: [] for e in ENGINES}
        self.cnt = {}
        self.lastwait = {}
        self.S = {}
        self.want = want
        self.waited = {}
        self.attached = {}

    def addsem(self, es, name):
        self.S[name] = es.enter_context(self.nc.semaphore(name))
        self.cnt[name] = 0

    def wait(self, eng, sem, val):
        if val is None or val <= 0:
            return
        key = (eng, sem)
        if self.lastwait.get(key, 0) >= val:
            return
        self.lastwait[key] = val
        self.waited.setdefault(sem, set()).add(int(val))
        if self.want is not None and sem in self.THIN:
            att = self.attached.get(sem, [])
            rv = bisect.bisect_right(att, int(val))
            assert rv > 0 and att[rv - 1] == int(val), \
                f"wait on unattached tick {sem}>={val}"
            self.prog[eng].append([None, (sem, rv)])
        else:
            self.prog[eng].append([None, (sem, int(val))])

    def inc(self, eng, sem, by):
        self.cnt[sem] += by
        if (self.want is not None and sem in self.THIN
                and self.cnt[sem] not in self.want.get(sem, ())):
            return self.cnt[sem]
        self.attached.setdefault(sem, []).append(self.cnt[sem])
        slot = None
        for item in reversed(self.prog[eng]):
            if item[0] is not None:
                slot = item
                break
        assert slot is not None
        fn = slot[0]
        sh = self.S[sem]
        slot[0] = (lambda e, fn=fn, sh=sh, by=by: fn(e).then_inc(sh, by))
        return self.cnt[sem]

    def op(self, eng, fn):
        if self.serial and eng in self.BL:
            for s in self.BL[eng]:
                self.wait(eng, s, self.cnt[s])
        self.prog[eng].append([fn, None])
        if eng in self.TICK:
            return self.inc(eng, self.TICK[eng], 1)
        return None

    def opx(self, eng, fn, waits=()):
        for s, v in waits:
            self.wait(eng, s, v)
        self.prog[eng].append([fn, None])
        if eng in self.TICK:
            return self.inc(eng, self.TICK[eng], 1)
        return None

    def fence(self, engines=ENGINES):
        marks = dict(self.cnt)
        for e in engines:
            for s, v in marks.items():
                self.wait(e, s, v)

    def replay(self, block):
        def mk(eng):
            def body(e):
                for fn, w in self.prog[eng]:
                    if fn is not None:
                        fn(e)
                    else:
                        e.wait_ge(self.S[w[0]], w[1])
            return body
        for eng in ENGINES:
            getattr(block, eng)(mk(eng))


class Alloc:
    def __init__(self, arena, nbytes):
        self.arena = arena
        self.nbytes = nbytes
        self.off = 0
        self.peak = 0
        self.marks = []
        self.last_at = 0

    def view(self, at, elems, dt=F32):
        bs = 2 if dt == BF16 else 4
        assert at % bs == 0 and at + elems * bs <= self.nbytes
        if dt == F32:
            return self.arena[:, at // 4: at // 4 + elems]
        return self.arena[:, :].bitcast(dt)[:, at // bs: at // bs + elems]

    def alloc(self, elems, dt=F32):
        self.off = (self.off + 3) & ~3
        at = self.off
        self.off += elems * (2 if dt == BF16 else 4)
        self.peak = max(self.peak, self.off)
        assert self.off <= self.nbytes, f"arena overflow {self.off} > {self.nbytes}"
        self.last_at = at
        return self.view(at, elems, dt)

    def push(self):
        self.marks.append(self.off)

    def pop(self):
        self.off = self.marks.pop()


LAST_WAITED = {}


def build_nc(stop_after="out", taps=(), skip=(), want=None):
    nc = bass.Bass("TRN2", target_bir_lowering=False, debug=False)
    din = {}
    for name, shape in [
        ("x0T", [D, S]), ("x0T_own", [D, TOK]),
        ("wqkv_p", [128, 12 * 2048]), ("lrw_p", [128, 48]), ("lrb", [1, 3]),
        ("pvecs", [128, 28]),
        ("w0T", [128, 2048]), ("w2T", [128, 2048]), ("w1N", [128, 2048]),
        ("WoT_p", [128, 16 * 2048]),
        ("W1_p", [128, 64 * 2048]), ("W3_p", [128, 64 * 2048]), ("W2Tp", [8192, D]),
        ("wqkv1_p", [128, 24 * 2048]), ("wo1_p", [128, 16 * 2048]),
        ("wup_p", [128, 32 * 2048]), ("wdnTp", [4096, D]),
        ("cos0", [128, S]), ("sin0", [128, S]),
        ("cos1", [128, TOK]), ("sin1", [128, TOK]),
        ("PT", [128, 128]), ("ident", [128, 128]),
        ("maskM", [128, 1024]), ("maskW", [128, 1024]), ("maskWc", [128, 1024]),
        ("H128", [128, 128]), ("H64I2", [128, 128]), ("H32I4", [128, 128]),
        ("H16I8", [128, 128]), ("msel", [128, 4]), ("hsel", [128, 4]),
    ]:
        din[name] = nc.dram_tensor(name, shape, F32, kind="ExternalInput")

    y_out = nc.dram_tensor("y_out", [D, TOK], F32, kind="ExternalOutput")
    tap_t = {n: nc.dram_tensor("tap_" + n, list(sh), F32, kind="ExternalOutput")
             for n, sh in dict(taps).items()}

    wqkv_bfd = nc.dram_tensor("wqkv_bfd", [128, 12 * 2048], BF16)
    ag_mix_in = [nc.dram_tensor(f"ag_mix_in{c}", [512, CH], BF16) for c in range(NCH)]
    ag_mix_out = [nc.dram_tensor(f"ag_mix_out{c}", [D, CH], BF16) for c in range(NCH)]
    ar_in = [nc.dram_tensor(f"ar_in{c}", [1, 2048], F32) for c in range(NCH)]
    ar_out = [nc.dram_tensor(f"ar_out{c}", [1, 2048], F32) for c in range(NCH)]
    ag_kv_in = nc.dram_tensor("ag_kv_in", [1024, 512], BF16)
    ag_kv_out = nc.dram_tensor("ag_kv_out", [4096, 512], BF16)
    shufd = nc.dram_tensor("shufd", [128, 64 * 1024], BF16)
    RG = [[0, 1, 2, 3], [4, 5, 6, 7]]

    marks = {}
    es = ExitStack()
    C = Ctx(nc, serial=True, want=want)
    for s in ("T", "V", "S", "L", "G", "GA", "GB", "LS", "LE",
              "LX0", "LX1", "LW0", "LW1", "LM0", "LM1", "CCR", "CCA"):
        C.addsem(es, s)

    def sbt(name, shape, dt=F32):
        return es.enter_context(nc.sbuf_tensor(name, shape, dt))

    ARN = sbt("ARN", [128, ARN_ELEMS])
    AL = Alloc(ARN, ARN_ELEMS * 4)
    cpool = AL.alloc(len(CONST_VALS))
    for i, cv in enumerate(CONST_VALS):
        nc.const_aps.register(F32, float(cv), cpool[:, i:i + 1])
    pvecs = AL.alloc(28)
    ones_bf = AL.alloc(2, BF16)
    onesc_f = AL.alloc(1)
    onesr_f = AL.alloc(128)
    PT_bf = AL.alloc(128, BF16)
    id_bf = AL.alloc(128, BF16)
    H128_bf = AL.alloc(128, BF16)
    H64I2_bf = AL.alloc(128, BF16)
    H32I4_bf = AL.alloc(128, BF16)
    H16I8_bf = AL.alloc(128, BF16)
    maskM_bf = AL.alloc(1024, BF16)
    maskW_bf = AL.alloc(1024, BF16)
    maskWc_bf = AL.alloc(1024, BF16)
    rows = AL.alloc(2048)
    rows2 = AL.alloc(2048)
    lrb_bc = AL.alloc(3)
    lrw_bf = AL.alloc(48, BF16)
    lr_sb = AL.alloc(24)
    srow = AL.alloc(8)
    msel = AL.alloc(4)
    hsel = AL.alloc(4)
    mxacc = AL.alloc(8)
    pp = [es.enter_context(nc.psum_tensor(f"pb{i}", [128, 512], F32)) for i in range(8)]
    ppb = [pp[i][:, :].bitcast(BF16) for i in range(8)]

    def dma(qeng, out, in_, sem=None):
        if sem is None:
            sem = "G" if qeng == "gpsimd" else "L"
        C.op(qeng, lambda e, o=out, i=in_: e.dma_start(out=o, in_=i))
        return C.inc(qeng, sem, 16)

    def mm(out, lhsT, rhs, start=True, stop=True):
        return C.op("tensor", lambda e, o=out, l=lhsT, r=rhs, a=start, b=stop:
                    e.matmul(o, l, r, start=a, stop=b))

    def vop(fn):
        return C.op("vector", fn)

    def sop(fn):
        return C.op("scalar", fn)

    def mmx(out, lhsT, rhs, start=True, stop=True, waits=()):
        return C.opx("tensor", lambda e, o=out, l=lhsT, r=rhs, a=start, b=stop:
                     e.matmul(o, l, r, start=a, stop=b), waits)

    def vopx(fn, waits=()):
        return C.opx("vector", fn, waits)

    def sopx(fn, waits=()):
        return C.opx("scalar", fn, waits)

    def colsum(dst_ps, src_bf, start=True, stop=True):
        return mm(dst_ps, ones_bf[:, 0:1], src_bf, start, stop)

    def bcast(dst_ps, row_ap):
        return mm(dst_ps, onesr_f[:1, :], row_ap, True, True)

    def transpose(dst_ps, src_bf):
        return C.op("tensor", lambda e, o=dst_ps, i=src_bf:
                    e.transpose(o, i, id_bf[:, :]))

    def collective(kind, op_, in_t, out_t, sem="CCA"):
        C.op("gpsimd", lambda e, k=kind, o=op_, i=in_t, u=out_t: e.collective_compute(
            k, o, replica_groups=RG, ins=[i.ap().opt()], outs=[u.ap().opt()]))
        return C.inc("gpsimd", sem, 1)

    AL.push()
    ist = [AL.alloc(2048), AL.alloc(2048)]
    ist_bf = AL.alloc(2048, BF16)
    cast_v = {0: 0, 1: 0}
    st_g = {0: 0}

    def phase_init():
        for i, cv in enumerate(CONST_VALS):
            vop(lambda e, i=i, cv=float(cv): e.memset(cpool[:, i:i + 1], cv))
        vop(lambda e: e.memset(ones_bf[:, 0:1], 1.0))
        vop(lambda e: e.memset(onesc_f[:, :], 1.0))
        vop(lambda e: e.memset(onesr_f[:, :], 1.0))
        dma("sync", pvecs[:, :], din["pvecs"][:, :])
        dma("sync", msel[:, :], din["msel"][:, :])
        dma("sync", hsel[:, :], din["hsel"][:, :])
        lv = dma("sync", ist[0][:, 0:48], din["lrw_p"][:, :], "LX0")
        C.wait("vector", "LX0", lv)
        vv = vop(lambda e: e.tensor_copy(lrw_bf[:, :], ist[0][:, 0:48]))
        C.wait("sync", "V", vv)
        lv = dma("sync", rows[:1, 0:3], din["lrb"][:, :])
        C.wait("vector", "L", lv)
        vop(lambda e: e.tensor_scalar(rows[:1, 0:3], rows[:1, 0:3],
                                      BASE_LR_INV, None, ALU.add))
        bcast(pp[0][:, 0:3], rows[:1, 0:3])
        vop(lambda e: e.tensor_copy(lrb_bc[:, :], pp[0][:, 0:3]))
        for nm, dst, n in [("PT", PT_bf, 128), ("ident", id_bf, 128),
                           ("H128", H128_bf, 128), ("H64I2", H64I2_bf, 128),
                           ("H32I4", H32I4_bf, 128), ("H16I8", H16I8_bf, 128),
                           ("maskM", maskM_bf, 1024), ("maskW", maskW_bf, 1024),
                           ("maskWc", maskWc_bf, 1024)]:
            lv = dma("sync", ist[0][:, 0:n], din[nm][:, :], "LX0")
            C.wait("vector", "LX0", lv)
            vv = vop(lambda e, d=dst, s=ist[0][:, 0:n]: e.tensor_copy(d[:, :], s))
            C.wait("sync", "V", vv)
        for m_ in range(12):
            buf = ist[m_ % 2]
            if m_ >= 2:
                C.wait("sync", "V", cast_v[m_ % 2])
            lv = dma("sync", buf, din["wqkv_p"][:, m_ * 2048:(m_ + 1) * 2048],
                     "LX%d" % (m_ % 2))
            C.wait("vector", "LX%d" % (m_ % 2), lv)
            C.wait("vector", "G", 0 if m_ == 0 else st_g[0])
            vv = vop(lambda e, b=buf: e.tensor_copy(ist_bf[:, :], b))
            cast_v[m_ % 2] = vv
            C.wait("gpsimd", "V", vv)
            gv = dma("gpsimd", wqkv_bfd[:, m_ * 2048:(m_ + 1) * 2048], ist_bf[:, :])
            st_g[0] = gv
            C.wait("vector", "G", gv)
        C.fence()

    phase_init()

    def layer_norm(src_bf, sq_bf, mbc, rbc, scrA, dst_bf=None, out_fn=None):
        for fc in range(16):
            sop(lambda e, fc=fc: e.activation(
                sq_bf[:, 0:1024], src_bf[:, fc * 1024:(fc + 1) * 1024], AF.Square))
            for half in range(2):
                h0 = half * 512
                colsum(pp[6 + half][:1, :],
                       src_bf[:, fc * 1024 + h0:fc * 1024 + h0 + 512],
                       fc == 0, fc == 15)
                colsum(pp[4 + half][:1, :], sq_bf[:, h0:h0 + 512], fc == 0, fc == 15)
        vop(lambda e: e.tensor_copy(rows[:1, 0:512], pp[6][:1, :]))
        vop(lambda e: e.tensor_copy(rows[:1, 512:1024], pp[7][:1, :]))
        vop(lambda e: e.tensor_copy(rows[:1, 1024:1536], pp[4][:1, :]))
        vop(lambda e: e.tensor_copy(rows[:1, 1536:2048], pp[5][:1, :]))
        vop(lambda e: e.tensor_scalar(rows[:1, 0:2048], rows[:1, 0:2048],
                                      1.0 / D, None, ALU.mult))
        vop(lambda e: e.tensor_mul(rows2[:1, 0:1024], rows[:1, 0:1024],
                                   rows[:1, 0:1024]))
        vop(lambda e: e.tensor_sub(rows[:1, 1024:2048], rows[:1, 1024:2048],
                                   rows2[:1, 0:1024]))
        sop(lambda e: e.activation(rows[:1, 1024:2048], rows[:1, 1024:2048],
                                   AF.Ln, bias=1e-5, scale=1.0))
        sop(lambda e: e.activation(rows[:1, 1024:2048], rows[:1, 1024:2048],
                                   AF.Exp, bias=0.0, scale=-0.5))
        for half in range(2):
            h0 = half * 512
            bcast(pp[0][:, :], rows[:1, h0:h0 + 512])
            bcast(pp[1][:, :], rows[:1, 1024 + h0:1024 + h0 + 512])
            vop(lambda e, h0=h0: e.tensor_copy(mbc[:, h0:h0 + 512], pp[0][:, :]))
            vop(lambda e, h0=h0: e.tensor_copy(rbc[:, h0:h0 + 512], pp[1][:, :]))
        if dst_bf is None:
            dst_bf = src_bf
        for fc in range(16):
            vop(lambda e, fc=fc: e.tensor_sub(
                scrA[:, 0:1024], src_bf[:, fc * 1024:(fc + 1) * 1024], mbc[:, :]))
            if out_fn is not None:
                out_fn(fc, scrA, rbc)
            else:
                vop(lambda e, fc=fc: e.tensor_mul(
                    dst_bf[:, fc * 1024:(fc + 1) * 1024], scrA[:, 0:1024], rbc[:, :]))

    def dump(name, src_ap, rows_n, cols_n):
        if name not in tap_t:
            return
        C.fence()
        stg = AL.view(AL.nbytes - 8192, 2048, F32)
        for r in range(rows_n // 128):
            vv = vop(lambda e, r=r: e.tensor_copy(
                stg[:, 0:cols_n], src_ap[:, r * cols_n:(r + 1) * cols_n]))
            C.wait("gpsimd", "V", vv)
            gv = dma("gpsimd", tap_t[name][r * 128:(r + 1) * 128, :], stg[:, 0:cols_n])
            C.wait("vector", "G", gv)
        C.fence()

    AL.pop()
    AL.push()
    x0bf = AL.alloc(16 * 512, BF16)
    at_x0bf = AL.last_at
    wslab = [AL.alloc(2048, BF16), AL.alloc(2048, BF16)]
    x0f = [AL.alloc(512), AL.alloc(512)]
    q_f = AL.alloc(4096, BF16)
    k_f = AL.alloc(4096, BF16)
    fvf = AL.alloc(4096, BF16)
    fvl1 = AL.alloc(4096, BF16)
    fkt = AL.alloc(4096, BF16)
    akbuf = [AL.alloc(4096, BF16), AL.alloc(4096, BF16)]
    vtbuf = [AL.alloc(4096, BF16), AL.alloc(4096, BF16)]
    attnT = AL.alloc(4096, BF16)
    ctab = AL.alloc(1024)
    at_ctab = AL.last_at
    stab = AL.alloc(1024)
    rqbc = AL.alloc(1024)
    at_rqbc = AL.last_at
    rkbc = AL.alloc(1024)
    aqh = AL.alloc(1024, BF16)
    at_aqh = AL.last_at
    expt = [AL.alloc(512, BF16), AL.alloc(512, BF16)]
    st_w0 = AL.alloc(2048)
    st_w2 = AL.alloc(2048)
    st_w1 = AL.alloc(2048)
    st_bf = AL.alloc(3 * 2048, BF16)
    scr = AL.alloc(2048)
    scrB = AL.alloc(1024)
    scr_bf = AL.alloc(2048, BF16)
    sbw0, sbw2, sbw1 = st_bf[:, 0:2048], st_bf[:, 2048:4096], st_bf[:, 4096:6144]
    hid = AL.view(at_x0bf, 4096, BF16)
    dgate0 = AL.view(at_x0bf + 8192, 4096, BF16)
    dhpre2 = AL.view(at_x0bf + 16384, 4096, BF16)
    hq = AL.view(at_ctab, 4096, BF16)
    oisb = AL.view(at_rqbc, 4096, BF16)
    w1NT = AL.view(at_aqh, 2048, BF16)
    L0_PEAK = AL.off

    lv = dma("sync", st_w0, din["w0T"][:, :], "LS")
    lv = dma("sync", st_w2, din["w2T"][:, :], "LS")
    lv = dma("sync", st_w1, din["w1N"][:, :], "LS")
    C.wait("vector", "LS", lv)

    slab_t = {0: 0, 1: 0}
    cast_t = {0: 0, 1: 0}
    lv_tab = {0: 0}

    def l0_qkv(c):
        C.wait("sync", "T", C.cnt["T"])
        C.wait("sync", "V", C.cnt["V"])
        dma("sync", ctab, din["cos0"][:, c * CH:c * CH + 1024], "LE")
        lv_tab[0] = dma("sync", stab, din["sin0"][:, c * CH:c * CH + 1024], "LE")
        for half in range(2):
            tok0 = c * CH + half * 512
            for k in range(16):
                buf = x0f[k % 2]
                if k >= 2:
                    C.wait("sync", "V", cast_t[k % 2])
                lv = dma("sync", buf, din["x0T"][k * 128:(k + 1) * 128,
                                                 tok0:tok0 + 512],
                         "LX%d" % (k % 2))
                C.wait("vector", "LX%d" % (k % 2), lv)
                cast_t[k % 2] = vop(lambda e, k=k, b=buf: e.tensor_copy(
                    x0bf[:, k * 512:(k + 1) * 512], b))
            for ttl in range(4):
                tt = half * 4 + ttl
                for k in range(16):
                    mm(pp[5][:, 0:3],
                       x0bf[:, k * 512 + ttl * 128:k * 512 + ttl * 128 + 128],
                       lrw_bf[:, k * 3:(k + 1) * 3], k == 0, k == 15)
                vop(lambda e: e.tensor_add(scr[:, 0:3], pp[5][:, 0:3], lrb_bc[:, :]))
                sop(lambda e: e.activation(scr[:, 4:7], scr[:, 0:3], AF.Exp))
                sop(lambda e, tt=tt: e.activation(
                    lr_sb[:, tt * 3:tt * 3 + 3], scr[:, 4:7], AF.Ln,
                    bias=1.0, scale=1.0))
            for m_ in range(12):
                if m_ == 8:
                    vop(lambda e, half=half: e.tensor_copy(
                        rows[:1, half * 512:half * 512 + 512], pp[6][:1, :]))
                    vop(lambda e, half=half: e.tensor_copy(
                        rows[:1, 1024 + half * 512:1024 + half * 512 + 512],
                        pp[7][:1, :]))
                    if half == 1:
                        l0_stats_issue(c)
                wb = wslab[m_ % 2]
                if m_ >= 2 or half > 0 or c > 0:
                    C.wait("sync", "T", slab_t[m_ % 2])
                lv = dma("sync", wb, wqkv_bfd[:, m_ * 2048:(m_ + 1) * 2048],
                         "LW%d" % (m_ % 2))
                C.wait("tensor", "LW%d" % (m_ % 2), lv)
                bk = m_ % 2
                for k in range(16):
                    tv = mm(pp[bk][:, :], wb[:, k * 128:(k + 1) * 128],
                            x0bf[:, k * 512:(k + 1) * 512], k == 0, k == 15)
                slab_t[m_ % 2] = tv
                if m_ < 8:
                    fc = m_ % 4
                    dst = q_f if m_ < 4 else k_f
                    vop(lambda e, d=dst, fc=fc, half=half, bk=bk: e.tensor_copy(
                        d[:, fc * 1024 + half * 512:fc * 1024 + half * 512 + 512],
                        pp[bk][:, :]))
                    sop(lambda e, bk=bk: e.activation(
                        scr_bf[:, 0:512], pp[bk][:, :], AF.Square))
                    colsum(pp[6 if m_ < 4 else 7][:1, :], scr_bf[:, 0:512],
                           fc == 0, fc == 3)
                else:
                    oc = m_ - 8
                    sop(lambda e, bk=bk: e.activation(
                        scr[:, 0:512], pp[bk][:, :], AF.Sigmoid))
                    vop(lambda e, oc=oc, half=half, bk=bk: e.tensor_mul(
                        fvf[:, oc * 1024 + half * 512:oc * 1024 + half * 512 + 512],
                        scr[:, 0:512], pp[bk][:, :]))
                    vop(lambda e, bk=bk: e.tensor_copy(
                        scr_bf[:, 1024:1536], pp[bk][:, :]))
                    for sub in range(4):
                        tb = 3 if sub % 2 == 0 else 4
                        transpose(ppb[tb][:, 0:128],
                                  scr_bf[:, 1024 + sub * 128:1024 + sub * 128 + 128])
                        tt = half * 4 + sub
                        vop(lambda e, tb=tb, tt=tt, oc=oc, c=c: e.tensor_copy(
                            vtbuf[c % 2][:, tt * 512 + oc * 128:
                                         tt * 512 + oc * 128 + 128],
                            ppb[tb][:, 0:128]))

    def l0_stats_issue(c):
        C.fence(("gpsimd",))
        gv = dma("gpsimd", ar_in[c][:, :], rows[:1, 0:2048])
        C.wait("gpsimd", "G", gv)
        collective("AllReduce", ALU.add, ar_in[c], ar_out[c], "CCR")

    def l0_stats_ar(c):
        cv = C.cnt["CCR"]
        C.wait("gpsimd", "CCR", cv)
        gv = dma("gpsimd", rows[:1, 0:2048], ar_out[c][:, :])
        for e_ in ("scalar", "vector", "tensor"):
            C.wait(e_, "G", gv)
        sop(lambda e: e.activation(rows2[:1, 0:2048], rows[:1, 0:2048],
                                   AF.Ln, bias=1e-6, scale=1.0 / D))
        sop(lambda e: e.activation(rows2[:1, 0:2048], rows2[:1, 0:2048],
                                   AF.Exp, bias=0.0, scale=-0.5))
        for half in range(2):
            h0 = half * 512
            bcast(pp[0][:, :], rows2[:1, h0:h0 + 512])
            bcast(pp[1][:, :], rows2[:1, 1024 + h0:1024 + h0 + 512])
            vop(lambda e, h0=h0: e.tensor_copy(rqbc[:, h0:h0 + 512], pp[0][:, :]))
            vop(lambda e, h0=h0: e.tensor_copy(rkbc[:, h0:h0 + 512], pp[1][:, :]))

    def l0_rope_ak(c):
        C.wait("vector", "LE", lv_tab[0])
        for fc in range(4):
            vop(lambda e, fc=fc: e.tensor_scalar(
                scr_bf[:, 0:1024], k_f[:, fc * 1024:(fc + 1) * 1024],
                pvecs[:, 4 + fc:4 + fc + 1], None, ALU.mult))
            vop(lambda e: e.tensor_mul(scr_bf[:, 0:1024], scr_bf[:, 0:1024],
                                       rkbc[:, 0:1024]))
            for half in range(2):
                mm(pp[half][:, :], PT_bf[:, :],
                   scr_bf[:, half * 512:half * 512 + 512])
            vop(lambda e: e.tensor_mul(scr[:, 0:1024], scr_bf[:, 0:1024], ctab[:, :]))
            vop(lambda e: e.tensor_mul(scrB[:, 0:512], pp[0][:, :], stab[:, 0:512]))
            vop(lambda e: e.tensor_mul(scrB[:, 512:1024], pp[1][:, :],
                                       stab[:, 512:1024]))
            vop(lambda e, fc=fc, c=c: e.tensor_add(
                akbuf[c % 2][:, fc * 1024:(fc + 1) * 1024], scr[:, 0:1024],
                scrB[:, 0:1024]))

    def l0_attention(c):
        for h in range(4):
            vop(lambda e, h=h: e.tensor_scalar(
                scr_bf[:, 0:1024], q_f[:, h * 1024:(h + 1) * 1024],
                pvecs[:, 0 + h:0 + h + 1], None, ALU.mult))
            vop(lambda e: e.tensor_mul(scr_bf[:, 0:1024], scr_bf[:, 0:1024],
                                       rqbc[:, 0:1024]))
            for half in range(2):
                mm(pp[half][:, :], PT_bf[:, :],
                   scr_bf[:, half * 512:half * 512 + 512])
            vop(lambda e: e.tensor_mul(scr[:, 0:1024], scr_bf[:, 0:1024], ctab[:, :]))
            vop(lambda e: e.tensor_mul(scrB[:, 0:512], pp[0][:, :], stab[:, 0:512]))
            vop(lambda e: e.tensor_mul(scrB[:, 512:1024], pp[1][:, :],
                                       stab[:, 512:1024]))
            vop(lambda e: e.tensor_add(aqh[:, :], scr[:, 0:1024], scrB[:, 0:1024]))
            for qt in range(2):
                q0g = c * CH + qt * 512
                subs = []
                for k0 in range(q0g - 1024, q0g + 512, 128):
                    if k0 < 0:
                        continue
                    delta = k0 - q0g
                    if delta <= -640:
                        subs.append((k0, maskW_bf, (delta + 1024) // 128))
                    elif delta < 0:
                        subs.append((k0, None, 0))
                    else:
                        subs.append((k0, maskM_bf, delta // 128))
                nsub = len(subs)
                for i, (k0, mk, j) in enumerate(subs):
                    kc = (k0 // CH) % 2
                    ko = k0 % CH
                    bk = i % 2
                    mm(pp[bk][:, :],
                       akbuf[kc][:, h * 1024 + ko:h * 1024 + ko + 128],
                       aqh[:, qt * 512:qt * 512 + 512])
                    eb = expt[i % 2]
                    sop(lambda e, eb=eb, bk=bk: e.activation(
                        eb[:, :], pp[bk][:, :], AF.Exp, bias=-C_L0, scale=SC0))
                    if mk is not None:
                        vop(lambda e, eb=eb, mk=mk, j=j: e.tensor_mul(
                            eb[:, :], eb[:, :],
                            mk[:, 512 - 128 * j:1024 - 128 * j]))
                    colsum(pp[3][:1, :], eb[:, :], i == 0, i == nsub - 1)
                    mm(pp[2][:, :],
                       vtbuf[kc][:, (ko // 128) * 512 + h * 128:
                                  (ko // 128) * 512 + h * 128 + 128],
                       eb[:, :], i == 0, i == nsub - 1)
                vop(lambda e: e.reciprocal(rows[:1, 0:512], pp[3][:1, :]))
                vop(lambda e: e.tensor_copy(scr[:, 0:512], pp[2][:, :]))
                bcast(pp[4][:, :], rows[:1, 0:512])
                vop(lambda e, h=h, qt=qt: e.tensor_mul(
                    attnT[:, h * 1024 + qt * 512:h * 1024 + qt * 512 + 512],
                    scr[:, 0:512], pp[4][:, :]))

    def l0_derived(c):
        for which, src in [(0, q_f), (1, k_f)]:
            sc_col, of_col = (8, 12) if which == 0 else (16, 20)
            for fc in range(4):
                sop(lambda e, s=src, fc=fc: e.activation(
                    scr[:, 0:1024], s[:, fc * 1024:(fc + 1) * 1024], AF.Sigmoid))
                vop(lambda e, s=src, fc=fc: e.tensor_mul(
                    scr[:, 1024:2048], scr[:, 0:1024],
                    s[:, fc * 1024:(fc + 1) * 1024]))
                vop(lambda e, s=src, fc=fc, sc=sc_col, of=of_col: e.tensor_scalar(
                    s[:, fc * 1024:(fc + 1) * 1024], scr[:, 1024:2048],
                    pvecs[:, sc + fc:sc + fc + 1], pvecs[:, of + fc:of + fc + 1],
                    ALU.mult, ALU.add))
                sop(lambda e, s=src, fc=fc: e.activation(
                    scr_bf[:, 0:1024], s[:, fc * 1024:(fc + 1) * 1024], AF.Square))
                b0 = 2 + which
                colsum(pp[b0][:1, :], scr_bf[:, 0:512], fc == 0, fc == 3)
                colsum(pp[b0 + 2][:1, :], scr_bf[:, 512:1024], fc == 0, fc == 3)
        vop(lambda e: e.tensor_copy(rows[:1, 0:512], pp[2][:1, :]))
        vop(lambda e: e.tensor_copy(rows[:1, 512:1024], pp[4][:1, :]))
        vop(lambda e: e.tensor_copy(rows[:1, 1024:1536], pp[3][:1, :]))
        vop(lambda e: e.tensor_copy(rows[:1, 1536:2048], pp[5][:1, :]))
        sop(lambda e: e.activation(rows[:1, 0:2048], rows[:1, 0:2048],
                                   AF.Ln, bias=1e-24, scale=1.0))
        sop(lambda e: e.activation(rows[:1, 0:2048], rows[:1, 0:2048],
                                   AF.Exp, bias=0.0, scale=0.5))
        sop(lambda e: e.add(rows[:1, 0:2048], rows[:1, 0:2048], 1e-12))
        vop(lambda e: e.reciprocal(rows2[:1, 0:2048], rows[:1, 0:2048]))
        for which, src in [(0, q_f), (1, k_f)]:
            for half in range(2):
                bcast(pp[half][:, :],
                      rows2[:1, which * 1024 + half * 512:
                            which * 1024 + half * 512 + 512])
            for fc in range(4):
                for half in range(2):
                    o = fc * 1024 + half * 512
                    vop(lambda e, s=src, o=o, half=half: e.tensor_mul(
                        s[:, o:o + 512], s[:, o:o + 512], pp[half][:, :]))
        for dc in range(4):
            for tt in range(8):
                tb = 3 if tt % 2 == 0 else 4
                transpose(ppb[tb][:, 0:128],
                          k_f[:, dc * 1024 + tt * 128:dc * 1024 + tt * 128 + 128])
                vop(lambda e, tb=tb, tt=tt, dc=dc: e.tensor_copy(
                    fkt[:, tt * 512 + dc * 128:tt * 512 + dc * 128 + 128],
                    ppb[tb][:, 0:128]))
        for tt in range(8):
            sop(lambda e, tt=tt, c=c: e.activation(
                scr[:, 0:512], vtbuf[c % 2][:, tt * 512:(tt + 1) * 512], AF.Sigmoid))
            vop(lambda e, tt=tt, c=c: e.scalar_tensor_tensor(
                scrB[:, 0:512], scr[:, 0:512], lr_sb[:, tt * 3 + 1:tt * 3 + 2],
                vtbuf[c % 2][:, tt * 512:(tt + 1) * 512], ALU.mult, ALU.mult))
            vop(lambda e, tt=tt: e.tensor_copy(fvl1[:, tt * 512:(tt + 1) * 512],
                                               scrB[:, 0:512]))

    def l0_ttt(c):
        for dst, src in [(sbw0, st_w0), (sbw2, st_w2), (sbw1, st_w1)]:
            vop(lambda e, d=dst, s=src: e.tensor_copy(d, s))
        for tt in range(8):
            for bk, lhs_buf, rhs in [(0, k_f, sbw0), (1, k_f, sbw2), (2, fvf, sbw1)]:
                for dc in range(4):
                    mm(pp[bk][:, :],
                       lhs_buf[:, dc * 1024 + tt * 128:dc * 1024 + tt * 128 + 128],
                       rhs[:, dc * 512:(dc + 1) * 512], dc == 0, dc == 3)
            sop(lambda e: e.activation(scr[:, 0:512], pp[0][:, :], AF.Sigmoid))
            vop(lambda e: e.tensor_mul(scr[:, 512:1024], scr[:, 0:512], pp[0][:, :]))
            vop(lambda e: e.tensor_sub(scr[:, 1024:1536], pp[0][:, :],
                                       scr[:, 512:1024]))
            vop(lambda e: e.tensor_mul(scr[:, 1536:2048], scr[:, 0:512],
                                       scr[:, 1024:1536]))
            vop(lambda e: e.tensor_add(scr[:, 1536:2048], scr[:, 1536:2048],
                                       scr[:, 0:512]))
            vop(lambda e: e.tensor_copy(scrB[:, 0:512], pp[1][:, :]))
            vop(lambda e: e.tensor_mul(scrB[:, 512:1024], pp[2][:, :], scrB[:, 0:512]))
            vop(lambda e, tt=tt: e.tensor_mul(hid[:, tt * 512:(tt + 1) * 512],
                                              scr[:, 512:1024], scrB[:, 0:512]))
            vop(lambda e, tt=tt: e.scalar_tensor_tensor(
                dgate0[:, tt * 512:(tt + 1) * 512], scrB[:, 512:1024],
                lr_sb[:, tt * 3:tt * 3 + 1], scr[:, 1536:2048], ALU.mult, ALU.mult))
            vop(lambda e, tt=tt: e.scalar_tensor_tensor(
                dhpre2[:, tt * 512:(tt + 1) * 512], pp[2][:, :],
                lr_sb[:, tt * 3 + 2:tt * 3 + 3], scr[:, 512:1024],
                ALU.mult, ALU.mult))
        for st_f, lhs_buf, rhs_buf in [(st_w0, fkt, dgate0), (st_w2, fkt, dhpre2),
                                       (st_w1, fvl1, hid)]:
            for mc in range(4):
                bk = mc % 2
                for tt in range(8):
                    mm(pp[bk][:, :],
                       lhs_buf[:, tt * 512 + mc * 128:tt * 512 + mc * 128 + 128],
                       rhs_buf[:, tt * 512:(tt + 1) * 512], tt == 0, tt == 7)
                vop(lambda e, s=st_f, mc=mc, bk=bk: e.tensor_add(
                    s[:, mc * 512:(mc + 1) * 512], s[:, mc * 512:(mc + 1) * 512],
                    pp[bk][:, :]))
        for dst, src in [(sbw0, st_w0), (sbw2, st_w2), (sbw1, st_w1)]:
            vop(lambda e, d=dst, s=src: e.tensor_copy(d, s))
        for ocn in range(4):
            for hcn in range(4):
                tb = 3 if (ocn * 4 + hcn) % 2 == 0 else 4
                transpose(ppb[tb][:, 0:128],
                          sbw1[:, ocn * 512 + hcn * 128:ocn * 512 + hcn * 128 + 128])
                vop(lambda e, tb=tb, ocn=ocn, hcn=hcn: e.tensor_copy(
                    w1NT[:, hcn * 512 + ocn * 128:hcn * 512 + ocn * 128 + 128],
                    ppb[tb][:, 0:128]))
        for mh in range(4):
            for nh in range(2):
                for dc in range(4):
                    mm(pp[0][:, :],
                       sbw0[:, dc * 512 + mh * 128:dc * 512 + mh * 128 + 128],
                       q_f[:, dc * 1024 + nh * 512:dc * 1024 + nh * 512 + 512],
                       dc == 0, dc == 3)
                    mm(pp[1][:, :],
                       sbw2[:, dc * 512 + mh * 128:dc * 512 + mh * 128 + 128],
                       q_f[:, dc * 1024 + nh * 512:dc * 1024 + nh * 512 + 512],
                       dc == 0, dc == 3)
                sop(lambda e: e.activation(scr[:, 0:512], pp[0][:, :], AF.Sigmoid))
                vop(lambda e: e.tensor_mul(scr[:, 512:1024], scr[:, 0:512],
                                           pp[0][:, :]))
                vop(lambda e, mh=mh, nh=nh: e.tensor_mul(
                    hq[:, mh * 1024 + nh * 512:mh * 1024 + nh * 512 + 512],
                    scr[:, 512:1024], pp[1][:, :]))
        for nh in range(2):
            for oc in range(4):
                bk = oc % 2
                for hc in range(4):
                    mm(pp[bk][:, :],
                       w1NT[:, hc * 512 + oc * 128:hc * 512 + oc * 128 + 128],
                       hq[:, hc * 1024 + nh * 512:hc * 1024 + nh * 512 + 512],
                       hc == 0, hc == 3)
                sop(lambda e, bk=bk: e.activation(scr_bf[:, 0:512], pp[bk][:, :],
                                                  AF.Square))
                vop(lambda e, oc=oc, nh=nh, bk=bk: e.tensor_copy(
                    oisb[:, oc * 1024 + nh * 512:oc * 1024 + nh * 512 + 512],
                    pp[bk][:, :]))
                colsum(pp[5][:1, :], scr_bf[:, 0:512], oc == 0, oc == 3)
            sop(lambda e: e.activation(rows[:1, 0:512], pp[5][:1, :],
                                       AF.Ln, bias=1e-6, scale=1.0 / 512.0))
            sop(lambda e: e.activation(rows[:1, 0:512], rows[:1, 0:512],
                                       AF.Exp, bias=0.0, scale=-0.5))
            bcast(pp[6][:, :], rows[:1, 0:512])
            for oc in range(4):
                vop(lambda e, oc=oc, nh=nh: e.scalar_tensor_tensor(
                    scr[:, 0:512],
                    oisb[:, oc * 1024 + nh * 512:oc * 1024 + nh * 512 + 512],
                    pvecs[:, 24 + oc:24 + oc + 1], pp[6][:, :], ALU.mult, ALU.mult))
                vv = vop(lambda e, oc=oc, nh=nh: e.tensor_add(
                    scr_bf[:, 1024:1536], scr[:, 0:512],
                    attnT[:, oc * 1024 + nh * 512:oc * 1024 + nh * 512 + 512]))
                C.wait("gpsimd", "V", vv)
                gv = dma("gpsimd", ag_mix_in[c][oc * 128:(oc + 1) * 128,
                                                nh * 512:nh * 512 + 512],
                         scr_bf[:, 1024:1536])
                C.wait("vector", "G", gv)
        C.wait("gpsimd", "G", C.cnt["G"])
        return collective("AllGather", ALU.bypass, ag_mix_in[c], ag_mix_out[c])

    cv_ags = [0, 0, 0, 0]
    for c in range(NCH):
        l0_qkv(c)
        if stop_after == f"l0q{c}":
            break
        if "stats" not in skip:
            l0_stats_ar(c)
        if "rope" not in skip:
            l0_rope_ak(c)
        if "attn" not in skip:
            l0_attention(c)
        if "derived" not in skip:
            l0_derived(c)
        if "ttt" not in skip:
            cv_ags[c] = l0_ttt(c)
        if stop_after == f"l0c{c}":
            break

    def finish():
        C.fence()
        gv = dma("gpsimd", y_out[0:128, :], ARN[:, 0:1024])
        C.wait("gpsimd", "G", gv)
        C.replay(es.enter_context(nc.Block()))
        es.close()
        return nc

    if stop_after.startswith("l0"):
        dump("x0bf", x0bf, 2048, 512)
        dump("qf", q_f, 512, 1024)
        dump("kf", k_f, 512, 1024)
        dump("fvf", fvf, 512, 1024)
        dump("attnT", attnT, 512, 1024)
        dump("ak", akbuf[(NCH - 1) % 2] if stop_after == "l0c3" else akbuf[0],
             512, 1024)
        dump("lr", lr_sb, 128, 24)
        dump("oisb", oisb, 512, 1024)
        dump("stw0", st_w0, 128, 2048)
        return finish()

    AL.pop()
    AL.push()
    slotA = AL.alloc(16 * 1024, BF16)
    slotB = AL.alloc(16 * 1024, BF16)
    slotB_at = AL.last_at
    AL.push()
    mixT = AL.alloc(16 * 1024, BF16)
    mstage = [AL.alloc(1024, BF16), AL.alloc(1024, BF16)]
    wstage = [AL.alloc(2048), AL.alloc(2048)]
    w_bf2 = [AL.alloc(2048, BF16), AL.alloc(2048, BF16)]
    x0own = [AL.alloc(1024), AL.alloc(1024)]
    sq_bf = AL.alloc(1024, BF16)
    mbc = AL.alloc(1024)
    rbc = AL.alloc(1024)
    scrA = AL.alloc(1024)
    W_PEAK = AL.off

    def phase_w():
        C.fence(("sync",))
        msel_t = {0: 0, 1: 0}
        for r in range(4):
            C.wait("sync", "CCA", cv_ags[r])
            for k in range(16):
                sb2 = mstage[k % 2]
                if r > 0 or k >= 2:
                    C.wait("sync", "V", msel_t[k % 2])
                lv = dma("sync", sb2, ag_mix_out[r][k * 128:(k + 1) * 128, :],
                         "LM%d" % (k % 2))
                if r == 0:
                    msel_t[k % 2] = vopx(lambda e, k=k, s=sb2: e.tensor_scalar(
                        mixT[:, k * 1024:(k + 1) * 1024], s, msel[:, 0:1],
                        None, ALU.mult), [("LM%d" % (k % 2), lv)])
                else:
                    msel_t[k % 2] = vopx(lambda e, k=k, r=r, s=sb2:
                                         e.scalar_tensor_tensor(
                        mixT[:, k * 1024:(k + 1) * 1024], s, msel[:, r:r + 1],
                        mixT[:, k * 1024:(k + 1) * 1024], ALU.mult, ALU.add),
                        [("LM%d" % (k % 2), lv)])
        C.wait("sync", "T", C.cnt["T"])
        cast_w, mmg, addm = {}, {}, {}
        addt = {0: 0, 1: 0}
        for m_ in range(16):
            e_ = m_ % 2
            sl = wstage[e_]
            if m_ >= 2:
                C.wait("sync", "V", cast_w[m_ - 2])
                C.wait("sync", "V", addm[(m_ - 2, 1)])
            dma("sync", sl, din["WoT_p"][:, m_ * 2048:(m_ + 1) * 2048],
                "LW%d" % e_)
            lv = dma("sync", x0own[e_], din["x0T_own"][m_ * 128:(m_ + 1) * 128, :],
                     "LW%d" % e_)
            cwt = [("LW%d" % e_, lv)]
            if m_ >= 2:
                cwt.append(("T", mmg[(m_ - 2, 1)]))
            cast_w[m_] = vopx(lambda e, sl=sl, w=w_bf2[e_]: e.tensor_copy(w, sl),
                              cwt)
            for nh in range(2):
                wt0 = [("V", cast_w[m_])]
                if m_ >= 1:
                    wt0.append(("V", addt[nh]))
                else:
                    wt0.append(("V", max(msel_t[0], msel_t[1])))
                for k in range(16):
                    mmg[(m_, nh)] = mmx(
                        pp[nh][:, :], w_bf2[e_][:, k * 128:(k + 1) * 128],
                        mixT[:, k * 1024 + nh * 512:k * 1024 + nh * 512 + 512],
                        k == 0, k == 15, wt0 if k == 0 else ())
            for nh in range(2):
                addt[nh] = vopx(lambda e, m_=m_, nh=nh, e_=e_: e.tensor_add(
                    slotA[:, m_ * 1024 + nh * 512:m_ * 1024 + nh * 512 + 512],
                    pp[nh][:, :], x0own[e_][:, nh * 512:nh * 512 + 512]),
                    [("T", mmg[(m_, nh)])])
                addm[(m_, nh)] = addt[nh]
        layer_norm(slotA, sq_bf, mbc, rbc, scrA)

    phase_w()
    if stop_after == "xA":
        dump("xA", slotA, 2048, 1024)
        return finish()

    AL.pop()
    AL.push()
    stage4 = [AL.alloc(2048), AL.alloc(2048), AL.alloc(2048), AL.alloc(2048)]
    w13 = AL.alloc(4 * 2048, BF16)
    w2sl = AL.alloc(4 * 2048, BF16)
    hgrp = AL.alloc(4 * 1024, BF16)
    f2t = [AL.alloc(1024, BF16), AL.alloc(1024, BF16)]
    sq_bf0 = AL.alloc(1024, BF16)
    mbc0 = AL.alloc(1024)
    rbc0 = AL.alloc(1024)
    scr0 = AL.alloc(2048)
    scr0bf = AL.alloc(2048, BF16)
    FFN0_PEAK = AL.off

    def phase_ffn0():
        C.fence(("sync",))
        vop(lambda e: e.memset(slotB[:, :], 0.0))
        C.fence()
        cast2, mmg, sig, evac, h128t, cpy = {}, {}, {}, {}, {}, {}
        gv_prev = {0: 0, 1: 0}
        for m_ in range(64):
            e_ = m_ % 2
            s1, s3 = stage4[e_ * 2], stage4[e_ * 2 + 1]
            wA = w13[:, e_ * 4096:e_ * 4096 + 2048]
            wB = w13[:, e_ * 4096 + 2048:e_ * 4096 + 4096]
            if m_ >= 2:
                C.wait("sync", "V", cast2[m_ - 2])
            dma("sync", s1, din["W1_p"][:, m_ * 2048:(m_ + 1) * 2048],
                "LW%d" % e_)
            lv = dma("sync", s3, din["W3_p"][:, m_ * 2048:(m_ + 1) * 2048],
                     "LW%d" % e_)
            wts = [("LW%d" % e_, lv)]
            if m_ >= 2:
                wts.append(("T", mmg[(m_ - 2, 1)]))
            vopx(lambda e, s=s1, w=wA: e.tensor_copy(w, s), wts)
            cast2[m_] = vopx(lambda e, s=s3, w=wB: e.tensor_copy(w, s))
            for nh in range(2):
                b0, b1, bh = nh * 2, nh * 2 + 1, 4 + nh
                o5 = nh * 1024
                wt0 = [("V", cast2[m_])]
                if m_ >= 1:
                    wt0 += [("V", evac[(m_ - 1, nh)]), ("S", sig[(m_ - 1, nh)])]
                for k in range(16):
                    mmx(pp[b0][:, :], wA[:, k * 128:(k + 1) * 128],
                        slotA[:, k * 1024 + nh * 512:k * 1024 + nh * 512 + 512],
                        k == 0, k == 15, wt0 if k == 0 else ())
                for k in range(16):
                    mmg[(m_, nh)] = mmx(
                        pp[b1][:, :], wB[:, k * 128:(k + 1) * 128],
                        slotA[:, k * 1024 + nh * 512:k * 1024 + nh * 512 + 512],
                        k == 0, k == 15)
                swt = [("T", mmg[(m_, nh)])]
                if m_ >= 1:
                    swt.append(("V", evac[(m_ - 1, nh)]))
                sig[(m_, nh)] = sopx(lambda e, b=b0, o5=o5: e.activation(
                    scr0[:, o5:o5 + 512], pp[b][:, :], AF.Sigmoid), swt)
                vopx(lambda e, b=b0, o5=o5: e.tensor_mul(
                    scr0[:, o5 + 512:o5 + 1024], scr0[:, o5:o5 + 512],
                    pp[b][:, :]), [("T", mmg[(m_, nh)]), ("S", sig[(m_, nh)])])
                evac[(m_, nh)] = vopx(lambda e, b=b1, o5=o5: e.tensor_mul(
                    scr0bf[:, o5:o5 + 512], scr0[:, o5 + 512:o5 + 1024],
                    pp[b][:, :]))
                hwt = [("V", evac[(m_, nh)])]
                if m_ >= 1:
                    hwt.append(("V", cpy[(m_ - 1, nh)]))
                h128t[(m_, nh)] = mmx(pp[bh][:, :], H128_bf[:, :],
                                      scr0bf[:, o5:o5 + 512], True, True, hwt)
                gs = "GA" if nh == 0 else "GB"
                cpy[(m_, nh)] = vopx(lambda e, b=bh, o5=o5: e.tensor_copy(
                    scr0bf[:, o5 + 512:o5 + 1024], pp[b][:, :]),
                    [("T", h128t[(m_, nh)]), (gs, gv_prev[nh])])
                C.wait("gpsimd", "V", cpy[(m_, nh)])
                for l in range(2):
                    def g_sh(e, m_=m_, l=l, nh=nh, o5=o5):
                        dstap = shufd[m_ * 2 + l:m_ * 2 + l + 1,
                                      0:64 * 1024].rearrange(
                            "p (u hh t) -> (p u) hh t", hh=2, t=512)
                        return e.dma_start(out=dstap[:, nh, :],
                                           in_=scr0bf[l:128:2, o5 + 512:o5 + 1024])
                    C.op("gpsimd", g_sh)
                    gv = C.inc("gpsimd", gs, 16)
                gv_prev[nh] = gv
        C.fence()
        w2c, hgt, h64 = {}, {}, {}
        odlast = {}
        oduse = {0: 0, 1: 0, 2: 0, 3: 0}
        f2use = {0: 0, 1: 0}
        for grp in range(16):
            for j in range(4):
                m2 = grp * 4 + j
                jj = j % 2
                fb = f2t[jj]
                prev = (grp, j - 2) if j >= 2 else (grp - 1, j + 2)
                C.wait("sync", "T", f2use[jj])
                C.wait("sync", "V", w2c.get(prev, 0))
                lv = dma("sync", fb, shufd[:, m2 * 1024:(m2 + 1) * 1024],
                         "LX%d" % jj)
                lv2 = dma("sync", stage4[2 + jj],
                          din["W2Tp"][m2 * 128:(m2 + 1) * 128, :],
                          "LW%d" % jj)
                cwt = [("LW%d" % jj, lv2)]
                if grp > 0:
                    cwt.append(("T", odlast[grp - 1]))
                w2c[(grp, j)] = vopx(lambda e, j=j, s=stage4[2 + jj]:
                                     e.tensor_copy(
                    w2sl[:, j * 2048:(j + 1) * 2048], s), cwt)
                t0 = mmx(pp[6 + jj][:, :], H64I2_bf[:, :], fb[:, 0:512],
                         True, True,
                         [("LX%d" % jj, lv), ("V", hgt.get(prev, 0))])
                hwt = [("T", t0)]
                if grp > 0:
                    hwt.append(("T", odlast[grp - 1]))
                g0 = vopx(lambda e, j=j, jj=jj: e.tensor_scalar(
                    hgrp[:, j * 1024:j * 1024 + 512],
                    pp[6 + jj][:, :], R8192, None, ALU.mult), hwt)
                t1 = mmx(pp[6 + jj][:, :], H64I2_bf[:, :], fb[:, 512:1024],
                         True, True, [("V", g0)])
                hgt[(grp, j)] = vopx(lambda e, j=j, jj=jj: e.tensor_scalar(
                    hgrp[:, j * 1024 + 512:j * 1024 + 1024],
                    pp[6 + jj][:, :], R8192, None, ALU.mult), [("T", t1)])
                f2use[jj] = t1
            for od in range(16):
                for nh in range(2):
                    bk = (od * 2 + nh) % 4
                    gwt = [("V", oduse[bk])]
                    if od == 0 and nh == 0:
                        gwt += [("V", w2c[(grp, 3)]), ("V", hgt[(grp, 3)])]
                    for j in range(4):
                        tvx = mmx(
                            pp[bk][:, :],
                            w2sl[:, j * 2048 + od * 128:j * 2048 + od * 128 + 128],
                            hgrp[:, j * 1024 + nh * 512:j * 1024 + nh * 512 + 512],
                            j == 0, j == 3, gwt if j == 0 else ())
                    odlast[grp] = tvx
                    oduse[bk] = vopx(lambda e, od=od, nh=nh, bk=bk: e.tensor_add(
                        slotB[:, od * 1024 + nh * 512:od * 1024 + nh * 512 + 512],
                        slotB[:, od * 1024 + nh * 512:od * 1024 + nh * 512 + 512],
                        pp[bk][:, :]), [("T", tvx)])
        C.fence()
        for fc in range(16):
            vop(lambda e, fc=fc: e.tensor_add(
                slotB[:, fc * 1024:(fc + 1) * 1024],
                slotB[:, fc * 1024:(fc + 1) * 1024],
                slotA[:, fc * 1024:(fc + 1) * 1024]))
        layer_norm(slotB, sq_bf0, mbc0, rbc0, scr0[:, 0:1024])

    phase_ffn0()
    if stop_after == "xB":
        dump("xB", slotB, 2048, 1024)
        return finish()

    AL.pop()
    AL.push()
    q1 = AL.alloc(8 * 1024, BF16)
    k1 = AL.alloc(4 * 1536, BF16)
    v1t = AL.alloc(4 * 1536, BF16)
    wstage1 = [AL.alloc(2048), AL.alloc(2048)]
    w1bf2 = [AL.alloc(2048, BF16), AL.alloc(2048, BF16)]
    c1tab = AL.alloc(1024)
    s1tab = AL.alloc(1024)
    s1tab_at = AL.last_at
    kvstage = [AL.alloc(512, BF16), AL.alloc(512, BF16)]
    scr1 = AL.alloc(2048)
    scr1b = AL.alloc(1024)
    scr1bf = AL.alloc(1536, BF16)
    expt1 = [AL.alloc(512, BF16), AL.alloc(512, BF16)]
    sq_bf1 = AL.alloc(1024, BF16)
    mbc1 = AL.alloc(1024)
    rbc1 = AL.alloc(1024)
    L1A_PEAK = AL.off

    def phase_l1a():
        C.fence(("sync",))
        dma("sync", c1tab, din["cos1"][:, :], "LE")
        lv = dma("sync", s1tab, din["sin1"][:, :], "LE")
        C.wait("vector", "LE", lv)
        C.wait("tensor", "LE", lv)
        qst = {"hist": [], "cast": {}, "mmg": {}, "evac": {}, "ptv": 0,
               "sbfT": 0, "tpv": {3: 0, 4: 0}}

        def slab_load(m_, dname):
            h = qst["hist"]
            uid = len(h)
            p2 = h[-2] if uid >= 2 else None
            p1 = h[-1] if uid >= 1 else None
            h.append(uid)
            e_ = uid % 2
            sl = wstage1[e_]
            if p2 is not None:
                C.wait("sync", "V", qst["cast"][p2])
            lv = dma("sync", sl, din[dname][:, m_ * 2048:(m_ + 1) * 2048],
                     "LW%d" % e_)
            cwt = [("LW%d" % e_, lv)]
            if p2 is not None:
                cwt.append(("T", qst["mmg"][p2]))
            qst["cast"][uid] = vopx(lambda e, s=sl, w=w1bf2[e_]:
                                    e.tensor_copy(w, s), cwt)
            return uid, e_, p1

        def qkv1_slab(m_):
            uid, e_, p1 = slab_load(m_, "wqkv1_p")
            for nh in range(2):
                wt0 = [("V", qst["cast"][uid])]
                if p1 is not None:
                    wt0.append(("V", qst["evac"][p1]))
                for k in range(16):
                    qst["mmg"][uid] = mmx(
                        pp[nh][:, :], w1bf2[e_][:, k * 128:(k + 1) * 128],
                        slotB[:, k * 1024 + nh * 512:k * 1024 + nh * 512 + 512],
                        k == 0, k == 15, wt0 if k == 0 else ())
            return uid

        def rope_post(uid, dst_ap):
            cwt = [("T", qst["mmg"][uid]), ("T", qst["sbfT"])]
            vopx(lambda e: e.tensor_copy(scr1bf[:, 0:512], pp[0][:, :]), cwt)
            ev = vopx(lambda e: e.tensor_copy(scr1bf[:, 512:1024], pp[1][:, :]))
            qst["evac"][uid] = ev
            ptl = 0
            for half in range(2):
                ptl = mmx(pp[2 + half][:, :], PT_bf[:, :],
                          scr1bf[:, half * 512:half * 512 + 512], True, True,
                          [("V", ev), ("V", qst["ptv"])])
            qst["sbfT"] = ptl
            vopx(lambda e: e.tensor_mul(scr1[:, 0:1024], scr1bf[:, 0:1024],
                                        c1tab[:, :]))
            vopx(lambda e: e.tensor_mul(scr1b[:, 0:512], pp[2][:, :],
                                        s1tab[:, 0:512]), [("T", ptl)])
            vopx(lambda e: e.tensor_mul(scr1b[:, 512:1024], pp[3][:, :],
                                        s1tab[:, 512:1024]))
            qst["ptv"] = vopx(lambda e, d=dst_ap: e.tensor_add(
                d, scr1[:, 0:1024], scr1b[:, 0:1024]))

        C.fence()
        for m_ in range(16, 24):
            uid = qkv1_slab(m_)
            kvh = (m_ - 16) % 4
            if m_ < 20:
                rope_post(uid, k1[:, kvh * 1536 + 512:kvh * 1536 + 1536])
            else:
                cwt = [("T", qst["mmg"][uid]), ("T", qst["sbfT"])]
                vopx(lambda e: e.tensor_copy(scr1bf[:, 0:512], pp[0][:, :]), cwt)
                ev = vopx(lambda e: e.tensor_copy(scr1bf[:, 512:1024],
                                                  pp[1][:, :]))
                qst["evac"][uid] = ev
                tpt = 0
                for sub in range(8):
                    tb = 3 if sub % 2 == 0 else 4
                    tpt = C.opx("tensor", lambda e, tb=tb, sub=sub: e.transpose(
                        ppb[tb][:, 0:128],
                        scr1bf[:, sub * 128:sub * 128 + 128], id_bf[:, :]),
                        [("V", ev), ("V", qst["tpv"][tb])])
                    qst["tpv"][tb] = vopx(lambda e, tb=tb, kvh=kvh, sub=sub:
                                          e.tensor_copy(
                        v1t[:, kvh * 1536 + 512 + sub * 128:
                            kvh * 1536 + 512 + sub * 128 + 128],
                        ppb[tb][:, 0:128]), [("T", tpt)])
                qst["sbfT"] = tpt
        C.fence(("gpsimd",))
        for kvh in range(4):
            dma("gpsimd", ag_kv_in[kvh * 128:(kvh + 1) * 128, :],
                k1[:, kvh * 1536 + 1024:kvh * 1536 + 1536])
            dma("gpsimd", ag_kv_in[(4 + kvh) * 128:(5 + kvh) * 128, :],
                v1t[:, kvh * 1536 + 1024:kvh * 1536 + 1536])
        C.wait("gpsimd", "G", C.cnt["G"])
        cv_kv = collective("AllGather", ALU.bypass, ag_kv_in, ag_kv_out, "CCA")

        def q_group(g2):
            for hl in range(8):
                uid = qkv1_slab(g2 * 8 + hl)
                rope_post(uid, q1[:, hl * 1024:(hl + 1) * 1024])

        def halo_select():
            C.wait("sync", "CCA", cv_kv)
            ks_t = {0: 0, 1: 0}
            for r in range(4):
                for ch in range(8):
                    stg = kvstage[ch % 2]
                    if r > 0 or ch >= 2:
                        C.wait("sync", "V", ks_t[ch % 2])
                    lv = dma("sync", stg,
                             ag_kv_out[r * 1024 + ch * 128:
                                       r * 1024 + (ch + 1) * 128, :],
                             "LX%d" % (ch % 2))
                    C.wait("vector", "LX%d" % (ch % 2), lv)
                    if ch < 4:
                        dst = k1[:, ch * 1536:ch * 1536 + 512]
                    else:
                        dst = v1t[:, (ch - 4) * 1536:(ch - 4) * 1536 + 512]
                    if r == 0:
                        ks_t[ch % 2] = vop(lambda e, d=dst, s=stg: e.tensor_scalar(
                            d, s, hsel[:, 0:1], None, ALU.mult))
                    else:
                        ks_t[ch % 2] = vop(
                            lambda e, d=dst, s=stg, r=r: e.scalar_tensor_tensor(
                                d, s, hsel[:, r:r + 1], d, ALU.mult, ALU.add))

        def l1_attention(g2):
            for hl in range(8):
                h = g2 * 8 + hl
                kvh = h // 4
                for qt in range(2):
                    q0l = qt * 512
                    subs = []
                    for k0 in range(q0l - 512, q0l + 512, 128):
                        delta = k0 - q0l
                        if delta < 0:
                            mk = maskWc_bf if k0 < 0 else maskW_bf
                            subs.append((k0, mk, (delta + 512) // 128))
                        else:
                            subs.append((k0, maskM_bf, delta // 128))
                    nsub = len(subs)
                    for i, (k0, mk, j) in enumerate(subs):
                        kb = k0 + 512
                        bk = i % 2
                        mm(pp[bk][:, :],
                           k1[:, kvh * 1536 + kb:kvh * 1536 + kb + 128],
                           q1[:, hl * 1024 + q0l:hl * 1024 + q0l + 512])
                        eb = expt1[i % 2]
                        sop(lambda e, eb=eb, bk=bk: e.activation(
                            eb[:, :], pp[bk][:, :], AF.Exp, bias=-C_L1, scale=SC0))
                        vop(lambda e, eb=eb, mk=mk, j=j: e.tensor_mul(
                            eb[:, :], eb[:, :], mk[:, 512 - 128 * j:1024 - 128 * j]))
                        colsum(pp[3][:1, :], eb[:, :], i == 0, i == nsub - 1)
                        mm(pp[2][:, :],
                           v1t[:, kvh * 1536 + (kb // 128) * 128:
                               kvh * 1536 + (kb // 128) * 128 + 128],
                           eb[:, :], i == 0, i == nsub - 1)
                    vop(lambda e: e.reciprocal(rows[:1, 0:512], pp[3][:1, :]))
                    vop(lambda e: e.tensor_copy(scr1[:, 0:512], pp[2][:, :]))
                    bcast(pp[4][:, :], rows[:1, 0:512])
                    vop(lambda e, hl=hl, q0l=q0l: e.tensor_mul(
                        q1[:, hl * 1024 + q0l:hl * 1024 + q0l + 512],
                        scr1[:, 0:512], pp[4][:, :]))

        def wo1_partial(g2):
            addw = {}
            for m_ in range(16):
                uid, e_, p1 = slab_load(m_, "wo1_p")
                for nh in range(2):
                    wt0 = [("V", qst["cast"][uid]), ("V", C.cnt["V"])]
                    if p1 is not None:
                        wt0.append(("V", qst["evac"][p1]))
                    for kl in range(8):
                        k = g2 * 8 + kl
                        qst["mmg"][uid] = mmx(
                            pp[nh][:, :], w1bf2[e_][:, k * 128:(k + 1) * 128],
                            q1[:, kl * 1024 + nh * 512:kl * 1024 + nh * 512 + 512],
                            kl == 0, kl == 7, wt0 if kl == 0 else ())
                for nh in range(2):
                    if g2 == 0:
                        addw[nh] = vopx(lambda e, m_=m_, nh=nh: e.tensor_add(
                            slotA[:, m_ * 1024 + nh * 512:
                                  m_ * 1024 + nh * 512 + 512],
                            pp[nh][:, :],
                            slotB[:, m_ * 1024 + nh * 512:
                                  m_ * 1024 + nh * 512 + 512]),
                            [("T", qst["mmg"][uid])])
                    else:
                        addw[nh] = vopx(lambda e, m_=m_, nh=nh: e.tensor_add(
                            slotA[:, m_ * 1024 + nh * 512:
                                  m_ * 1024 + nh * 512 + 512],
                            slotA[:, m_ * 1024 + nh * 512:
                                  m_ * 1024 + nh * 512 + 512],
                            pp[nh][:, :]), [("T", qst["mmg"][uid])])
                qst["evac"][uid] = addw[1]

        q_group(0)
        halo_select()
        l1_attention(0)
        wo1_partial(0)
        q_group(1)
        marks["wm"] = dict(C.cnt)
        l1_attention(1)
        wo1_partial(1)
        layer_norm(slotA, sq_bf1, mbc1, rbc1, scr1[:, 0:1024])

    phase_l1a()
    if stop_after == "xC":
        dump("xC", slotA, 2048, 1024)
        return finish()

    AL.pop()
    AL.push()
    xq = AL.alloc(16 * 1024, BF16)
    stageF = [AL.alloc(2048), AL.alloc(2048)]
    wslbf = AL.alloc(4 * 2048, BF16)
    scFbf = AL.alloc(2048, BF16)
    hgrp2 = AL.alloc(4 * 1024, BF16)
    f2b = [AL.alloc(1024, BF16), AL.alloc(1024, BF16)]
    scbc = AL.alloc(1024)
    invbc = AL.alloc(1024)
    scF = AL.alloc(2048)
    mxtab = AL.alloc(256)
    i32t = AL.alloc(1024, I32)
    sq_bf2 = AL.alloc(1024, BF16)
    mbc2 = AL.alloc(1024)
    rbc2 = AL.alloc(1024)
    outf = [AL.alloc(1024), AL.alloc(1024)]
    L1F_PEAK = AL.off

    wm_stage = [AL.view(slotB_at, 2048, F32),
                AL.view(slotB_at + 8192, 2048, F32)]
    wm_cols = AL.view(s1tab_at, 64, F32)

    def wmean_scale(dram, nld, slot, entry=None):
        wm_t = {0: 0, 1: 0}
        for r in range(nld):
            buf = wm_stage[r % 2]
            if r >= 2:
                C.wait("sync", "V", wm_t[r % 2])
            elif entry is not None:
                for sem in ("T", "V", "S"):
                    C.wait("sync", sem, entry[sem])
            lv = dma("sync", buf, dram[r], "LM%d" % (r % 2))
            wm_t[r % 2] = vopx(lambda e, b=buf, r=r: e.tensor_reduce(
                wm_cols[:, r:r + 1], b, AX.X, ALU.add,
                apply_absolute_value=True), [("LM%d" % (r % 2), lv)])
        sop(lambda e, nld=nld: e.add(wm_cols[:, 0:nld], wm_cols[:, 0:nld], 0.0))
        vop(lambda e, nld=nld: e.tensor_reduce(
            wm_cols[:, 40:41], wm_cols[:, 0:nld], AX.X, ALU.add))
        mm(pp[0][:1, 0:1], onesc_f[:, 0:1], wm_cols[:, 40:41])
        vop(lambda e: e.tensor_copy(rows[:1, 1024:1025], pp[0][:1, 0:1]))
        sop(lambda e: e.activation(rows[:1, 1025:1026], rows[:1, 1024:1025],
                                   AF.Ln, bias=0.0, scale=1.0 / 8388608.0))
        vop(lambda e: e.tensor_scalar(rows[:1, 1026:1027], rows[:1, 1025:1026],
                                      math.log(1e-5), None, ALU.max))
        sop(lambda e: e.activation(rows[:1, 4:5], rows[:1, 1026:1027],
                                   AF.Exp, bias=0.0, scale=-1.0))
        sop(lambda e: e.activation(rows[:1, 5:6], rows[:1, 1026:1027],
                                   AF.Exp, bias=math.log(0.25), scale=2.0))
        sop(lambda e: e.activation(rows[:1, 6:7], rows[:1, 1026:1027],
                                   AF.Exp, bias=math.log(0.5), scale=1.0))
        bcast(pp[0][:, 0:3], rows[:1, 4:7])
        vop(lambda e, slot=slot: e.tensor_copy(srow[:, slot:slot + 1],
                                               pp[0][:, 0:1]))
        vop(lambda e, slot=slot: e.tensor_copy(srow[:, slot + 1:slot + 2],
                                               pp[0][:, 1:2]))
        vop(lambda e, slot=slot: e.tensor_copy(srow[:, slot + 2:slot + 3],
                                               pp[0][:, 2:3]))

    def fwht_rows_from_maxcols(nsc, stride):
        for tt in range(8):
            vop(lambda e, tt=tt, st=stride: e.tensor_reduce(
                mxacc[:, tt:tt + 1], mxtab[:, tt * st:(tt + 1) * st],
                AX.X, ALU.max))
        for tt in range(8):
            sop(lambda e, tt=tt: e.add(scFbf[:, tt:tt + 1],
                                       mxacc[:, tt:tt + 1], 0.0))
        for tt in range(8):
            tb = 5 if tt % 2 == 0 else 6
            transpose(ppb[tb][:1, 0:128], scFbf[:, tt:tt + 1])
            vop(lambda e, tb=tb, tt=tt: e.tensor_copy(
                rows[:1, tt * 128:(tt + 1) * 128], ppb[tb][:1, 0:128]))
        sop(lambda e: e.add(rows[:1, 0:1024], rows[:1, 0:1024], 0.0))
        vop(lambda e, nsc=nsc: e.tensor_scalar(rows[:1, 0:1024], rows[:1, 0:1024],
                                               nsc, None, ALU.mult))
        vop(lambda e: e.tensor_scalar(rows[:1, 0:1024], rows[:1, 0:1024],
                                      1e-5, None, ALU.max))
        vop(lambda e: e.reciprocal(rows2[:1, 0:1024], rows[:1, 0:1024]))
        vop(lambda e: e.tensor_scalar(rows2[:1, 0:1024], rows2[:1, 0:1024],
                                      127.0, None, ALU.mult))
        vop(lambda e: e.tensor_scalar(rows[:1, 0:1024], rows[:1, 0:1024],
                                      1.0 / 127.0, None, ALU.mult))
        for half in range(2):
            h0 = half * 512
            bcast(pp[0][:, :], rows2[:1, h0:h0 + 512])
            bcast(pp[1][:, :], rows[:1, h0:h0 + 512])
            vop(lambda e, h0=h0: e.tensor_copy(scbc[:, h0:h0 + 512], pp[0][:, :]))
            vop(lambda e, h0=h0: e.tensor_copy(invbc[:, h0:h0 + 512], pp[1][:, :]))

    def phase_l1f():
        wmean_scale([din["wup_p"][:, r * 2048:(r + 1) * 2048] for r in range(32)],
                    32, 0, entry=marks["wm"])
        if stop_after == "wsc":
            dump("srow", srow, 128, 8)
            return True
        C.fence(("gpsimd",))
        for u in range(8):
            def g_pre(e, u=u):
                src = slotA[u * 16:(u + 1) * 16, :].rearrange(
                    "cp (c t) -> cp c t", t=1024)
                dstap = shufd[:, 0:16 * 1024].rearrange("p (cp t) -> p cp t", t=1024)
                return e.dma_start(out=dstap[u:128:8, :, :].transpose([1, 0, 2]),
                                   in_=src)
            C.op("gpsimd", g_pre)
            C.inc("gpsimd", "G", 16)
        C.wait("gpsimd", "G", C.cnt["G"])
        gv = dma("gpsimd", xq[:, :], shufd[:, 0:16 * 1024])
        C.wait("tensor", "G", gv)
        C.wait("vector", "G", gv)
        s1cp = {0: 0, 1: 0}
        for fc in range(16):
            s1mm = {}
            for nh in range(2):
                s1mm[nh] = mmx(pp[nh][:, :], H128_bf[:, :],
                               xq[:, fc * 1024 + nh * 512:
                                  fc * 1024 + nh * 512 + 512], True, True,
                               [("V", s1cp[nh])])
            for nh in range(2):
                s1cp[nh] = vopx(lambda e, fc=fc, nh=nh: e.tensor_copy(
                    xq[:, fc * 1024 + nh * 512:fc * 1024 + nh * 512 + 512],
                    pp[nh][:, :]), [("T", s1mm[nh])])
        C.fence(("gpsimd",))
        for l in range(8):
            def g_mid(e, l=l):
                src = xq[l:128:8, :].rearrange("hg (c t) -> hg c t", t=1024)
                dstap = shufd[:, 0:16 * 1024].rearrange("p (hg t) -> p hg t", t=1024)
                return e.dma_start(out=dstap[l:128:8, :, :].transpose([1, 0, 2]),
                                   in_=src)
            C.op("gpsimd", g_mid)
            C.inc("gpsimd", "G", 16)
        C.wait("gpsimd", "G", C.cnt["G"])
        gv = dma("gpsimd", xq[:, :], shufd[:, 0:16 * 1024])
        C.wait("tensor", "G", gv)
        C.wait("vector", "G", gv)
        rdp = {2: 0, 3: 0, 4: 0, 5: 0}
        s2ts = {0: 0, 1: 0}
        for fc in range(16):
            for tt in range(8):
                bb = 2 + (tt % 4)
                tmm = mmx(pp[bb][:, 0:128],
                          xq[:, fc * 1024 + tt * 128:fc * 1024 + tt * 128 + 128],
                          H16I8_bf[:, :], True, True, [("V", rdp[bb])])
                rdp[bb] = vopx(lambda e, bb=bb, tt=tt, fc=fc: e.tensor_reduce(
                    mxtab[:, tt * 16 + fc:tt * 16 + fc + 1], pp[bb][:, 0:128],
                    AX.X, ALU.max, apply_absolute_value=True), [("T", tmm)])
            s2mm = {}
            for nh in range(2):
                s2mm[nh] = mmx(pp[nh][:, :], H16I8_bf[:, :],
                               xq[:, fc * 1024 + nh * 512:
                                  fc * 1024 + nh * 512 + 512], True, True,
                               [("V", s2ts[nh])])
            for nh in range(2):
                s2ts[nh] = vopx(lambda e, fc=fc, nh=nh: e.tensor_scalar(
                    xq[:, fc * 1024 + nh * 512:fc * 1024 + nh * 512 + 512],
                    pp[nh][:, :], R2048, None, ALU.mult), [("T", s2mm[nh])])
        C.fence()
        fwht_rows_from_maxcols(R2048, 16)
        if stop_after == "xh1":
            dump("xh", xq, 2048, 1024)
            dump("scbc", scbc, 128, 1024)
            dump("invbc", invbc, 128, 1024)
            return True
        for fc in range(16):
            vop(lambda e, fc=fc: e.tensor_mul(
                scF[:, 0:1024], xq[:, fc * 1024:(fc + 1) * 1024], scbc[:, :]))
            vop(lambda e: e.tensor_copy(i32t[:, :], scF[:, 0:1024]))
            vop(lambda e, fc=fc: e.scalar_tensor_tensor(
                xq[:, fc * 1024:(fc + 1) * 1024], i32t[:, :], 1.0, invbc[:, :],
                ALU.mult, ALU.mult))
        if stop_after == "xq1":
            dump("xq", xq, 2048, 1024)
            return True
        C.fence()
        sg2, addt, mmg, rel, stt, h128t, cpy = {}, {}, {}, {}, {}, {}, {}
        gv_prev = {0: C.cnt["GA"], 1: C.cnt["GB"]}
        for m_ in range(32):
            e_ = m_ % 2
            sl = stageF[e_]
            sA = wslbf[:, e_ * 4096:e_ * 4096 + 2048]
            sB = wslbf[:, e_ * 4096 + 2048:e_ * 4096 + 4096]
            if m_ >= 2:
                C.wait("sync", "S", sg2[m_ - 2])
            lv = dma("sync", sl, din["wup_p"][:, m_ * 2048:(m_ + 1) * 2048],
                     "LW%d" % e_)
            swt = [("LW%d" % e_, lv)]
            if m_ >= 2:
                swt.append(("T", mmg[(m_ - 2, 1)]))
            sopx(lambda e, s=sl, w=sA: e.activation(
                w, s, AF.Sign, bias=-0.5, scale=srow[:, 0:1]), swt)
            sg2[m_] = sopx(lambda e, s=sl, w=sB: e.activation(
                w, s, AF.Sign, bias=0.5, scale=srow[:, 0:1]),
                [("V", addt.get(m_ - 2, 0))])
            addt[m_] = vopx(lambda e, a=sA, b=sB: e.tensor_add(a, a, b),
                            [("S", sg2[m_])])
            for nh in range(2):
                o5 = nh * 1024
                wt0 = [("V", addt[m_])]
                if m_ >= 1:
                    wt0.append(("S", rel[(m_ - 1, nh)]))
                for k in range(16):
                    mmg[(m_, nh)] = mmx(
                        pp[nh][:, :], sA[:, k * 128:(k + 1) * 128],
                        xq[:, k * 1024 + nh * 512:k * 1024 + nh * 512 + 512],
                        k == 0, k == 15, wt0 if k == 0 else ())
                rwt = [("T", mmg[(m_, nh)])]
                if m_ >= 1:
                    rwt.append(("V", stt[(m_ - 1, nh)]))
                rel[(m_, nh)] = sopx(lambda e, nh=nh: e.activation(
                    scF[:, nh * 512:nh * 512 + 512], pp[nh][:, :], AF.Relu), rwt)
                stt[(m_, nh)] = vopx(lambda e, nh=nh, o5=o5:
                                     e.scalar_tensor_tensor(
                    scFbf[:, o5:o5 + 512], scF[:, nh * 512:nh * 512 + 512],
                    srow[:, 1:2], scF[:, nh * 512:nh * 512 + 512],
                    ALU.mult, ALU.mult), [("S", rel[(m_, nh)])])
                hwt = [("V", stt[(m_, nh)])]
                if m_ >= 1:
                    hwt.append(("V", cpy[(m_ - 1, nh)]))
                h128t[(m_, nh)] = mmx(pp[2 + nh][:, :], H128_bf[:, :],
                                      scFbf[:, o5:o5 + 512], True, True, hwt)
                gs = "GA" if nh == 0 else "GB"
                cpy[(m_, nh)] = vopx(lambda e, nh=nh, o5=o5: e.tensor_copy(
                    scFbf[:, o5 + 512:o5 + 1024], pp[2 + nh][:, :]),
                    [("T", h128t[(m_, nh)]), (gs, gv_prev[nh])])
                C.wait("gpsimd", "V", cpy[(m_, nh)])
                for l in range(4):
                    def g_sh4(e, m_=m_, l=l, nh=nh, o5=o5):
                        dstap = shufd[m_ * 4 + l:m_ * 4 + l + 1,
                                      0:32 * 1024].rearrange(
                            "p (u hh t) -> (p u) hh t", hh=2, t=512)
                        return e.dma_start(out=dstap[:, nh, :],
                                           in_=scFbf[l:128:4, o5 + 512:o5 + 1024])
                    C.op("gpsimd", g_sh4)
                    gv = C.inc("gpsimd", gs, 16)
                gv_prev[nh] = gv
        wmean_scale([din["wdnTp"][r * 128:(r + 1) * 128, :] for r in range(32)],
                    32, 4)
        C.fence()
        if stop_after == "h4s":
            if "h4s" in tap_t:
                stg = AL.view(AL.nbytes - 8192, 2048, F32)
                for m2 in range(32):
                    lv = dma("sync", f2b[0], shufd[:, m2 * 1024:(m2 + 1) * 1024],
                             "LX0")
                    C.wait("vector", "LX0", lv)
                    vv = vop(lambda e: e.tensor_copy(stg[:, 0:1024], f2b[0]))
                    C.wait("gpsimd", "V", vv)
                    gv = dma("gpsimd", tap_t["h4s"][m2 * 128:(m2 + 1) * 128, :],
                             stg[:, 0:1024])
                    C.wait("vector", "G", gv)
                    C.wait("sync", "G", gv)
            return True
        C.fence()
        mx_t = {0: 0, 1: 0}
        rdp = {2: 0, 3: 0, 4: 0, 5: 0}
        for m2 in range(32):
            fb = f2b[m2 % 2]
            if m2 >= 2:
                C.wait("sync", "T", mx_t[m2 % 2])
            lv = dma("sync", fb, shufd[:, m2 * 1024:(m2 + 1) * 1024],
                     "LX%d" % (m2 % 2))
            for tt in range(8):
                bb = 2 + (tt % 4)
                mwt = [("V", rdp[bb])]
                if tt == 0:
                    mwt.append(("LX%d" % (m2 % 2), lv))
                tvx = mmx(pp[bb][:, 0:128], fb[:, tt * 128:(tt + 1) * 128],
                          H32I4_bf[:, :], True, True, mwt)
                rdp[bb] = vopx(lambda e, bb=bb, tt=tt, m2=m2: e.tensor_reduce(
                    mxtab[:, tt * 32 + m2:tt * 32 + m2 + 1], pp[bb][:, 0:128],
                    AX.X, ALU.max, apply_absolute_value=True), [("T", tvx)])
            mx_t[m2 % 2] = tvx
        C.fence()
        fwht_rows_from_maxcols(R4096, 32)
        if stop_after == "fw4m":
            dump("scbc", scbc, 128, 1024)
            dump("invbc", invbc, 128, 1024)
            return True
        vop(lambda e: e.memset(slotB[:, :], 0.0))
        C.fence()
        sg1d, sg2d, addd, qtt, bankev = {}, {}, {}, {}, {}
        odlast = {}
        oduse = {0: 0, 1: 0, 2: 0, 3: 0}
        f2use = {0: 0, 1: 0}
        for grp in range(8):
            for j in range(4):
                m2 = grp * 4 + j
                jj = j % 2
                fb = f2b[jj]
                prev = (grp, j - 2) if j >= 2 else (grp - 1, j + 2)
                prevj = (grp, j - 1) if j >= 1 else (grp - 1, 3)
                C.wait("sync", "T", f2use[jj])
                C.wait("sync", "S", sg2d.get(prev, 0))
                lv = dma("sync", fb, shufd[:, m2 * 1024:(m2 + 1) * 1024],
                         "LX%d" % jj)
                sl = stageF[jj]
                lv2 = dma("sync", sl, din["wdnTp"][m2 * 128:(m2 + 1) * 128, :],
                          "LW%d" % jj)
                swt = [("LW%d" % jj, lv2), ("V", addd.get(prevj, 0))]
                sg1d[(grp, j)] = sopx(lambda e, s=sl: e.activation(
                    scFbf[:, 0:2048], s, AF.Sign, bias=-0.5,
                    scale=srow[:, 4:5]), swt)
                s2wt = []
                if grp > 0:
                    s2wt.append(("T", odlast[grp - 1]))
                sg2d[(grp, j)] = sopx(lambda e, s=sl, j=j: e.activation(
                    wslbf[:, j * 2048:(j + 1) * 2048], s, AF.Sign, bias=0.5,
                    scale=srow[:, 4:5]), s2wt)
                addd[(grp, j)] = vopx(lambda e, j=j: e.tensor_add(
                    wslbf[:, j * 2048:(j + 1) * 2048],
                    wslbf[:, j * 2048:(j + 1) * 2048],
                    scFbf[:, 0:2048]), [("S", sg2d[(grp, j)])])
                for nh in range(2):
                    o6 = nh * 1024
                    mwt = [("V", bankev.get(jj, 0))]
                    if nh == 0:
                        mwt.append(("LX%d" % jj, lv))
                    tvx = mmx(pp[6 + jj][:, :], H32I4_bf[:, :],
                              fb[:, nh * 512:nh * 512 + 512], True, True, mwt)
                    bankev[jj] = vopx(lambda e, o6=o6, jj=jj: e.tensor_scalar(
                        scF[:, o6:o6 + 512], pp[6 + jj][:, :], R4096, None,
                        ALU.mult), [("T", tvx)])
                    vopx(lambda e, o6=o6, nh=nh: e.tensor_mul(
                        scF[:, o6 + 512:o6 + 1024], scF[:, o6:o6 + 512],
                        scbc[:, nh * 512:nh * 512 + 512]))
                    vopx(lambda e, nh=nh, o6=o6: e.tensor_copy(
                        i32t[:, nh * 512:nh * 512 + 512],
                        scF[:, o6 + 512:o6 + 1024]))
                    hwt = []
                    if grp > 0:
                        hwt.append(("T", odlast[grp - 1]))
                    qtt[(grp, j)] = vopx(lambda e, j=j, nh=nh:
                                         e.scalar_tensor_tensor(
                        hgrp2[:, j * 1024 + nh * 512:j * 1024 + nh * 512 + 512],
                        i32t[:, nh * 512:nh * 512 + 512], 1.0,
                        invbc[:, nh * 512:nh * 512 + 512],
                        ALU.mult, ALU.mult), hwt)
                    f2use[jj] = tvx
            for od in range(16):
                for nh in range(2):
                    bk = (od * 2 + nh) % 4
                    gwt = [("V", oduse[bk])]
                    if od == 0 and nh == 0:
                        gwt += [("V", addd[(grp, 3)]), ("V", qtt[(grp, 3)])]
                    for j in range(4):
                        tvx = mmx(
                            pp[bk][:, :],
                            wslbf[:, j * 2048 + od * 128:j * 2048 + od * 128 + 128],
                            hgrp2[:, j * 1024 + nh * 512:j * 1024 + nh * 512 + 512],
                            j == 0, j == 3, gwt if j == 0 else ())
                    odlast[grp] = tvx
                    oduse[bk] = vopx(lambda e, od=od, nh=nh, bk=bk:
                                     e.scalar_tensor_tensor(
                        slotB[:, od * 1024 + nh * 512:od * 1024 + nh * 512 + 512],
                        pp[bk][:, :], srow[:, 6:7],
                        slotB[:, od * 1024 + nh * 512:od * 1024 + nh * 512 + 512],
                        ALU.mult, ALU.add), [("T", tvx)])
        C.fence()
        for fc in range(16):
            vop(lambda e, fc=fc: e.tensor_add(
                slotB[:, fc * 1024:(fc + 1) * 1024],
                slotB[:, fc * 1024:(fc + 1) * 1024],
                slotA[:, fc * 1024:(fc + 1) * 1024]))
        out_g = {0: 0, 1: 0}

        def out_fn(fc, scrA_, rbc_):
            ob = outf[fc % 2]
            gs = "GA" if fc % 2 == 0 else "GB"
            C.wait("vector", gs, out_g[fc % 2])
            vv = vop(lambda e, ob=ob: e.tensor_mul(ob[:, :], scrA_[:, 0:1024],
                                                   rbc_[:, :]))
            C.wait("gpsimd", "V", vv)
            out_g[fc % 2] = dma("gpsimd", y_out[fc * 128:(fc + 1) * 128, :],
                                ob[:, :], gs)

        layer_norm(slotB, sq_bf2, mbc2, rbc2, scF[:, 0:1024], out_fn=out_fn)
        C.wait("gpsimd", "GA", C.cnt["GA"])
        C.wait("gpsimd", "GB", C.cnt["GB"])

    if phase_l1f():
        return finish()
    C.fence()
    LAST_WAITED.clear()
    LAST_WAITED.update({k: set(v) for k, v in C.waited.items()})
    C.replay(es.enter_context(nc.Block()))
    es.close()
    return nc


_NC_CACHE = {}


def kernel(**inputs):
    in_maps = host_prepare(inputs)
    key = "full"
    if key not in _NC_CACHE:
        build_nc(stop_after="out")
        _NC_CACHE[key] = build_nc(stop_after="out",
                                  want=dict(LAST_WAITED))
    nc = _NC_CACHE[key]
    res = run_bass_kernel_spmd(nc, in_maps, core_ids=list(range(8)))
    out = np.zeros((B, S, D), np.float32)
    for core in range(8):
        g, t = core // 4, core % 4
        out[g, t * TOK:(t + 1) * TOK, :] = np.asarray(res.results[core]["y_out"]).T
    return out
